# version 13
# baseline (speedup 1.0000x reference)
import numpy as np
import ml_dtypes
import concourse.bass as bass
import concourse.mybir as mybir
from concourse.bass_utils import run_bass_kernel_spmd

F32 = mybir.dt.float32
BF16 = mybir.dt.bfloat16
AF = mybir.ActivationFunctionType
ALU = mybir.AluOpType
AX = mybir.AxisListType

S, D, NH, NKV, HD, WIN, HF = 2048, 2048, 16, 4, 128, 1024, 8192
NDC = D // 128
NTS, TS = 4, 512
NQT = S // 128
QG, NQG = 512, 4
EPS = 1e-5
FW_SC = float(1.0 / np.sqrt(HF))
NEG = -1.0e30
ENGINES = ("tensor", "vector", "scalar", "sync", "gpsimd")


def _hadamard(n):
    h = np.array([[1.0]], dtype=np.float32)
    while h.shape[0] < n:
        h = np.block([[h, h], [h, -h]])
    return h.astype(np.float32)


def host_constants():
    t = np.arange(S, dtype=np.float32)
    inv = 1.0 / (10000.0 ** (np.arange(0, HD, 2, dtype=np.float32) / HD))
    fr = np.outer(t, inv)
    emb = np.concatenate([fr, fr], axis=-1)
    cos = np.ascontiguousarray(np.cos(emb).T)
    sin = np.ascontiguousarray(np.sin(emb).T)
    sgn = np.ones((HD, 1), dtype=np.float32)
    sgn[:64] = -1.0
    sc = np.float32(1.0 / np.sqrt(HD))
    qq = np.arange(128)[:, None]
    kk = np.arange(128)[None, :]
    return {
        "cosq": (cos * sc).astype(np.float32),
        "sinq": (sin * sgn * sc).astype(np.float32),
        "cosk": cos.astype(np.float32),
        "sink": (sin * sgn).astype(np.float32),
        "ident": np.eye(128, dtype=np.float32),
        "h128": _hadamard(128),
        "h64x2": np.kron(np.eye(2, dtype=np.float32), _hadamard(64)),
        "onescol": np.ones((128, 1), dtype=np.float32),
        "onesrow": np.ones((1, 128), dtype=np.float32),
        "negrow": -np.ones((1, 128), dtype=np.float32),
        "cvec": np.tile(np.array([[EPS, -0.5, FW_SC, 1.0 / D]], dtype=np.float32), (128, 1)),
        "maskd": np.where(qq >= kk, np.float32(0), np.float32(NEG)),
        "maskf": np.where(qq <= kk, np.float32(0), np.float32(NEG)),
    }


class Sem:
    def __init__(self, handle, name):
        self.h, self.name, self.n = handle, name, 0


class Prog:
    def __init__(self):
        self.ops = {e: [] for e in ENGINES}
        self.serial_sem = None
        self.serial_dma = False

    def op(self, eng, fn, waits=(), incs=(), tag=""):
        waits = list(waits)
        incs = list(incs)
        serial_followup = None
        if self.serial_sem is not None and fn is not None:
            so = self.serial_sem
            waits.append((so, so.n))
            if not incs:
                incs.append((so, 1))
            else:
                serial_followup = (eng, so)
        self.serial_dma = False
        rec = []
        for sem, amt in incs:
            sem.n += amt
            rec.append((sem, amt))
        self.ops[eng].append((fn, tuple(waits), tuple(rec), tag))
        if serial_followup is not None:
            eng2, so = serial_followup
            s0, _ = rec[0]
            self.ops[eng2].append((lambda e: e.nop(), ((s0, s0.n),), ((so, 1),), tag + ".serord"))
            so.n += 1

    def barrier(self, sbar, k):
        for e in ENGINES:
            self.op(e, lambda eng: eng.nop(), incs=[(sbar, 1)], tag=f"bar{k}.inc")
        for e in ENGINES:
            self.op(e, None, waits=[(sbar, 5 * k)], tag=f"bar{k}.wait")

    def check_deadlock(self):
        idx = {e: 0 for e in ENGINES}
        vals = {}
        progress = True
        while progress:
            progress = False
            for e in ENGINES:
                while idx[e] < len(self.ops[e]):
                    fn, waits, incs, tag = self.ops[e][idx[e]]
                    if any(vals.get(s, 0) < v for s, v in waits):
                        break
                    for s, amt in incs:
                        vals[s] = vals.get(s, 0) + amt
                    idx[e] += 1
                    progress = True
        stuck = {e: i for e, i in idx.items() if i < len(self.ops[e])}
        if stuck:
            lines = []
            for e, i in stuck.items():
                fn, waits, incs, tag = self.ops[e][i]
                wd = [(s.name, v, vals.get(s, 0)) for s, v in waits]
                lines.append(f"  {e}[{i}/{len(self.ops[e])}] tag={tag} waits={wd}")
            raise RuntimeError("DEADLOCK in emitted program:\n" + "\n".join(lines))

    def replay(self, name, eng):
        for fn, waits, incs, tag in self.ops[name]:
            for s, v in waits:
                eng.wait_ge(s.h, v)
            if fn is None:
                continue
            inst = fn(eng)
            for s, amt in incs:
                inst = inst.then_inc(s.h, amt)


class DmaPool:

    def __init__(self, P, eng, mksem, name, K=8):
        self.P, self.eng, self.K = P, eng, K
        self.sems = [mksem(f"{name}{i}") for i in range(K)]
        self.uses = [0] * K
        self.i = 0

    def dma(self, fn, waits=(), tag=""):
        j = self.i % self.K
        s = self.sems[j]
        w = list(waits)
        if self.uses[j]:
            w.append((s, 16 * self.uses[j]))
        self.P.serial_dma = True
        self.P.op(self.eng, fn, waits=w, incs=[(s, 16)], tag=tag)
        self.uses[j] += 1
        self.i += 1
        return [(s, 16 * self.uses[j])]

    def snap(self):
        return [(s, 16 * u) for s, u in zip(self.sems, self.uses) if u]


def build_nc(taps=False, serial=False):
    import contextlib
    nc = bass.Bass("TRN2", target_bir_lowering=False, dynamic_dma_scratch_size=16384)

    def din(name, shape):
        return nc.dram_tensor(name, list(shape), F32, kind="ExternalInput").ap()

    xT = din("xT", (D, S))
    wq, wk, wv, wo = din("wq", (D, D)), din("wk", (D, 512)), din("wv", (D, 512)), din("wo", (D, D))
    w1t = din("w1t", (16, 128, 16, 512))
    w3t = din("w3t", (16, 128, 16, 512))
    w2t = din("w2t", (8, 128, 64, 256))
    cosqd, sinqd = din("cosq", (128, S)), din("sinq", (128, S))
    coskd, sinkd = din("cosk", (128, S)), din("sink", (128, S))
    identd, h128d, h64d = din("ident", (128, 128)), din("h128", (128, 128)), din("h64x2", (128, 128))
    onescold, onesrowd, negrowd = din("onescol", (128, 1)), din("onesrow", (1, 128)), din("negrow", (1, 128))
    maskdd, maskfd = din("maskd", (128, 128)), din("maskf", (128, 128))
    cvecd = din("cvec", (128, 4))

    yT = nc.dram_tensor("yT", [D, S], F32, kind="ExternalOutput").ap()
    ikind = "ExternalOutput" if taps else "Internal"
    h2d = nc.dram_tensor("h2d", [64, NTS, 128, TS], BF16, kind=ikind).ap()
    h3d = nc.dram_tensor("h3d", [NTS, 128, 64, TS], BF16, kind=ikind).ap()
    x1d = nc.dram_tensor("x1d", [NDC, 128, S], F32, kind=ikind).ap()
    if taps:
        tqfm = nc.dram_tensor("tqfm", [128, NH * S], BF16, kind="ExternalOutput").ap()
        tkfm = nc.dram_tensor("tkfm", [128, NKV * S], BF16, kind="ExternalOutput").ap()
        tvtm = nc.dram_tensor("tvtm", [128, 16 * 512], BF16, kind="ExternalOutput").ap()
        tac = nc.dram_tensor("tac", [128, NH * QG], BF16, kind="ExternalOutput").ap()

    P = Prog()
    stack = contextlib.ExitStack()

    def sbuf(name, shape, dt):
        return stack.enter_context(nc.sbuf_tensor(name, list(shape), dt))

    def psum(name, shape, dt=F32):
        return stack.enter_context(nc.psum_tensor(name, list(shape), dt))

    def sem(name):
        return Sem(stack.enter_context(nc.semaphore(name)), name)

    with stack:
        A = sbuf("A", (128, 32768), BF16)
        B = sbuf("B", (128, 32768), BF16)
        C = sbuf("C", (128, 8192), BF16)
        Dv = sbuf("Dv", (128, 8192), BF16)
        W0 = sbuf("W0", (128, 8192), BF16)
        W1b = sbuf("W1b", (128, 8192), BF16)
        SCR = sbuf("SCR", (128, 3584), F32)

        AF32 = A[:].bitcast(F32)
        CF32 = C[:].bitcast(F32)
        DF32 = Dv[:].bitcast(F32)
        SCRb = SCR[:].bitcast(BF16)

        nm2 = sbuf("nm2", (128, 2), F32)
        ssum2 = sbuf("ssum2", (128, 2), F32)
        rr2 = sbuf("rr2", (128, 2), F32)
        h64b = sbuf("h64b", (128, 128), BF16)
        onescolb = sbuf("onescolb", (128, 1), BF16)
        onescolf = sbuf("onescolf", (128, 1), F32)
        onesrow_s = sbuf("onesrow_s", (1, 128), F32)
        negrow_s = sbuf("negrow_s", (1, 128), F32)
        cvec_s = sbuf("cvec_s", (128, 4), F32)

        PSA = psum("PSA", (128, 1536), F32)
        PSB = psum("PSB", (128, 1536), F32)
        PTO = psum("PTO", (128, 1024), F32)
        PTObf = PTO[:].bitcast(BF16)

        maskf_s = AF32[:, 15488:15616]
        maskd_s = AF32[:, 15616:15744]
        identb = A[:, 31488:31616]
        h128b = A[:, 0:128]
        mrow1 = SCR[0:1, 2560:3072]
        vrow1 = SCR[0:1, 3072:3584]
        trow = PSB[0:1, 1024:1536]

        sBar = sem("sBar")
        sPre, sRpV, sRpA = sem("sPre"), sem("sRpV"), sem("sRpA")
        sQK, sMax, sExp, sRec, sScl = sem("sQK"), sem("sMax"), sem("sExp"), sem("sRec"), sem("sScl")
        sTP, sTPe, sPV, sOev = sem("sTP"), sem("sTPe"), sem("sPV"), sem("sOev")
        sWo, sZ, sZb, sSq, sSt = sem("sWo"), sem("sZ"), sem("sZb"), sem("sSq"), sem("sSt")
        sFinV, sFinA, sBc, sApl = sem("sFinV"), sem("sFinA"), sem("sBc"), sem("sApl")
        sUp, sSil, sGat, sH2e = sem("sUp"), sem("sSil"), sem("sGat"), sem("sH2e")
        sH64, sH64e, sDn, sDnE = sem("sH64"), sem("sH64e"), sem("sDn"), sem("sDnE")
        gp = DmaPool(P, "gpsimd", sem, "dG", 8)
        sy = DmaPool(P, "sync", sem, "dS", 8)
        sc = DmaPool(P, "scalar", sem, "dC", 8)
        if serial:
            P.serial_sem = sem("sOrd")

        sy.dma(lambda e: e.dma_start(cvec_s[:], cvecd), tag="ld.cvec")
        sy.dma(lambda e: e.dma_start(onesrow_s[:], onesrowd), tag="ld.onesrow")
        consts_sy = sy.dma(lambda e: e.dma_start(negrow_s[:], negrowd), tag="ld.negrow") + sy.snap()
        gp.dma(lambda e: e.dma_start(h64b[:], h64d), tag="ld.h64")
        gp.dma(lambda e: e.dma_start(onescolb[:], onescold), tag="ld.onescol")
        sy.dma(lambda e: e.dma_start(onescolf[:], onescold), tag="ld.onescolf")
        xv = xT.rearrange("(dc p) t -> p dc t", p=128)
        for i in range(4):
            gp.dma(lambda e, i=i: e.dma_start(A[:, i * 8192:(i + 1) * 8192], xv[:, 4 * i:4 * (i + 1), :]),
                   tag=f"ld.x{i}")

        t1f = PTO[:, 0:512]
        swpf = PSA[:, 512:1024]
        t2f = SCR[:, 2048:2560]
        cos_sl = lambda par: SCR[:, par * 1024: par * 1024 + 512]
        sin_sl = lambda par: SCR[:, par * 1024 + 512: par * 1024 + 1024]

        def pp(par):
            return (PSA if par == 0 else PSB)[:, 0:512]

        rope_done = []

        def emit_rope(par, dst, cs, sn, tag):
            mmw = (sPre, sPre.n)
            P.op("vector", lambda e, par=par, cs=cs: e.tensor_tensor(t1f, pp(par), cs, ALU.mult),
                 waits=[mmw], incs=[(sRpV, 1)], tag=f"{tag}.t1")
            swq = [(sRpV, sRpV.n)]
            P.op("scalar", lambda e, par=par: e.activation(
                swpf[0:64, :], (PSA if par == 0 else PSB)[64:128, 0:512], AF.Copy),
                waits=[mmw] + swq, incs=[(sRpA, 1)], tag=f"{tag}.sw0")
            P.op("scalar", lambda e, par=par: e.activation(
                swpf[64:128, :], (PSA if par == 0 else PSB)[0:64, 0:512], AF.Copy),
                incs=[(sRpA, 1)], tag=f"{tag}.sw1")
            P.op("vector", lambda e, sn=sn: e.tensor_tensor(t2f, swpf[:], sn, ALU.mult),
                 waits=[(sRpA, sRpA.n)], incs=[(sRpV, 1)], tag=f"{tag}.t2")
            P.op("vector", lambda e, dst=dst: e.tensor_tensor(dst, t2f, t1f, ALU.add),
                 incs=[(sRpV, 1)], tag=f"{tag}.add")
            rope_done.append(sRpV.n)

        wkv = wk.rearrange("(kc p) o -> p kc o", p=128)
        gp.dma(lambda e: e.dma_start(W0[:], wkv), tag="ld.wk")
        k_ld = gp.snap()
        nunit = 0
        tab_user = {0: 0, 1: 0}
        tab_ready = {0: [], 1: []}

        def load_tab(par, ts, cosd, sind, tag):
            t1 = sy.dma(lambda e, par=par, ts=ts, cosd=cosd: e.dma_start(
                cos_sl(par), cosd[:, ts * 512:(ts + 1) * 512]),
                waits=[(sRpV, tab_user[par])], tag=f"{tag}.cos")
            t2 = sy.dma(lambda e, par=par, ts=ts, sind=sind: e.dma_start(
                sin_sl(par), sind[:, ts * 512:(ts + 1) * 512]), tag=f"{tag}.sin")
            tab_ready[par] = t1 + t2

        for ts in range(NTS):
            tpar = ts % 2
            load_tab(tpar, ts, coskd, sinkd, f"ktab.{ts}")
            for g in range(NKV):
                par = nunit % 2
                w0 = k_ld + tab_ready[tpar]
                if nunit >= 2:
                    w0 = w0 + [(sRpV, rope_done[nunit - 2])]
                for kc in range(NDC):
                    P.op("tensor", lambda e, g=g, ts=ts, kc=kc, par=par: e.matmul(
                        pp(par), W0[:, kc * 512 + g * 128: kc * 512 + (g + 1) * 128],
                        A[:, kc * 2048 + ts * 512: kc * 2048 + (ts + 1) * 512],
                        start=(kc == 0), stop=(kc == NDC - 1)),
                        waits=(w0 if kc == 0 else ()),
                        incs=([(sPre, 1)] if kc == NDC - 1 else ()), tag=f"k.mm.{g}.{ts}")
                emit_rope(par, C[:, g * 2048 + ts * 512: g * 2048 + (ts + 1) * 512],
                          cos_sl(tpar), sin_sl(tpar), f"k.rope.{g}.{ts}")
                tab_user[tpar] = sRpV.n
                nunit += 1

        wvv = wv.rearrange("(kc p) o -> p kc o", p=128)
        gp.dma(lambda e: e.dma_start(W0[:], wvv), waits=[(sPre, sPre.n)], tag="ld.wv")
        v_ld = gp.snap()
        for tc in range(16):
            par = nunit % 2
            w0 = list(v_ld)
            if nunit >= 2:
                w0.append((sRpV, rope_done[nunit - 2]))
            for kc in range(NDC):
                P.op("tensor", lambda e, tc=tc, kc=kc, par=par: e.matmul(
                    pp(par), A[:, kc * 2048 + tc * 128: kc * 2048 + (tc + 1) * 128],
                    W0[:, kc * 512:(kc + 1) * 512],
                    start=(kc == 0), stop=(kc == NDC - 1)),
                    waits=(w0 if kc == 0 else ()),
                    incs=([(sPre, 1)] if kc == NDC - 1 else ()), tag=f"v.mm.{tc}")
            P.op("vector", lambda e, tc=tc, par=par: e.tensor_copy(
                Dv[:, tc * 512:(tc + 1) * 512], pp(par)),
                waits=[(sPre, sPre.n)], incs=[(sRpV, 1)], tag=f"v.ev.{tc}")
            rope_done.append(sRpV.n)
            nunit += 1

        wqv = wq.rearrange("(kc p) o -> p kc o", p=128)
        q_mm_at_eighth = {}
        for ei in range(8):
            hb = ei % 2
            ldw = [] if ei < 2 else [(sPre, q_mm_at_eighth[ei - 2])]
            q_tok = gp.dma(lambda e, ei=ei, hb=hb: e.dma_start(
                W1b[:, hb * 4096:(hb + 1) * 4096], wqv[:, :, ei * 256:(ei + 1) * 256]),
                waits=ldw, tag=f"ld.wq{ei}")
            for ts in range(NTS):
                tpar = ts % 2
                load_tab(tpar, ts, cosqd, sinqd, f"qtab.{ei}.{ts}")
                for qc2 in range(2):
                    h = ei * 2 + qc2
                    par = nunit % 2
                    w0 = v_ld + q_tok + tab_ready[tpar]
                    if nunit >= 2:
                        w0 = w0 + [(sRpV, rope_done[nunit - 2])]
                    for kc in range(NDC):
                        P.op("tensor", lambda e, h=h, ts=ts, kc=kc, par=par, hb=hb, qc2=qc2: e.matmul(
                            pp(par),
                            W1b[:, hb * 4096 + kc * 256 + qc2 * 128: hb * 4096 + kc * 256 + (qc2 + 1) * 128],
                            A[:, kc * 2048 + ts * 512: kc * 2048 + (ts + 1) * 512],
                            start=(kc == 0), stop=(kc == NDC - 1)),
                            waits=(w0 if kc == 0 else ()),
                            incs=([(sPre, 1)] if kc == NDC - 1 else ()), tag=f"q.mm.{h}.{ts}")
                    emit_rope(par, B[:, h * 2048 + ts * 512: h * 2048 + (ts + 1) * 512],
                              cos_sl(tpar), sin_sl(tpar), f"q.rope.{h}.{ts}")
                    tab_user[tpar] = sRpV.n
                    nunit += 1
            q_mm_at_eighth[ei] = sPre.n

        P.barrier(sBar, 1)
        mask_tok = sy.dma(lambda e: e.dma_start(maskd_s, maskdd), tag="ld.maskd")
        mask_tok += sy.dma(lambda e: e.dma_start(maskf_s, maskfd), tag="ld.maskf")
        ident_tok = gp.dma(lambda e: e.dma_start(identb, identd), tag="ld.ident")
        if taps:
            sy.dma(lambda e: e.dma_start(tqfm, B[:]), tag="tap.q")
            sy.dma(lambda e: e.dma_start(tkfm, C[:]), tag="tap.k")
            sy.dma(lambda e: e.dma_start(tvtm, Dv[:]), tag="tap.v")

        Pbf = A[:, 20480:21632]
        PT_sb = A[:, 21632:22784]
        attn_col = A[:, 22784:30976]
        O_ps = PTO[:, 768:896]

        def S_ps(par):
            return PSA if par == 0 else PSB

        z1_t = lambda dc: AF32[:, dc * 512:(dc + 1) * 512]
        xres_t = lambda par: AF32[:, 8192 + par * 512: 8192 + (par + 1) * 512]
        x1st_t = lambda par: SCR[:, par * 512:(par + 1) * 512]
        zsq1_t = lambda par: AF32[:, 9216 + par * 512: 9216 + (par + 1) * 512]

        unit = 0
        pend = []
        exp_done = {}
        tp_done = {}
        apl_gate = [0]

        def emit_front(qt, h, qrel):
            nonlocal unit
            par = unit % 2
            nkt = min(qt + 1, 9)
            kt0 = max(0, qt - 8)
            g = h // 4
            w0 = []
            if unit >= 2:
                w0.append((sExp, exp_done[unit - 2]))
            if unit % 64 == 0 and apl_gate[0]:
                w0.append((sApl, apl_gate[0]))
            for i in range(nkt):
                kt = kt0 + i
                P.op("tensor", lambda e, h=h, qt=qt, kt=kt, i=i, par=par, g=g: e.matmul(
                    S_ps(par)[:, i * 128:(i + 1) * 128],
                    B[:, h * 2048 + qt * 128: h * 2048 + qt * 128 + 128],
                    C[:, g * 2048 + kt * 128: g * 2048 + kt * 128 + 128],
                    start=True, stop=True),
                    waits=(w0 if i == 0 else ()),
                    incs=([(sQK, 1)] if i == nkt - 1 else ()), tag=f"qk.{unit}")
            mw = [(sQK, sQK.n)]
            if unit == 0:
                mw = mw + mask_tok
            if qt >= 8:
                P.op("vector", lambda e, par=par: e.tensor_tensor(
                    S_ps(par)[:, 0:128], S_ps(par)[:, 0:128], maskf_s, ALU.add),
                    waits=mw, tag=f"mf.{unit}")
                mw = []
            P.op("vector", lambda e, par=par, nkt=nkt: e.tensor_tensor(
                S_ps(par)[:, (nkt - 1) * 128: nkt * 128],
                S_ps(par)[:, (nkt - 1) * 128: nkt * 128], maskd_s, ALU.add),
                waits=mw, tag=f"md.{unit}")
            P.op("vector", lambda e, par=par, nkt=nkt: e.reduce_max(
                nm2[:, par:par + 1], S_ps(par)[:, 0:nkt * 128], axis=AX.X, negate=True),
                incs=[(sMax, 1)], tag=f"max.{unit}")
            pend.append((unit, qt, h, nkt, kt0, par, qrel))
            unit += 1

        def emit_midback():
            u, qt, h, nkt, kt0, par, qrel = pend.pop(0)
            g = h // 4
            expw = [(sMax, sMax.n)]
            if u >= 1 and (u - 1) in tp_done:
                expw.append((sTP, tp_done[u - 1]))
            P.op("scalar", lambda e, par=par, nkt=nkt: e.activation(
                Pbf[:, 0:nkt * 128], S_ps(par)[:, 0:nkt * 128], AF.Exp,
                bias=nm2[:, par:par + 1], accum_out=ssum2[:, par:par + 1]),
                waits=expw, incs=[(sExp, 1)], tag=f"exp.{u}")
            exp_done[u] = sExp.n
            P.op("vector", lambda e, par=par: e.reciprocal(rr2[:, par:par + 1], ssum2[:, par:par + 1]),
                 waits=[(sExp, sExp.n)], incs=[(sRec, 1)], tag=f"rec.{u}")
            P.op("vector", lambda e, par=par, nkt=nkt: e.tensor_scalar(
                Pbf[:, 0:nkt * 128], Pbf[:, 0:nkt * 128], rr2[:, par:par + 1], None, ALU.mult),
                waits=[(sRec, sRec.n)], incs=[(sScl, 1)], tag=f"scl.{u}")
            for i in range(nkt):
                P.op("tensor", lambda e, i=i: e.transpose(
                    PTObf[:, i * 128:(i + 1) * 128], Pbf[:, i * 128:(i + 1) * 128], identb),
                    waits=(([(sScl, sScl.n), (sOev, sOev.n)] + (ident_tok if u == 0 else []))
                          if i == 0 else ()),
                    incs=([(sTP, 1)] if i == nkt - 1 else ()), tag=f"tp.{u}")
            tp_done[u] = sTP.n
            P.op("scalar", lambda e, nkt=nkt: e.activation(
                PT_sb[:, 0:nkt * 128], PTObf[:, 0:nkt * 128], AF.Copy),
                waits=[(sTP, sTP.n)], incs=[(sTPe, 1)], tag=f"tpe.{u}")
            for i in range(nkt):
                kt = kt0 + i
                P.op("tensor", lambda e, i=i, kt=kt, g=g, nkt=nkt: e.matmul(
                    O_ps, Dv[:, kt * 512 + g * 128: kt * 512 + (g + 1) * 128],
                    PT_sb[:, i * 128:(i + 1) * 128],
                    start=(i == 0), stop=(i == nkt - 1)),
                    waits=([(sTPe, sTPe.n)] if i == 0 else ()),
                    incs=([(sPV, 1)] if i == nkt - 1 else ()), tag=f"pv.{u}")
            P.op("scalar", lambda e, h=h, qrel=qrel: e.activation(
                attn_col[:, h * 512 + qrel: h * 512 + qrel + 128], O_ps, AF.Copy),
                waits=[(sPV, sPV.n)], incs=[(sOev, 1)], tag=f"oev.{u}")

        def emit_ln_finalize(statS, statS2, R_ps, B_ps, prev_bc, tag, mrow=None, vrow=None):
            if mrow is None:
                mrow, vrow = mrow1, vrow1
            P.op("vector", lambda e: e.tensor_scalar(mrow, statS, cvec_s[0:1, 3:4], None, ALU.mult),
                 waits=[(sSt, sSt.n), (sBc, prev_bc)], tag=f"{tag}.mean")
            P.op("vector", lambda e: e.tensor_scalar(trow, statS2, cvec_s[0:1, 3:4], None, ALU.mult),
                 tag=f"{tag}.e2")
            P.op("vector", lambda e: e.tensor_tensor(vrow, mrow, mrow, ALU.mult),
                 tag=f"{tag}.m2")
            P.op("vector", lambda e: e.tensor_tensor(trow, trow, vrow, ALU.subtract),
                 incs=[(sFinV, 1)], tag=f"{tag}.var")
            P.op("scalar", lambda e: e.activation(vrow, trow, AF.Ln, bias=cvec_s[0:1, 0:1]),
                 waits=[(sFinV, sFinV.n), (sBc, prev_bc)], tag=f"{tag}.lnf")
            P.op("scalar", lambda e: e.activation(vrow, vrow, AF.Exp, scale=cvec_s[0:1, 1:2]),
                 incs=[(sFinA, 1)], tag=f"{tag}.expf")
            P.op("tensor", lambda e: e.matmul(R_ps, onesrow_s[:], vrow, start=True, stop=True),
                 waits=[(sFinA, sFinA.n)], incs=[(sBc, 1)], tag=f"{tag}.bcR")
            P.op("vector", lambda e: e.tensor_tensor(mrow, mrow, vrow, ALU.mult),
                 waits=[(sFinA, sFinA.n)], incs=[(sFinV, 1)], tag=f"{tag}.brow")
            P.op("tensor", lambda e: e.matmul(B_ps, negrow_s[:], mrow, start=True, stop=True),
                 waits=[(sFinV, sFinV.n)], incs=[(sBc, 1)], tag=f"{tag}.bcB")

        wov = wo.rearrange("(kc p) o -> p kc o", p=128)
        wo_user_done = {0: None, 1: None}
        x1_pend = []
        x1_tok = {}

        def flush_x1():
            qg_, dc_, apl = x1_pend.pop(0)
            x1_tok[(qg_, dc_)] = sy.dma(lambda e, dc=dc_, qg0_=qg_ * QG: e.dma_start(
                x1d[dc, :, qg0_:qg0_ + QG], z1_t(dc)),
                waits=[(sApl, apl)], tag=f"x1w.{qg_}.{dc_}")
        R1 = PSA[:, 1024:1536]
        B1 = PSB[:, 1024:1536]
        statS = PSB[0:1, 0:512]
        statS2 = PSB[0:1, 512:1024]

        for qg in range(NQG):
            qg0 = qg * QG
            for qt_l in range(4):
                for h in range(NH):
                    emit_front(qg * 4 + qt_l, h, qt_l * 128)
                    if len(pend) >= 2:
                        emit_midback()
            while pend:
                emit_midback()
            all_oev = sOev.n
            tac_tok = []
            if taps and qg == 0:
                tac_tok = sy.dma(lambda e: e.dma_start(tac, attn_col),
                                 waits=[(sOev, all_oev)], tag="tap.ac")
            z_done, zb_done, st_done = {}, {}, {}
            stat_pend = []

            def emit_stats(z_t, zsq_t, which):
                dc, zpar, zc, sqc = stat_pend.pop(0)
                P.op("tensor", lambda e, dc=dc, zpar=zpar: e.matmul(
                    statS, onescolf[:], z_t(dc),
                    start=(dc == 0), stop=(dc == NDC - 1)),
                    waits=[(sZ, zc)], tag=f"st1.{which}.{dc}")
                P.op("tensor", lambda e, dc=dc, zpar=zpar: e.matmul(
                    statS2, onescolf[:], zsq_t(zpar),
                    start=(dc == 0), stop=(dc == NDC - 1)),
                    waits=[(sSq, sqc)],
                    incs=([(sSt, 1)] if dc == NDC - 1 else ()), tag=f"st2.{which}.{dc}")
                st_done[dc] = sSt.n

            prev_bc = sBc.n
            for dcg in range(4):
                wpar = dcg % 2
                wobuf = W0 if wpar == 0 else W1b
                ldw = list(tac_tok) if dcg == 0 else []
                if wo_user_done[wpar] is not None:
                    ldw.append((sWo, wo_user_done[wpar]))
                elif qg == 0 and dcg < 2:
                    ldw.append((sPre, sPre.n))
                wo_ld = gp.dma(lambda e, dcg=dcg, wobuf=wobuf: e.dma_start(
                    wobuf[:], wov[:, :, dcg * 512:(dcg + 1) * 512]),
                    waits=ldw, tag=f"ld.wo.{qg}.{dcg}")
                for _ in range(4):
                    if x1_pend:
                        flush_x1()
                for dc_l in range(4):
                    dc = dcg * 4 + dc_l
                    zpar = dc % 2
                    woPS = PSA[:, zpar * 512:(zpar + 1) * 512]
                    xw = [] if dc < 2 else [(sZ, z_done[dc - 2])]
                    xres_tok = sy.dma(lambda e, dc=dc, zpar=zpar, qg0=qg0: e.dma_start(
                        xres_t(zpar), xv[:, dc, qg0:qg0 + QG]),
                        waits=xw, tag=f"xres.{qg}.{dc}")
                    w0 = wo_ld + [(sOev, all_oev)]
                    if dc >= 2:
                        w0.append((sZ, z_done[dc - 2]))
                    for kc in range(NDC):
                        P.op("tensor", lambda e, kc=kc, dc_l=dc_l, wobuf=wobuf, woPS=woPS: e.matmul(
                            woPS, wobuf[:, kc * 512 + dc_l * 128: kc * 512 + (dc_l + 1) * 128],
                            attn_col[:, kc * 512:(kc + 1) * 512],
                            start=(kc == 0), stop=(kc == NDC - 1)),
                            waits=(w0 if kc == 0 else ()),
                            incs=([(sWo, 1)] if kc == NDC - 1 else ()), tag=f"wo.{qg}.{dc}")
                    zaw = [(sWo, sWo.n)] + xres_tok
                    if (qg - 1, dc) in x1_tok:
                        zaw += x1_tok[(qg - 1, dc)]
                    P.op("vector", lambda e, dc=dc, zpar=zpar, woPS=woPS: e.tensor_tensor(
                        z1_t(dc), woPS, xres_t(zpar), ALU.add),
                        waits=zaw, incs=[(sZ, 1)], tag=f"z1.{qg}.{dc}")
                    z_done[dc] = sZ.n
                    zw = [(sZ, sZ.n)] if dc < 2 else [(sZ, sZ.n), (sSt, st_done.get(dc - 2, 0))]
                    P.op("vector", lambda e, dc=dc, zpar=zpar: e.tensor_tensor(
                        zsq1_t(zpar), z1_t(dc), z1_t(dc), ALU.mult),
                        waits=zw, incs=[(sSq, 1)], tag=f"zsq.{qg}.{dc}")
                    stat_pend.append((dc, zpar, z_done[dc], sSq.n))
                    if len(stat_pend) >= 2:
                        emit_stats(z1_t, zsq1_t, 1)
                wo_user_done[wpar] = sWo.n
            while stat_pend:
                emit_stats(z1_t, zsq1_t, 1)
            emit_ln_finalize(statS, statS2, R1, B1, prev_bc, f"ln1.{qg}")
            for dc in range(NDC):
                aw = [(sBc, sBc.n)]
                P.op("vector", lambda e, dc=dc: e.tensor_tensor(
                    z1_t(dc), z1_t(dc), R1, ALU.mult),
                    waits=aw, tag=f"ap1.{qg}.{dc}")
                P.op("vector", lambda e, dc=dc: e.tensor_tensor(
                    z1_t(dc), z1_t(dc), B1, ALU.add),
                    incs=[(sApl, 1)], tag=f"ap2.{qg}.{dc}")
                x1_pend.append((qg, dc, sApl.n))
            apl_gate[0] = sApl.n
            if qg == NQG - 1:
                while x1_pend:
                    flush_x1()

        x1w_all = sy.snap()
        qk_all = sQK.n
        pv_all = sPV.n
        wo_all = sWo.n

        x1v = x1d.rearrange("dc p t -> p dc t")
        for i in range(4):
            gp.dma(lambda e, i=i: e.dma_start(
                B[:, i * 8192:(i + 1) * 8192], x1v[:, 4 * i:4 * (i + 1), :]),
                waits=x1w_all + [(sQK, qk_all)], tag=f"ld.x1bf{i}")

        P.barrier(sBar, 2)
        h128_tok = gp.dma(lambda e: e.dma_start(h128b, h128d), tag="ld.h128")


        sg_t = lambda par: SCR[:, par * 512:(par + 1) * 512]
        hbf_t = lambda par: SCRb[:, 2048 + par * 512: 2048 + (par + 1) * 512]
        h2st_t = lambda par: SCRb[:, 3072 + par * 512: 3072 + (par + 1) * 512]

        def up_ps(par, k):
            return (PSA if par == 0 else PSB)[:, k * 512:(k + 1) * 512]

        up_idx = 0
        up_mm_done = {}
        sil_done, gat_done, h2e_done, h2w_done = {}, {}, {}, {}
        pend_up = []

        def up_tail(i):
            c, ts, par = pend_up.pop(0)
            hw = [(sGat, gat_done[i])]
            if i == 0:
                hw += h128_tok
            if i >= 2:
                hw.append((sH2e, h2e_done[i - 2]))
            P.op("tensor", lambda e, par=par: e.matmul(
                up_ps(par, 2), h128b, hbf_t(par), start=True, stop=True),
                waits=hw, incs=[(sUp, 1)], tag=f"h128.{i}")
            ew = [(sUp, sUp.n)]
            if i >= 2:
                ew += h2w_done[i - 2]
            P.op("scalar", lambda e, par=par: e.activation(
                h2st_t(par), up_ps(par, 2), AF.Copy, scale=cvec_s[:, 2:3]),
                waits=ew, incs=[(sH2e, 1)], tag=f"h2e.{i}")
            h2e_done[i] = sH2e.n
            h2w_done[i] = sy.dma(lambda e, c=c, ts=ts, par=par: e.dma_start(h2d[c, ts], h2st_t(par)),
                                 waits=[(sH2e, sH2e.n)], tag=f"h2w.{i}")

        for cg in range(16):
            wpar = cg % 2
            w1buf = C if wpar == 0 else W0
            w3buf = Dv if wpar == 0 else W1b
            ldw = []
            if cg >= 2:
                ldw.append((sUp, up_mm_done[(cg - 2) * 16 + 15]))
            else:
                ldw += [(sQK, qk_all), (sPV, pv_all), (sWo, wo_all)]
            gp.dma(lambda e, cg=cg, w1buf=w1buf: e.dma_start(w1buf[:], w1t[cg]),
                waits=ldw, tag=f"ld.w1.{cg}")
            gp.dma(lambda e, cg=cg, w3buf=w3buf: e.dma_start(w3buf[:], w3t[cg]), tag=f"ld.w3.{cg}")
            ld_done = gp.snap()
            for cl in range(4):
                c = cg * 4 + cl
                for ts in range(NTS):
                    i = up_idx
                    par = i % 2
                    w0 = list(ld_done)
                    if i >= 2:
                        w0.append((sSil, sil_done[i - 2]))
                    for dc in range(NDC):
                        P.op("tensor", lambda e, dc=dc, cl=cl, ts=ts, par=par, w1buf=w1buf: e.matmul(
                            up_ps(par, 0), w1buf[:, dc * 512 + cl * 128: dc * 512 + (cl + 1) * 128],
                            B[:, dc * 2048 + ts * 512: dc * 2048 + (ts + 1) * 512],
                            start=(dc == 0), stop=(dc == NDC - 1)),
                            waits=(w0 if dc == 0 else ()),
                            incs=([(sUp, 1)] if dc == NDC - 1 else ()), tag=f"upA.{i}")
                    w0b = [(sGat, gat_done[i - 2])] if i >= 2 else []
                    for dc in range(NDC):
                        P.op("tensor", lambda e, dc=dc, cl=cl, ts=ts, par=par, w3buf=w3buf: e.matmul(
                            up_ps(par, 1), w3buf[:, dc * 512 + cl * 128: dc * 512 + (cl + 1) * 128],
                            B[:, dc * 2048 + ts * 512: dc * 2048 + (ts + 1) * 512],
                            start=(dc == 0), stop=(dc == NDC - 1)),
                            waits=(w0b if dc == 0 else ()),
                            incs=([(sUp, 1)] if dc == NDC - 1 else ()), tag=f"upB.{i}")
                    up_mm_done[i] = sUp.n
                    P.op("scalar", lambda e, par=par: e.activation(sg_t(par), up_ps(par, 0), AF.Silu),
                         waits=[(sUp, sUp.n)], incs=[(sSil, 1)], tag=f"sil.{i}")
                    sil_done[i] = sSil.n
                    P.op("vector", lambda e, par=par: e.tensor_tensor(
                        hbf_t(par), sg_t(par), up_ps(par, 1), ALU.mult),
                        waits=[(sSil, sSil.n)], incs=[(sGat, 1)], tag=f"gat.{i}")
                    gat_done[i] = sGat.n
                    pend_up.append((c, ts, par))
                    if i >= 1:
                        up_tail(i - 1)
                    up_idx += 1
        up_tail(up_idx - 1)
        up_all = sUp.n
        h2w_all = sy.snap()

        h2r = h2d.rearrange("c s (js jl) t -> s js c jl t", js=2)
        h3r4 = h3d.rearrange("s (js cp) jl t -> s js jl cp t", js=2)
        hin_t = lambda sl: W1b[:, sl * 1024:(sl + 1) * 1024]
        h3st_t = lambda sl: W0[:, sl * 1024:(sl + 1) * 1024]
        x1r_t = lambda par: SCR[:, par * 512:(par + 1) * 512]
        zsq2_t = lambda par: SCR[:, 1024 + par * 512: 1024 + (par + 1) * 512]

        def z2_t(dc):
            return CF32[:, dc * 512:(dc + 1) * 512] if dc < 8 else DF32[:, (dc - 8) * 512:(dc - 7) * 512]

        h64_units = [(ts, p2) for ts in range(NTS) for p2 in range(32)]
        h64_idx = 0
        h64_mm_done, h64_ev_done, h64_w_done = {}, {}, {}
        h3w_at_ts = {}

        def emit_h64(n):
            nonlocal h64_idx
            for _ in range(n):
                if h64_idx >= len(h64_units):
                    return
                i = h64_idx
                ts, p2 = h64_units[i]
                sl = i % 8
                lw = list(h2w_all)
                if i >= 8:
                    lw.append((sH64, h64_mm_done[i - 8]))
                ld_tok = gp.dma(lambda e, ts=ts, p2=p2, sl=sl: e.dma_start(
                    hin_t(sl), h2r[ts, :, :, 2 * p2:2 * p2 + 2, :]), waits=lw, tag=f"h64l.{i}")
                mw = list(ld_tok)
                if i >= 1:
                    mw.append((sH64e, h64_ev_done[i - 1]))
                for half in range(2):
                    P.op("tensor", lambda e, sl=sl, half=half: e.matmul(
                        PTO[:, half * 512:(half + 1) * 512], h64b[:],
                        hin_t(sl)[:, half * 512:(half + 1) * 512], start=True, stop=True),
                        waits=(mw if half == 0 else ()),
                        incs=([(sH64, 1)] if half == 1 else ()), tag=f"h64m.{i}")
                h64_mm_done[i] = sH64.n
                ew = [(sH64, sH64.n)]
                if i >= 8:
                    ew += h64_w_done[i - 8]
                P.op("vector", lambda e, sl=sl: e.tensor_copy(h3st_t(sl), PTO[:, 0:1024]),
                     waits=ew, incs=[(sH64e, 1)], tag=f"h64e.{i}")
                h64_ev_done[i] = sH64e.n
                h64_w_done[i] = gp.dma(lambda e, ts=ts, p2=p2, sl=sl: e.dma_start(
                    h3d[ts, :, 2 * p2:2 * p2 + 2, :], h3st_t(sl)),
                    waits=[(sH64e, sH64e.n)], tag=f"h64w.{i}")
                h64_idx += 1
                if (i + 1) % 32 == 0:
                    h3w_at_ts[(i + 1) // 32 - 1] = gp.snap()

        emit_h64(32)

        yw = []
        y_pend = []
        y_tok = {}

        def flush_y():
            ts_, dc_, apl = y_pend.pop(0)
            y_tok[(ts_, dc_)] = sc.dma(lambda e, dc=dc_, ts=ts_: e.dma_start(
                yT[dc * 128:(dc + 1) * 128, ts * 512:(ts + 1) * 512], z2_t(dc)),
                waits=[(sApl, apl)], tag=f"yw.{ts_}.{dc_}")
            yw.append(y_tok[(ts_, dc_)])

        dn_done, dne_done = {}, {}
        st2_done = {}
        dn_mm_at_ts = {}
        gat_all = sGat.n
        for ts in range(NTS):
            lw = list(h3w_at_ts[ts])
            if ts == 0:
                lw += [(sApl, apl_gate[0])] + x1w_all
            else:
                lw.append((sDn, dn_mm_at_ts[ts - 1]))
            Av = A[:].rearrange("p (c t) -> p c t", t=512)
            for i8 in range(4):
                for js in range(2):
                    gp.dma(lambda e, ts=ts, i8=i8, js=js: e.dma_start(
                        Av[js * 64:(js + 1) * 64, i8 * 16:(i8 + 1) * 16, :],
                        h3r4[ts, js, :, i8 * 16:(i8 + 1) * 16, :]),
                        waits=(lw if (i8 == 0 and js == 0) else ()), tag=f"h3l.{ts}.{i8}.{js}")
            h3l_done = gp.snap()
            prev_bc2 = sBc.n
            stat_pend2 = []
            st_done2 = {}

            def emit_stats2(ts=ts):
                dc, zpar, zc, sqc = stat_pend2.pop(0)
                P.op("tensor", lambda e, dc=dc, zpar=zpar: e.matmul(
                    statS, onescolf[:], z2_t(dc), start=(dc == 0), stop=(dc == NDC - 1)),
                    waits=[(sDnE, zc)], tag=f"st1b.{ts}.{dc}")
                P.op("tensor", lambda e, dc=dc, zpar=zpar: e.matmul(
                    statS2, onescolf[:], zsq2_t(zpar), start=(dc == 0), stop=(dc == NDC - 1)),
                    waits=[(sSq, sqc)],
                    incs=([(sSt, 1)] if dc == NDC - 1 else ()), tag=f"st2b.{ts}.{dc}")
                st2_done[(ts, dc)] = sSt.n

            for dg in range(8):
                wpar = dg % 2
                ldw = []
                if ts == 0 and dg < 2:
                    ldw.append((sUp, up_all))
                else:
                    key = (ts, dg - 2) if dg >= 2 else (ts - 1, dg + 6)
                    ldw.append((sDn, dn_done[key]))
                gp.dma(lambda e, dg=dg, wpar=wpar: e.dma_start(
                    B[:, wpar * 16384:(wpar + 1) * 16384], w2t[dg]),
                    waits=ldw, tag=f"w2l.{ts}.{dg}")
                w2_ld = gp.snap()
                for _ in range(2):
                    if y_pend:
                        flush_y()
                for dc2 in range(2):
                    dc = dg * 2 + dc2
                    dps = PSA[:, dc2 * 512:(dc2 + 1) * 512]
                    zpar = dc % 2
                    xw = []
                    if ts > 0 or dc >= 1:
                        xw.append((sDnE, dne_done[(ts, dc - 1) if dc >= 1 else (ts - 1, 15)]))
                    else:
                        xw.append((sGat, gat_all))
                    x1r_tok = gp.dma(lambda e, dc=dc, ts=ts, zpar=zpar: e.dma_start(
                        x1r_t(zpar), x1d[dc, :, ts * 512:(ts + 1) * 512]),
                        waits=xw + (x1w_all if ts == 0 and dc == 0 else []), tag=f"x1r.{ts}.{dc}")
                    mw = w2_ld + h3l_done
                    if ts > 0 or dc >= 2:
                        mw.append((sDnE, dne_done[(ts, dc - 2) if dc >= 2 else (ts - 1, dc + 14)]))
                    for cc in range(64):
                        P.op("tensor", lambda e, cc=cc, dc2=dc2, dps=dps, wpar=wpar: e.matmul(
                            dps, B[:, wpar * 16384 + cc * 256 + dc2 * 128: wpar * 16384 + cc * 256 + (dc2 + 1) * 128],
                            A[:, cc * 512:(cc + 1) * 512],
                            start=(cc == 0), stop=(cc == 63)),
                            waits=(mw if cc == 0 else ()),
                            incs=([(sDn, 1)] if cc == 63 else ()), tag=f"dn.{ts}.{dc}")
                    z2w = [(sDn, sDn.n)] + x1r_tok
                    if (ts - 1, dc) in y_tok:
                        z2w += y_tok[(ts - 1, dc)]
                    P.op("vector", lambda e, dc=dc, dps=dps, zpar=zpar: e.tensor_tensor(
                        z2_t(dc), dps, x1r_t(zpar), ALU.add),
                        waits=z2w, incs=[(sDnE, 1)], tag=f"z2.{ts}.{dc}")
                    dne_done[(ts, dc)] = sDnE.n
                    zw = [(sDnE, sDnE.n)]
                    if ts > 0 or dc >= 2:
                        zw.append((sSt, st2_done.get((ts, dc - 2) if dc >= 2 else (ts - 1, dc + 14), 0)))
                    P.op("vector", lambda e, dc=dc, zpar=zpar: e.tensor_tensor(
                        zsq2_t(zpar), z2_t(dc), z2_t(dc), ALU.mult),
                        waits=zw, incs=[(sSq, 1)], tag=f"zsq2.{ts}.{dc}")
                    stat_pend2.append((dc, zpar, dne_done[(ts, dc)], sSq.n))
                    if len(stat_pend2) >= 2:
                        emit_stats2()
                dn_done[(ts, dg)] = sDn.n
                emit_h64(4)
            while stat_pend2:
                emit_stats2()
            dn_mm_at_ts[ts] = sDn.n
            emit_ln_finalize(statS, statS2, R1, B1, prev_bc2, f"ln2.{ts}")
            for dc in range(NDC):
                aw = [(sBc, sBc.n)]
                P.op("vector", lambda e, dc=dc: e.tensor_tensor(
                    z2_t(dc), z2_t(dc), R1, ALU.mult),
                    waits=aw, tag=f"apy1.{ts}.{dc}")
                P.op("vector", lambda e, dc=dc: e.tensor_tensor(
                    z2_t(dc), z2_t(dc), B1, ALU.add),
                    incs=[(sApl, 1)], tag=f"apy2.{ts}.{dc}")
                y_pend.append((ts, dc, sApl.n))
            if ts == NTS - 1:
                while y_pend:
                    flush_y()

        P.op("sync", None, waits=sy.snap(), tag="final.sy")
        P.op("gpsimd", None, waits=gp.snap(), tag="final.gp")
        P.op("scalar", None, waits=sc.snap(), tag="final.sc")

        P.check_deadlock()
        nops = {e: len(P.ops[e]) for e in ENGINES}

        with nc.Block() as block:
            @block.tensor
            def _(eng):
                P.replay("tensor", eng)

            @block.vector
            def _(eng):
                P.replay("vector", eng)

            @block.scalar
            def _(eng):
                P.replay("scalar", eng)

            @block.sync
            def _(eng):
                P.replay("sync", eng)

            @block.gpsimd
            def _(eng):
                P.replay("gpsimd", eng)

    return nc, nops


_CACHE = {}


def _get_nc(taps=False):
    key = bool(taps)
    if key not in _CACHE:
        _CACHE[key] = build_nc(taps=taps)
    return _CACHE[key]


def make_in_maps(inputs):
    x = np.asarray(inputs["L0_attn_hidden_states"], dtype=np.float32)
    consts = host_constants()
    shared = {
        "wq": np.ascontiguousarray(inputs["L0_attn_w_q"], dtype=np.float32),
        "wk": np.ascontiguousarray(inputs["L0_attn_w_k"], dtype=np.float32),
        "wv": np.ascontiguousarray(inputs["L0_attn_w_v"], dtype=np.float32),
        "wo": np.ascontiguousarray(inputs["L0_attn_w_o"], dtype=np.float32),
        "w1t": np.ascontiguousarray(
            np.asarray(inputs["L0_ffn_W1"], dtype=np.float32).T.reshape(16, 128, 16, 512)
            .transpose(3, 1, 0, 2).reshape(16, 128, 16, 512)) if False else
            np.ascontiguousarray(np.asarray(inputs["L0_ffn_W1"], dtype=np.float32).T
                                 .reshape(16, 128, 16, 512).transpose(2, 1, 0, 3)),
        "w3t": np.ascontiguousarray(np.asarray(inputs["L0_ffn_W3"], dtype=np.float32).T
                                    .reshape(16, 128, 16, 512).transpose(2, 1, 0, 3)),
        "w2t": np.ascontiguousarray(np.asarray(inputs["L0_ffn_W2"], dtype=np.float32).T
                                    .reshape(64, 128, 8, 256).transpose(2, 1, 0, 3)),
    }
    shared.update(consts)
    in_maps = []
    for i in range(8):
        m = dict(shared)
        m["xT"] = np.ascontiguousarray(x[i].T)
        in_maps.append(m)
    return in_maps


def kernel(**inputs):
    nc, _ = _get_nc(taps=False)
    in_maps = make_in_maps(inputs)
    res = run_bass_kernel_spmd(nc, in_maps, core_ids=list(range(8)))
    out = np.empty((8, S, D), dtype=np.float32)
    for i in range(8):
        out[i] = res.results[i]["yT"].T
    return out


# version 14
# speedup vs baseline: 1.1738x; 1.1738x over previous
import numpy as np
import ml_dtypes
import concourse.bass as bass
import concourse.mybir as mybir
from concourse.bass_utils import run_bass_kernel_spmd

F32 = mybir.dt.float32
BF16 = mybir.dt.bfloat16
AF = mybir.ActivationFunctionType
ALU = mybir.AluOpType
AX = mybir.AxisListType

S, D, NH, NKV, HD, WIN, HF = 2048, 2048, 16, 4, 128, 1024, 8192
NDC = D // 128
NTS, TS = 4, 512
NQT = S // 128
QG, NQG = 512, 4
EPS = 1e-5
FW_SC = float(1.0 / np.sqrt(HF))
NEG = -1.0e30
ENGINES = ("tensor", "vector", "scalar", "sync", "gpsimd")


def _hadamard(n):
    h = np.array([[1.0]], dtype=np.float32)
    while h.shape[0] < n:
        h = np.block([[h, h], [h, -h]])
    return h.astype(np.float32)


def host_constants():
    t = np.arange(S, dtype=np.float32)
    inv = 1.0 / (10000.0 ** (np.arange(0, HD, 2, dtype=np.float32) / HD))
    fr = np.outer(t, inv)
    emb = np.concatenate([fr, fr], axis=-1)
    cos = np.ascontiguousarray(np.cos(emb).T)
    sin = np.ascontiguousarray(np.sin(emb).T)
    sgn = np.ones((HD, 1), dtype=np.float32)
    sgn[:64] = -1.0
    sc = np.float32(1.0 / np.sqrt(HD))
    qq = np.arange(128)[:, None]
    kk = np.arange(128)[None, :]
    return {
        "cosq": (cos * sc).astype(np.float32),
        "sinq": (sin * sgn * sc).astype(np.float32),
        "cosk": cos.astype(np.float32),
        "sink": (sin * sgn).astype(np.float32),
        "ident": np.eye(128, dtype=np.float32),
        "h128": _hadamard(128),
        "h64x2": np.kron(np.eye(2, dtype=np.float32), _hadamard(64)),
        "onescol": np.ones((128, 1), dtype=np.float32),
        "onesrow": np.ones((1, 128), dtype=np.float32),
        "negrow": -np.ones((1, 128), dtype=np.float32),
        "cvec": np.tile(np.array([[EPS, -0.5, FW_SC, 1.0 / D]], dtype=np.float32), (128, 1)),
        "maskd": np.where(qq >= kk, np.float32(0), np.float32(NEG)),
        "maskf": np.where(qq <= kk, np.float32(0), np.float32(NEG)),
    }


class Sem:
    def __init__(self, handle, name):
        self.h, self.name, self.n = handle, name, 0


class Prog:
    def __init__(self):
        self.ops = {e: [] for e in ENGINES}
        self.serial_sem = None
        self.serial_dma = False

    def op(self, eng, fn, waits=(), incs=(), tag=""):
        waits = list(waits)
        incs = list(incs)
        serial_followup = None
        if self.serial_sem is not None and fn is not None:
            so = self.serial_sem
            waits.append((so, so.n))
            if not incs:
                incs.append((so, 1))
            else:
                serial_followup = (eng, so)
        self.serial_dma = False
        rec = []
        for sem, amt in incs:
            sem.n += amt
            rec.append((sem, amt))
        self.ops[eng].append((fn, tuple(waits), tuple(rec), tag))
        if serial_followup is not None:
            eng2, so = serial_followup
            s0, _ = rec[0]
            self.ops[eng2].append((lambda e: e.nop(), ((s0, s0.n),), ((so, 1),), tag + ".serord"))
            so.n += 1

    def barrier(self, sbar, k):
        for e in ENGINES:
            self.op(e, lambda eng: eng.nop(), incs=[(sbar, 1)], tag=f"bar{k}.inc")
        for e in ENGINES:
            self.op(e, None, waits=[(sbar, 5 * k)], tag=f"bar{k}.wait")

    def check_deadlock(self):
        idx = {e: 0 for e in ENGINES}
        vals = {}
        progress = True
        while progress:
            progress = False
            for e in ENGINES:
                while idx[e] < len(self.ops[e]):
                    fn, waits, incs, tag = self.ops[e][idx[e]]
                    if any(vals.get(s, 0) < v for s, v in waits):
                        break
                    for s, amt in incs:
                        vals[s] = vals.get(s, 0) + amt
                    idx[e] += 1
                    progress = True
        stuck = {e: i for e, i in idx.items() if i < len(self.ops[e])}
        if stuck:
            lines = []
            for e, i in stuck.items():
                fn, waits, incs, tag = self.ops[e][i]
                wd = [(s.name, v, vals.get(s, 0)) for s, v in waits]
                lines.append(f"  {e}[{i}/{len(self.ops[e])}] tag={tag} waits={wd}")
            raise RuntimeError("DEADLOCK in emitted program:\n" + "\n".join(lines))

    def replay(self, name, eng):
        for fn, waits, incs, tag in self.ops[name]:
            for s, v in waits:
                eng.wait_ge(s.h, v)
            if fn is None:
                continue
            inst = fn(eng)
            for s, amt in incs:
                inst = inst.then_inc(s.h, amt)


class DmaPool:

    def __init__(self, P, eng, mksem, name, K=8):
        self.P, self.eng, self.K = P, eng, K
        self.sems = [mksem(f"{name}{i}") for i in range(K)]
        self.uses = [0] * K
        self.i = 0

    def dma(self, fn, waits=(), tag=""):
        j = self.i % self.K
        s = self.sems[j]
        w = list(waits)
        if self.uses[j]:
            w.append((s, 16 * self.uses[j]))
        self.P.serial_dma = True
        self.P.op(self.eng, fn, waits=w, incs=[(s, 16)], tag=tag)
        self.uses[j] += 1
        self.i += 1
        return [(s, 16 * self.uses[j])]

    def snap(self):
        return [(s, 16 * u) for s, u in zip(self.sems, self.uses) if u]


def build_nc(taps=False, serial=False):
    import contextlib
    nc = bass.Bass("TRN2", target_bir_lowering=False, dynamic_dma_scratch_size=16384)

    def din(name, shape):
        return nc.dram_tensor(name, list(shape), F32, kind="ExternalInput").ap()

    xT = din("xT", (D, S))
    wq, wk, wv, wo = din("wq", (D, D)), din("wk", (D, 512)), din("wv", (D, 512)), din("wo", (D, D))
    w1t = din("w1t", (16, 128, 16, 512))
    w3t = din("w3t", (16, 128, 16, 512))
    w2t = din("w2t", (8, 128, 64, 256))
    cosqd, sinqd = din("cosq", (128, S)), din("sinq", (128, S))
    coskd, sinkd = din("cosk", (128, S)), din("sink", (128, S))
    identd, h128d, h64d = din("ident", (128, 128)), din("h128", (128, 128)), din("h64x2", (128, 128))
    onescold, onesrowd, negrowd = din("onescol", (128, 1)), din("onesrow", (1, 128)), din("negrow", (1, 128))
    maskdd, maskfd = din("maskd", (128, 128)), din("maskf", (128, 128))
    cvecd = din("cvec", (128, 4))

    yT = nc.dram_tensor("yT", [D, S], F32, kind="ExternalOutput").ap()
    ikind = "ExternalOutput" if taps else "Internal"
    h2d = nc.dram_tensor("h2d", [64, NTS, 128, TS], BF16, kind=ikind).ap()
    h3d = nc.dram_tensor("h3d", [NTS, 128, 64, TS], BF16, kind=ikind).ap()
    x1d = nc.dram_tensor("x1d", [NDC, 128, S], F32, kind=ikind).ap()
    if taps:
        tqfm = nc.dram_tensor("tqfm", [128, NH * S], BF16, kind="ExternalOutput").ap()
        tkfm = nc.dram_tensor("tkfm", [128, NKV * S], BF16, kind="ExternalOutput").ap()
        tvtm = nc.dram_tensor("tvtm", [128, 16 * 512], BF16, kind="ExternalOutput").ap()
        tac = nc.dram_tensor("tac", [128, NH * QG], BF16, kind="ExternalOutput").ap()

    P = Prog()
    stack = contextlib.ExitStack()

    def sbuf(name, shape, dt):
        return stack.enter_context(nc.sbuf_tensor(name, list(shape), dt))

    def psum(name, shape, dt=F32):
        return stack.enter_context(nc.psum_tensor(name, list(shape), dt))

    def sem(name):
        return Sem(stack.enter_context(nc.semaphore(name)), name)

    with stack:
        A = sbuf("A", (128, 32768), BF16)
        B = sbuf("B", (128, 32768), BF16)
        C = sbuf("C", (128, 8192), BF16)
        Dv = sbuf("Dv", (128, 8192), BF16)
        W0 = sbuf("W0", (128, 8192), BF16)
        W1b = sbuf("W1b", (128, 8192), BF16)
        SCR = sbuf("SCR", (128, 3584), F32)

        AF32 = A[:].bitcast(F32)
        CF32 = C[:].bitcast(F32)
        DF32 = Dv[:].bitcast(F32)
        SCRb = SCR[:].bitcast(BF16)

        nm2 = sbuf("nm2", (128, 2), F32)
        ssum2 = sbuf("ssum2", (128, 2), F32)
        rr2 = sbuf("rr2", (128, 2), F32)
        h64b = sbuf("h64b", (128, 128), BF16)
        onescolb = sbuf("onescolb", (128, 1), BF16)
        onescolf = sbuf("onescolf", (128, 1), F32)
        onesrow_s = sbuf("onesrow_s", (1, 128), F32)
        negrow_s = sbuf("negrow_s", (1, 128), F32)
        cvec_s = sbuf("cvec_s", (128, 4), F32)

        PSA = psum("PSA", (128, 1536), F32)
        PSB = psum("PSB", (128, 1536), F32)
        PTO = psum("PTO", (128, 1024), F32)
        PTObf = PTO[:].bitcast(BF16)

        maskf_s = AF32[:, 15488:15616]
        maskd_s = AF32[:, 15616:15744]
        identb = A[:, 31488:31616]
        h128b = A[:, 0:128]
        mrow1 = SCR[0:1, 2560:3072]
        vrow1 = SCR[0:1, 3072:3584]
        trow = PSB[0:1, 1024:1536]

        sBar = sem("sBar")
        sPre, sRpV, sRpA = sem("sPre"), sem("sRpV"), sem("sRpA")
        sQK, sMax, sExp, sRec, sScl = sem("sQK"), sem("sMax"), sem("sExp"), sem("sRec"), sem("sScl")
        sTP, sTPe, sPV, sOev = sem("sTP"), sem("sTPe"), sem("sPV"), sem("sOev")
        sWo, sZ, sZb, sSq, sSt = sem("sWo"), sem("sZ"), sem("sZb"), sem("sSq"), sem("sSt")
        sFinV, sFinA, sBc, sApl = sem("sFinV"), sem("sFinA"), sem("sBc"), sem("sApl")
        sUp, sSil, sGat, sH2e = sem("sUp"), sem("sSil"), sem("sGat"), sem("sH2e")
        sH64, sH64e, sDn, sDnE = sem("sH64"), sem("sH64e"), sem("sDn"), sem("sDnE")
        gp = DmaPool(P, "gpsimd", sem, "dG", 8)
        sy = DmaPool(P, "sync", sem, "dS", 8)
        sc = DmaPool(P, "scalar", sem, "dC", 8)
        if serial:
            P.serial_sem = sem("sOrd")

        sy.dma(lambda e: e.dma_start(cvec_s[:], cvecd), tag="ld.cvec")
        sy.dma(lambda e: e.dma_start(onesrow_s[:], onesrowd), tag="ld.onesrow")
        consts_sy = sy.dma(lambda e: e.dma_start(negrow_s[:], negrowd), tag="ld.negrow") + sy.snap()
        gp.dma(lambda e: e.dma_start(h64b[:], h64d), tag="ld.h64")
        gp.dma(lambda e: e.dma_start(onescolb[:], onescold), tag="ld.onescol")
        sy.dma(lambda e: e.dma_start(onescolf[:], onescold), tag="ld.onescolf")
        xv = xT.rearrange("(dc p) t -> p dc t", p=128)
        for i in range(4):
            gp.dma(lambda e, i=i: e.dma_start(A[:, i * 8192:(i + 1) * 8192], xv[:, 4 * i:4 * (i + 1), :]),
                   tag=f"ld.x{i}")

        t1f = PTO[:, 0:512]
        swpf = PSA[:, 512:1024]
        t2f = SCR[:, 2048:2560]
        cos_sl = lambda par: SCR[:, par * 1024: par * 1024 + 512]
        sin_sl = lambda par: SCR[:, par * 1024 + 512: par * 1024 + 1024]

        def pp(par):
            return (PSA if par == 0 else PSB)[:, 0:512]

        rope_done = []

        def emit_rope(par, dst, cs, sn, tag):
            mmw = (sPre, sPre.n)
            P.op("vector", lambda e, par=par, cs=cs: e.tensor_tensor(t1f, pp(par), cs, ALU.mult),
                 waits=[mmw], incs=[(sRpV, 1)], tag=f"{tag}.t1")
            swq = [(sRpV, sRpV.n)]
            P.op("scalar", lambda e, par=par: e.activation(
                swpf[0:64, :], (PSA if par == 0 else PSB)[64:128, 0:512], AF.Copy),
                waits=[mmw] + swq, incs=[(sRpA, 1)], tag=f"{tag}.sw0")
            P.op("scalar", lambda e, par=par: e.activation(
                swpf[64:128, :], (PSA if par == 0 else PSB)[0:64, 0:512], AF.Copy),
                incs=[(sRpA, 1)], tag=f"{tag}.sw1")
            P.op("vector", lambda e, sn=sn: e.tensor_tensor(t2f, swpf[:], sn, ALU.mult),
                 waits=[(sRpA, sRpA.n)], incs=[(sRpV, 1)], tag=f"{tag}.t2")
            P.op("vector", lambda e, dst=dst: e.tensor_tensor(dst, t2f, t1f, ALU.add),
                 incs=[(sRpV, 1)], tag=f"{tag}.add")
            rope_done.append(sRpV.n)

        wkv = wk.rearrange("(kc p) o -> p kc o", p=128)
        gp.dma(lambda e: e.dma_start(W0[:], wkv), tag="ld.wk")
        k_ld = gp.snap()
        nunit = 0
        tab_user = {0: 0, 1: 0}
        tab_ready = {0: [], 1: []}

        def load_tab(par, ts, cosd, sind, tag):
            t1 = sy.dma(lambda e, par=par, ts=ts, cosd=cosd: e.dma_start(
                cos_sl(par), cosd[:, ts * 512:(ts + 1) * 512]),
                waits=[(sRpV, tab_user[par])], tag=f"{tag}.cos")
            t2 = sy.dma(lambda e, par=par, ts=ts, sind=sind: e.dma_start(
                sin_sl(par), sind[:, ts * 512:(ts + 1) * 512]), tag=f"{tag}.sin")
            tab_ready[par] = t1 + t2

        for ts in range(NTS):
            tpar = ts % 2
            load_tab(tpar, ts, coskd, sinkd, f"ktab.{ts}")
            for g in range(NKV):
                par = nunit % 2
                w0 = k_ld + tab_ready[tpar]
                if nunit >= 2:
                    w0 = w0 + [(sRpV, rope_done[nunit - 2])]
                for kc in range(NDC):
                    P.op("tensor", lambda e, g=g, ts=ts, kc=kc, par=par: e.matmul(
                        pp(par), W0[:, kc * 512 + g * 128: kc * 512 + (g + 1) * 128],
                        A[:, kc * 2048 + ts * 512: kc * 2048 + (ts + 1) * 512],
                        start=(kc == 0), stop=(kc == NDC - 1)),
                        waits=(w0 if kc == 0 else ()),
                        incs=([(sPre, 1)] if kc == NDC - 1 else ()), tag=f"k.mm.{g}.{ts}")
                emit_rope(par, C[:, g * 2048 + ts * 512: g * 2048 + (ts + 1) * 512],
                          cos_sl(tpar), sin_sl(tpar), f"k.rope.{g}.{ts}")
                tab_user[tpar] = sRpV.n
                nunit += 1

        wvv = wv.rearrange("(kc p) o -> p kc o", p=128)
        gp.dma(lambda e: e.dma_start(W0[:], wvv), waits=[(sPre, sPre.n)], tag="ld.wv")
        v_ld = gp.snap()
        for tc in range(16):
            par = nunit % 2
            w0 = list(v_ld)
            if nunit >= 2:
                w0.append((sRpV, rope_done[nunit - 2]))
            for kc in range(NDC):
                P.op("tensor", lambda e, tc=tc, kc=kc, par=par: e.matmul(
                    pp(par), A[:, kc * 2048 + tc * 128: kc * 2048 + (tc + 1) * 128],
                    W0[:, kc * 512:(kc + 1) * 512],
                    start=(kc == 0), stop=(kc == NDC - 1)),
                    waits=(w0 if kc == 0 else ()),
                    incs=([(sPre, 1)] if kc == NDC - 1 else ()), tag=f"v.mm.{tc}")
            P.op("vector", lambda e, tc=tc, par=par: e.tensor_copy(
                Dv[:, tc * 512:(tc + 1) * 512], pp(par)),
                waits=[(sPre, sPre.n)], incs=[(sRpV, 1)], tag=f"v.ev.{tc}")
            rope_done.append(sRpV.n)
            nunit += 1

        wqv = wq.rearrange("(kc p) o -> p kc o", p=128)
        q_mm_at_eighth = {}
        for ei in range(8):
            hb = ei % 2
            ldw = [] if ei < 2 else [(sPre, q_mm_at_eighth[ei - 2])]
            q_tok = gp.dma(lambda e, ei=ei, hb=hb: e.dma_start(
                W1b[:, hb * 4096:(hb + 1) * 4096], wqv[:, :, ei * 256:(ei + 1) * 256]),
                waits=ldw, tag=f"ld.wq{ei}")
            for ts in range(NTS):
                tpar = ts % 2
                load_tab(tpar, ts, cosqd, sinqd, f"qtab.{ei}.{ts}")
                for qc2 in range(2):
                    h = ei * 2 + qc2
                    par = nunit % 2
                    w0 = v_ld + q_tok + tab_ready[tpar]
                    if nunit >= 2:
                        w0 = w0 + [(sRpV, rope_done[nunit - 2])]
                    for kc in range(NDC):
                        P.op("tensor", lambda e, h=h, ts=ts, kc=kc, par=par, hb=hb, qc2=qc2: e.matmul(
                            pp(par),
                            W1b[:, hb * 4096 + kc * 256 + qc2 * 128: hb * 4096 + kc * 256 + (qc2 + 1) * 128],
                            A[:, kc * 2048 + ts * 512: kc * 2048 + (ts + 1) * 512],
                            start=(kc == 0), stop=(kc == NDC - 1)),
                            waits=(w0 if kc == 0 else ()),
                            incs=([(sPre, 1)] if kc == NDC - 1 else ()), tag=f"q.mm.{h}.{ts}")
                    emit_rope(par, B[:, h * 2048 + ts * 512: h * 2048 + (ts + 1) * 512],
                              cos_sl(tpar), sin_sl(tpar), f"q.rope.{h}.{ts}")
                    tab_user[tpar] = sRpV.n
                    nunit += 1
            q_mm_at_eighth[ei] = sPre.n

        P.barrier(sBar, 1)
        mask_tok = sy.dma(lambda e: e.dma_start(maskd_s, maskdd), tag="ld.maskd")
        mask_tok += sy.dma(lambda e: e.dma_start(maskf_s, maskfd), tag="ld.maskf")
        ident_tok = gp.dma(lambda e: e.dma_start(identb, identd), tag="ld.ident")
        if taps:
            sy.dma(lambda e: e.dma_start(tqfm, B[:]), tag="tap.q")
            sy.dma(lambda e: e.dma_start(tkfm, C[:]), tag="tap.k")
            sy.dma(lambda e: e.dma_start(tvtm, Dv[:]), tag="tap.v")

        Pbf = A[:, 20480:21632]
        PT_sb = A[:, 21632:22784]
        attn_col = A[:, 22784:30976]
        O_ps = PTO[:, 768:896]

        def S_ps(par):
            return PSA if par == 0 else PSB

        z1_t = lambda dc: AF32[:, dc * 512:(dc + 1) * 512]
        xres_t = lambda par: AF32[:, 8192 + par * 512: 8192 + (par + 1) * 512]
        x1st_t = lambda par: SCR[:, par * 512:(par + 1) * 512]
        zsq1_t = lambda par: AF32[:, 9216 + par * 512: 9216 + (par + 1) * 512]

        unit = 0
        pend = []
        exp_done = {}
        tp_done = {}
        apl_gate = [0]

        def emit_front(qt, h, qrel):
            nonlocal unit
            par = unit % 2
            nkt = min(qt + 1, 9)
            kt0 = max(0, qt - 8)
            g = h // 4
            w0 = []
            if unit >= 2:
                w0.append((sExp, exp_done[unit - 2]))
            if unit % 64 == 0 and apl_gate[0]:
                w0.append((sApl, apl_gate[0]))
            for i in range(nkt):
                kt = kt0 + i
                P.op("tensor", lambda e, h=h, qt=qt, kt=kt, i=i, par=par, g=g: e.matmul(
                    S_ps(par)[:, i * 128:(i + 1) * 128],
                    B[:, h * 2048 + qt * 128: h * 2048 + qt * 128 + 128],
                    C[:, g * 2048 + kt * 128: g * 2048 + kt * 128 + 128],
                    start=True, stop=True),
                    waits=(w0 if i == 0 else ()),
                    incs=([(sQK, 1)] if i == nkt - 1 else ()), tag=f"qk.{unit}")
            mw = [(sQK, sQK.n)]
            if unit == 0:
                mw = mw + mask_tok
            if qt >= 8:
                P.op("vector", lambda e, par=par: e.tensor_tensor(
                    S_ps(par)[:, 0:128], S_ps(par)[:, 0:128], maskf_s, ALU.add),
                    waits=mw, tag=f"mf.{unit}")
                mw = []
            P.op("vector", lambda e, par=par, nkt=nkt: e.tensor_tensor(
                S_ps(par)[:, (nkt - 1) * 128: nkt * 128],
                S_ps(par)[:, (nkt - 1) * 128: nkt * 128], maskd_s, ALU.add),
                waits=mw, tag=f"md.{unit}")
            P.op("vector", lambda e, par=par, nkt=nkt: e.reduce_max(
                nm2[:, par:par + 1], S_ps(par)[:, 0:nkt * 128], axis=AX.X, negate=True),
                incs=[(sMax, 1)], tag=f"max.{unit}")
            pend.append((unit, qt, h, nkt, kt0, par, qrel))
            unit += 1

        def emit_midback():
            u, qt, h, nkt, kt0, par, qrel = pend.pop(0)
            g = h // 4
            expw = [(sMax, sMax.n)]
            if u >= 1 and (u - 1) in tp_done:
                expw.append((sTP, tp_done[u - 1]))
            P.op("scalar", lambda e, par=par, nkt=nkt: e.activation(
                Pbf[:, 0:nkt * 128], S_ps(par)[:, 0:nkt * 128], AF.Exp,
                bias=nm2[:, par:par + 1], accum_out=ssum2[:, par:par + 1]),
                waits=expw, incs=[(sExp, 1)], tag=f"exp.{u}")
            exp_done[u] = sExp.n
            P.op("vector", lambda e, par=par: e.reciprocal(rr2[:, par:par + 1], ssum2[:, par:par + 1]),
                 waits=[(sExp, sExp.n)], incs=[(sRec, 1)], tag=f"rec.{u}")
            P.op("vector", lambda e, par=par, nkt=nkt: e.tensor_scalar(
                Pbf[:, 0:nkt * 128], Pbf[:, 0:nkt * 128], rr2[:, par:par + 1], None, ALU.mult),
                waits=[(sRec, sRec.n)], incs=[(sScl, 1)], tag=f"scl.{u}")
            for i in range(nkt):
                P.op("tensor", lambda e, i=i: e.transpose(
                    PTObf[:, i * 128:(i + 1) * 128], Pbf[:, i * 128:(i + 1) * 128], identb),
                    waits=(([(sScl, sScl.n), (sOev, sOev.n)] + (ident_tok if u == 0 else []))
                          if i == 0 else ()),
                    incs=([(sTP, 1)] if i == nkt - 1 else ()), tag=f"tp.{u}")
            tp_done[u] = sTP.n
            P.op("scalar", lambda e, nkt=nkt: e.activation(
                PT_sb[:, 0:nkt * 128], PTObf[:, 0:nkt * 128], AF.Copy),
                waits=[(sTP, sTP.n)], incs=[(sTPe, 1)], tag=f"tpe.{u}")
            for i in range(nkt):
                kt = kt0 + i
                P.op("tensor", lambda e, i=i, kt=kt, g=g, nkt=nkt: e.matmul(
                    O_ps, Dv[:, kt * 512 + g * 128: kt * 512 + (g + 1) * 128],
                    PT_sb[:, i * 128:(i + 1) * 128],
                    start=(i == 0), stop=(i == nkt - 1)),
                    waits=([(sTPe, sTPe.n)] if i == 0 else ()),
                    incs=([(sPV, 1)] if i == nkt - 1 else ()), tag=f"pv.{u}")
            P.op("scalar", lambda e, h=h, qrel=qrel: e.activation(
                attn_col[:, h * 512 + qrel: h * 512 + qrel + 128], O_ps, AF.Copy),
                waits=[(sPV, sPV.n)], incs=[(sOev, 1)], tag=f"oev.{u}")

        def emit_ln_finalize(statS, statS2, R_ps, B_ps, prev_bc, tag, mrow=None, vrow=None):
            if mrow is None:
                mrow, vrow = mrow1, vrow1
            P.op("vector", lambda e: e.tensor_scalar(mrow, statS, cvec_s[0:1, 3:4], None, ALU.mult),
                 waits=[(sSt, sSt.n), (sBc, prev_bc)], tag=f"{tag}.mean")
            P.op("vector", lambda e: e.tensor_scalar(trow, statS2, cvec_s[0:1, 3:4], None, ALU.mult),
                 tag=f"{tag}.e2")
            P.op("vector", lambda e: e.tensor_tensor(vrow, mrow, mrow, ALU.mult),
                 tag=f"{tag}.m2")
            P.op("vector", lambda e: e.tensor_tensor(trow, trow, vrow, ALU.subtract),
                 incs=[(sFinV, 1)], tag=f"{tag}.var")
            P.op("scalar", lambda e: e.activation(vrow, trow, AF.Ln, bias=cvec_s[0:1, 0:1]),
                 waits=[(sFinV, sFinV.n), (sBc, prev_bc)], tag=f"{tag}.lnf")
            P.op("scalar", lambda e: e.activation(vrow, vrow, AF.Exp, scale=cvec_s[0:1, 1:2]),
                 incs=[(sFinA, 1)], tag=f"{tag}.expf")
            P.op("tensor", lambda e: e.matmul(R_ps, onesrow_s[:], vrow, start=True, stop=True),
                 waits=[(sFinA, sFinA.n)], incs=[(sBc, 1)], tag=f"{tag}.bcR")
            P.op("vector", lambda e: e.tensor_tensor(mrow, mrow, vrow, ALU.mult),
                 waits=[(sFinA, sFinA.n)], incs=[(sFinV, 1)], tag=f"{tag}.brow")
            P.op("tensor", lambda e: e.matmul(B_ps, negrow_s[:], mrow, start=True, stop=True),
                 waits=[(sFinV, sFinV.n)], incs=[(sBc, 1)], tag=f"{tag}.bcB")

        wov = wo.rearrange("(kc p) o -> p kc o", p=128)
        wo_user_done = {0: None, 1: None}
        x1_pend = []
        x1_tok = {}

        def flush_x1():
            qg_, dc_, apl = x1_pend.pop(0)
            x1_tok[(qg_, dc_)] = sy.dma(lambda e, dc=dc_, qg0_=qg_ * QG: e.dma_start(
                x1d[dc, :, qg0_:qg0_ + QG], z1_t(dc)),
                waits=[(sApl, apl)], tag=f"x1w.{qg_}.{dc_}")
        R1 = PSA[:, 1024:1536]
        B1 = PSB[:, 1024:1536]
        statS = PSB[0:1, 0:512]
        statS2 = PSB[0:1, 512:1024]

        for qg in range(NQG):
            qg0 = qg * QG
            for qt_l in range(4):
                for h in range(NH):
                    emit_front(qg * 4 + qt_l, h, qt_l * 128)
                    if len(pend) >= 2:
                        emit_midback()
            while pend:
                emit_midback()
            all_oev = sOev.n
            tac_tok = []
            if taps and qg == 0:
                tac_tok = sy.dma(lambda e: e.dma_start(tac, attn_col),
                                 waits=[(sOev, all_oev)], tag="tap.ac")
            z_done, zb_done, st_done = {}, {}, {}
            stat_pend = []

            def emit_stats(z_t, zsq_t, which):
                dc, zpar, zc, sqc = stat_pend.pop(0)
                P.op("tensor", lambda e, dc=dc, zpar=zpar: e.matmul(
                    statS, onescolf[:], z_t(dc),
                    start=(dc == 0), stop=(dc == NDC - 1)),
                    waits=[(sZ, zc)], tag=f"st1.{which}.{dc}")
                P.op("tensor", lambda e, dc=dc, zpar=zpar: e.matmul(
                    statS2, onescolf[:], zsq_t(zpar),
                    start=(dc == 0), stop=(dc == NDC - 1)),
                    waits=[(sSq, sqc)],
                    incs=([(sSt, 1)] if dc == NDC - 1 else ()), tag=f"st2.{which}.{dc}")
                st_done[dc] = sSt.n

            prev_bc = sBc.n
            for dcg in range(4):
                wpar = dcg % 2
                wobuf = W0 if wpar == 0 else W1b
                ldw = list(tac_tok) if dcg == 0 else []
                if wo_user_done[wpar] is not None:
                    ldw.append((sWo, wo_user_done[wpar]))
                elif qg == 0 and dcg < 2:
                    ldw.append((sPre, sPre.n))
                wo_ld = gp.dma(lambda e, dcg=dcg, wobuf=wobuf: e.dma_start(
                    wobuf[:], wov[:, :, dcg * 512:(dcg + 1) * 512]),
                    waits=ldw, tag=f"ld.wo.{qg}.{dcg}")
                for _ in range(4):
                    if x1_pend:
                        flush_x1()
                for dc_l in range(4):
                    dc = dcg * 4 + dc_l
                    zpar = dc % 2
                    woPS = PSA[:, zpar * 512:(zpar + 1) * 512]
                    xw = [] if dc < 2 else [(sZ, z_done[dc - 2])]
                    xres_tok = sy.dma(lambda e, dc=dc, zpar=zpar, qg0=qg0: e.dma_start(
                        xres_t(zpar), xv[:, dc, qg0:qg0 + QG]),
                        waits=xw, tag=f"xres.{qg}.{dc}")
                    w0 = wo_ld + [(sOev, all_oev)]
                    if dc >= 2:
                        w0.append((sZ, z_done[dc - 2]))
                    for kc in range(NDC):
                        P.op("tensor", lambda e, kc=kc, dc_l=dc_l, wobuf=wobuf, woPS=woPS: e.matmul(
                            woPS, wobuf[:, kc * 512 + dc_l * 128: kc * 512 + (dc_l + 1) * 128],
                            attn_col[:, kc * 512:(kc + 1) * 512],
                            start=(kc == 0), stop=(kc == NDC - 1)),
                            waits=(w0 if kc == 0 else ()),
                            incs=([(sWo, 1)] if kc == NDC - 1 else ()), tag=f"wo.{qg}.{dc}")
                    zaw = [(sWo, sWo.n)] + xres_tok
                    if (qg - 1, dc) in x1_tok:
                        zaw += x1_tok[(qg - 1, dc)]
                    P.op("vector", lambda e, dc=dc, zpar=zpar, woPS=woPS: e.tensor_tensor(
                        z1_t(dc), woPS, xres_t(zpar), ALU.add),
                        waits=zaw, incs=[(sZ, 1)], tag=f"z1.{qg}.{dc}")
                    z_done[dc] = sZ.n
                    zw = [(sZ, sZ.n)] if dc < 2 else [(sZ, sZ.n), (sSt, st_done.get(dc - 2, 0))]
                    P.op("vector", lambda e, dc=dc, zpar=zpar: e.tensor_tensor(
                        zsq1_t(zpar), z1_t(dc), z1_t(dc), ALU.mult),
                        waits=zw, incs=[(sSq, 1)], tag=f"zsq.{qg}.{dc}")
                    stat_pend.append((dc, zpar, z_done[dc], sSq.n))
                    if len(stat_pend) >= 2:
                        emit_stats(z1_t, zsq1_t, 1)
                wo_user_done[wpar] = sWo.n
            while stat_pend:
                emit_stats(z1_t, zsq1_t, 1)
            emit_ln_finalize(statS, statS2, R1, B1, prev_bc, f"ln1.{qg}")
            for dc in range(NDC):
                aw = [(sBc, sBc.n)]
                P.op("vector", lambda e, dc=dc: e.tensor_tensor(
                    z1_t(dc), z1_t(dc), R1, ALU.mult),
                    waits=aw, tag=f"ap1.{qg}.{dc}")
                P.op("vector", lambda e, dc=dc: e.tensor_tensor(
                    z1_t(dc), z1_t(dc), B1, ALU.add),
                    incs=[(sApl, 1)], tag=f"ap2.{qg}.{dc}")
                x1_pend.append((qg, dc, sApl.n))
            apl_gate[0] = sApl.n
            if qg == NQG - 1:
                while x1_pend:
                    flush_x1()

        x1w_all = sy.snap()
        qk_all = sQK.n
        pv_all = sPV.n
        wo_all = sWo.n

        x1v = x1d.rearrange("dc p t -> p dc t")
        for i in range(4):
            gp.dma(lambda e, i=i: e.dma_start(
                B[:, i * 8192:(i + 1) * 8192], x1v[:, 4 * i:4 * (i + 1), :]),
                waits=x1w_all + [(sQK, qk_all)], tag=f"ld.x1bf{i}")

        P.barrier(sBar, 2)
        h128_tok = gp.dma(lambda e: e.dma_start(h128b, h128d), tag="ld.h128")


        sg_t = lambda par: SCR[:, par * 512:(par + 1) * 512]
        hbf_t = lambda par: SCRb[:, 2048 + par * 512: 2048 + (par + 1) * 512]
        h2st_t = lambda par: SCRb[:, 3072 + par * 512: 3072 + (par + 1) * 512]

        def up_ps(par, k):
            return (PSA if par == 0 else PSB)[:, k * 512:(k + 1) * 512]

        up_idx = 0
        up_mm_done = {}
        sil_done, gat_done, h2e_done, h2w_done = {}, {}, {}, {}
        pend_up = []

        def up_tail(i):
            c, ts, par = pend_up.pop(0)
            hw = [(sGat, gat_done[i])]
            if i == 0:
                hw += h128_tok
            if i >= 2:
                hw.append((sH2e, h2e_done[i - 2]))
            P.op("tensor", lambda e, par=par: e.matmul(
                up_ps(par, 2), h128b, hbf_t(par), start=True, stop=True),
                waits=hw, incs=[(sUp, 1)], tag=f"h128.{i}")
            ew = [(sUp, sUp.n)]
            if i >= 2:
                ew += h2w_done[i - 2]
            P.op("scalar", lambda e, par=par: e.activation(
                h2st_t(par), up_ps(par, 2), AF.Copy, scale=cvec_s[:, 2:3]),
                waits=ew, incs=[(sH2e, 1)], tag=f"h2e.{i}")
            h2e_done[i] = sH2e.n
            h2w_done[i] = sy.dma(lambda e, c=c, ts=ts, par=par: e.dma_start(h2d[c, ts], h2st_t(par)),
                                 waits=[(sH2e, sH2e.n)], tag=f"h2w.{i}")

        for cg in range(16):
            wpar = cg % 2
            w1buf = C if wpar == 0 else W0
            w3buf = Dv if wpar == 0 else W1b
            ldw = []
            if cg >= 2:
                ldw.append((sUp, up_mm_done[(cg - 2) * 16 + 15]))
            else:
                ldw += [(sQK, qk_all), (sPV, pv_all), (sWo, wo_all)]
            gp.dma(lambda e, cg=cg, w1buf=w1buf: e.dma_start(w1buf[:], w1t[cg]),
                waits=ldw, tag=f"ld.w1.{cg}")
            gp.dma(lambda e, cg=cg, w3buf=w3buf: e.dma_start(w3buf[:], w3t[cg]), tag=f"ld.w3.{cg}")
            ld_done = gp.snap()
            for cl in range(4):
                c = cg * 4 + cl
                for ts in range(NTS):
                    i = up_idx
                    par = i % 2
                    w0 = list(ld_done)
                    if i >= 2:
                        w0.append((sSil, sil_done[i - 2]))
                    for dc in range(NDC):
                        P.op("tensor", lambda e, dc=dc, cl=cl, ts=ts, par=par, w1buf=w1buf: e.matmul(
                            up_ps(par, 0), w1buf[:, dc * 512 + cl * 128: dc * 512 + (cl + 1) * 128],
                            B[:, dc * 2048 + ts * 512: dc * 2048 + (ts + 1) * 512],
                            start=(dc == 0), stop=(dc == NDC - 1)),
                            waits=(w0 if dc == 0 else ()),
                            incs=([(sUp, 1)] if dc == NDC - 1 else ()), tag=f"upA.{i}")
                    w0b = [(sGat, gat_done[i - 2])] if i >= 2 else []
                    for dc in range(NDC):
                        P.op("tensor", lambda e, dc=dc, cl=cl, ts=ts, par=par, w3buf=w3buf: e.matmul(
                            up_ps(par, 1), w3buf[:, dc * 512 + cl * 128: dc * 512 + (cl + 1) * 128],
                            B[:, dc * 2048 + ts * 512: dc * 2048 + (ts + 1) * 512],
                            start=(dc == 0), stop=(dc == NDC - 1)),
                            waits=(w0b if dc == 0 else ()),
                            incs=([(sUp, 1)] if dc == NDC - 1 else ()), tag=f"upB.{i}")
                    up_mm_done[i] = sUp.n
                    P.op("scalar", lambda e, par=par: e.activation(sg_t(par), up_ps(par, 0), AF.Silu),
                         waits=[(sUp, sUp.n)], incs=[(sSil, 1)], tag=f"sil.{i}")
                    sil_done[i] = sSil.n
                    P.op("vector", lambda e, par=par: e.tensor_tensor(
                        hbf_t(par), sg_t(par), up_ps(par, 1), ALU.mult),
                        waits=[(sSil, sSil.n)], incs=[(sGat, 1)], tag=f"gat.{i}")
                    gat_done[i] = sGat.n
                    pend_up.append((c, ts, par))
                    if i >= 1:
                        up_tail(i - 1)
                    up_idx += 1
        up_tail(up_idx - 1)
        up_all = sUp.n
        h2w_all = sy.snap()

        h2r = h2d.rearrange("c s (js jl) t -> s js c jl t", js=2)
        h3r4 = h3d.rearrange("s (js cp) jl t -> s js jl cp t", js=2)
        hin_t = lambda sl: W1b[:, sl * 1024:(sl + 1) * 1024]
        h3st_t = lambda sl: W0[:, sl * 1024:(sl + 1) * 1024]
        x1r_t = lambda par: SCR[:, par * 512:(par + 1) * 512]
        zsq2_t = lambda par: SCR[:, 1024 + par * 512: 1024 + (par + 1) * 512]

        def z2_t(dc):
            return CF32[:, dc * 512:(dc + 1) * 512] if dc < 8 else DF32[:, (dc - 8) * 512:(dc - 7) * 512]

        h64_units = [(ts, p2) for ts in range(NTS) for p2 in range(32)]
        h64_idx = 0
        h64_mm_done, h64_ev_done, h64_w_done = {}, {}, {}
        h3w_at_ts = {}
        h64_store_pend = []

        def flush_h64_store():
            i, ts_, p2_, sl_, ev = h64_store_pend.pop(0)
            h64_w_done[i] = gp.dma(lambda e, ts=ts_, p2=p2_, sl=sl_: e.dma_start(
                h3d[ts, :, 2 * p2:2 * p2 + 2, :], h3st_t(sl)),
                waits=[(sH64e, ev)], tag=f"h64w.{i}")

        def emit_h64(n):
            nonlocal h64_idx
            for _ in range(n):
                if h64_idx >= len(h64_units):
                    return
                i = h64_idx
                ts, p2 = h64_units[i]
                sl = i % 8
                lw = list(h2w_all)
                if i >= 8:
                    lw.append((sH64, h64_mm_done[i - 8]))
                ld_tok = gp.dma(lambda e, ts=ts, p2=p2, sl=sl: e.dma_start(
                    hin_t(sl), h2r[ts, :, :, 2 * p2:2 * p2 + 2, :]), waits=lw, tag=f"h64l.{i}")
                mw = list(ld_tok)
                if i >= 1:
                    mw.append((sH64e, h64_ev_done[i - 1]))
                for half in range(2):
                    P.op("tensor", lambda e, sl=sl, half=half: e.matmul(
                        PTO[:, half * 512:(half + 1) * 512], h64b[:],
                        hin_t(sl)[:, half * 512:(half + 1) * 512], start=True, stop=True),
                        waits=(mw if half == 0 else ()),
                        incs=([(sH64, 1)] if half == 1 else ()), tag=f"h64m.{i}")
                h64_mm_done[i] = sH64.n
                ew = [(sH64, sH64.n)]
                if (i - 8) in h64_w_done:
                    ew += h64_w_done[i - 8]
                elif i >= 8:
                    raise RuntimeError("h64 store lag exceeded staging depth")
                P.op("vector", lambda e, sl=sl: e.tensor_copy(h3st_t(sl), PTO[:, 0:1024]),
                     waits=ew, incs=[(sH64e, 1)], tag=f"h64e.{i}")
                h64_ev_done[i] = sH64e.n
                h64_store_pend.append((i, ts, p2, sl, sH64e.n))
                if len(h64_store_pend) > 4:
                    flush_h64_store()
                h64_idx += 1
                if (i + 1) % 32 == 0:
                    while h64_store_pend:
                        flush_h64_store()
                    h3w_at_ts[(i + 1) // 32 - 1] = gp.snap()

        emit_h64(32)

        yw = []
        y_pend = []
        y_tok = {}

        def flush_y():
            ts_, dc_, apl = y_pend.pop(0)
            y_tok[(ts_, dc_)] = sc.dma(lambda e, dc=dc_, ts=ts_: e.dma_start(
                yT[dc * 128:(dc + 1) * 128, ts * 512:(ts + 1) * 512], z2_t(dc)),
                waits=[(sApl, apl)], tag=f"yw.{ts_}.{dc_}")
            yw.append(y_tok[(ts_, dc_)])

        dn_done, dne_done = {}, {}
        st2_done = {}
        dn_mm_at_ts = {}
        gat_all = sGat.n
        for ts in range(NTS):
            lw = list(h3w_at_ts[ts])
            if ts == 0:
                lw += [(sApl, apl_gate[0])] + x1w_all
            else:
                lw.append((sDn, dn_mm_at_ts[ts - 1]))
            Av = A[:].rearrange("p (c t) -> p c t", t=512)
            for i8 in range(4):
                for js in range(2):
                    gp.dma(lambda e, ts=ts, i8=i8, js=js: e.dma_start(
                        Av[js * 64:(js + 1) * 64, i8 * 16:(i8 + 1) * 16, :],
                        h3r4[ts, js, :, i8 * 16:(i8 + 1) * 16, :]),
                        waits=(lw if (i8 == 0 and js == 0) else ()), tag=f"h3l.{ts}.{i8}.{js}")
            h3l_done = gp.snap()
            prev_bc2 = sBc.n
            stat_pend2 = []
            st_done2 = {}

            def emit_stats2(ts=ts):
                dc, zpar, zc, sqc = stat_pend2.pop(0)
                P.op("tensor", lambda e, dc=dc, zpar=zpar: e.matmul(
                    statS, onescolf[:], z2_t(dc), start=(dc == 0), stop=(dc == NDC - 1)),
                    waits=[(sDnE, zc)], tag=f"st1b.{ts}.{dc}")
                P.op("tensor", lambda e, dc=dc, zpar=zpar: e.matmul(
                    statS2, onescolf[:], zsq2_t(zpar), start=(dc == 0), stop=(dc == NDC - 1)),
                    waits=[(sSq, sqc)],
                    incs=([(sSt, 1)] if dc == NDC - 1 else ()), tag=f"st2b.{ts}.{dc}")
                st2_done[(ts, dc)] = sSt.n

            for dg in range(8):
                wpar = dg % 2
                ldw = []
                if ts == 0 and dg < 2:
                    ldw.append((sUp, up_all))
                else:
                    key = (ts, dg - 2) if dg >= 2 else (ts - 1, dg + 6)
                    ldw.append((sDn, dn_done[key]))
                gp.dma(lambda e, dg=dg, wpar=wpar: e.dma_start(
                    B[:, wpar * 16384:(wpar + 1) * 16384], w2t[dg]),
                    waits=ldw, tag=f"w2l.{ts}.{dg}")
                w2_ld = gp.snap()
                for _ in range(2):
                    if y_pend:
                        flush_y()
                for dc2 in range(2):
                    dc = dg * 2 + dc2
                    dps = PSA[:, dc2 * 512:(dc2 + 1) * 512]
                    zpar = dc % 2
                    xw = []
                    if ts > 0 or dc >= 1:
                        xw.append((sDnE, dne_done[(ts, dc - 1) if dc >= 1 else (ts - 1, 15)]))
                    else:
                        xw.append((sGat, gat_all))
                    x1r_tok = sy.dma(lambda e, dc=dc, ts=ts, zpar=zpar: e.dma_start(
                        x1r_t(zpar), x1d[dc, :, ts * 512:(ts + 1) * 512]),
                        waits=xw + (x1w_all if ts == 0 and dc == 0 else []), tag=f"x1r.{ts}.{dc}")
                    mw = w2_ld + h3l_done
                    if ts > 0 or dc >= 2:
                        mw.append((sDnE, dne_done[(ts, dc - 2) if dc >= 2 else (ts - 1, dc + 14)]))
                    for cc in range(64):
                        P.op("tensor", lambda e, cc=cc, dc2=dc2, dps=dps, wpar=wpar: e.matmul(
                            dps, B[:, wpar * 16384 + cc * 256 + dc2 * 128: wpar * 16384 + cc * 256 + (dc2 + 1) * 128],
                            A[:, cc * 512:(cc + 1) * 512],
                            start=(cc == 0), stop=(cc == 63)),
                            waits=(mw if cc == 0 else ()),
                            incs=([(sDn, 1)] if cc == 63 else ()), tag=f"dn.{ts}.{dc}")
                    z2w = [(sDn, sDn.n)] + x1r_tok
                    if (ts - 1, dc) in y_tok:
                        z2w += y_tok[(ts - 1, dc)]
                    P.op("vector", lambda e, dc=dc, dps=dps, zpar=zpar: e.tensor_tensor(
                        z2_t(dc), dps, x1r_t(zpar), ALU.add),
                        waits=z2w, incs=[(sDnE, 1)], tag=f"z2.{ts}.{dc}")
                    dne_done[(ts, dc)] = sDnE.n
                    zw = [(sDnE, sDnE.n)]
                    if ts > 0 or dc >= 2:
                        zw.append((sSt, st2_done.get((ts, dc - 2) if dc >= 2 else (ts - 1, dc + 14), 0)))
                    P.op("vector", lambda e, dc=dc, zpar=zpar: e.tensor_tensor(
                        zsq2_t(zpar), z2_t(dc), z2_t(dc), ALU.mult),
                        waits=zw, incs=[(sSq, 1)], tag=f"zsq2.{ts}.{dc}")
                    stat_pend2.append((dc, zpar, dne_done[(ts, dc)], sSq.n))
                    if len(stat_pend2) >= 2:
                        emit_stats2()
                dn_done[(ts, dg)] = sDn.n
                emit_h64(4)
            while stat_pend2:
                emit_stats2()
            dn_mm_at_ts[ts] = sDn.n
            emit_ln_finalize(statS, statS2, R1, B1, prev_bc2, f"ln2.{ts}")
            for dc in range(NDC):
                aw = [(sBc, sBc.n)]
                P.op("vector", lambda e, dc=dc: e.tensor_tensor(
                    z2_t(dc), z2_t(dc), R1, ALU.mult),
                    waits=aw, tag=f"apy1.{ts}.{dc}")
                P.op("vector", lambda e, dc=dc: e.tensor_tensor(
                    z2_t(dc), z2_t(dc), B1, ALU.add),
                    incs=[(sApl, 1)], tag=f"apy2.{ts}.{dc}")
                y_pend.append((ts, dc, sApl.n))
            if ts == NTS - 1:
                while y_pend:
                    flush_y()

        P.op("sync", None, waits=sy.snap(), tag="final.sy")
        P.op("gpsimd", None, waits=gp.snap(), tag="final.gp")
        P.op("scalar", None, waits=sc.snap(), tag="final.sc")

        P.check_deadlock()
        nops = {e: len(P.ops[e]) for e in ENGINES}

        with nc.Block() as block:
            @block.tensor
            def _(eng):
                P.replay("tensor", eng)

            @block.vector
            def _(eng):
                P.replay("vector", eng)

            @block.scalar
            def _(eng):
                P.replay("scalar", eng)

            @block.sync
            def _(eng):
                P.replay("sync", eng)

            @block.gpsimd
            def _(eng):
                P.replay("gpsimd", eng)

    return nc, nops


_CACHE = {}


def _get_nc(taps=False):
    key = bool(taps)
    if key not in _CACHE:
        _CACHE[key] = build_nc(taps=taps)
    return _CACHE[key]


def make_in_maps(inputs):
    x = np.asarray(inputs["L0_attn_hidden_states"], dtype=np.float32)
    consts = host_constants()
    shared = {
        "wq": np.ascontiguousarray(inputs["L0_attn_w_q"], dtype=np.float32),
        "wk": np.ascontiguousarray(inputs["L0_attn_w_k"], dtype=np.float32),
        "wv": np.ascontiguousarray(inputs["L0_attn_w_v"], dtype=np.float32),
        "wo": np.ascontiguousarray(inputs["L0_attn_w_o"], dtype=np.float32),
        "w1t": np.ascontiguousarray(
            np.asarray(inputs["L0_ffn_W1"], dtype=np.float32).T.reshape(16, 128, 16, 512)
            .transpose(3, 1, 0, 2).reshape(16, 128, 16, 512)) if False else
            np.ascontiguousarray(np.asarray(inputs["L0_ffn_W1"], dtype=np.float32).T
                                 .reshape(16, 128, 16, 512).transpose(2, 1, 0, 3)),
        "w3t": np.ascontiguousarray(np.asarray(inputs["L0_ffn_W3"], dtype=np.float32).T
                                    .reshape(16, 128, 16, 512).transpose(2, 1, 0, 3)),
        "w2t": np.ascontiguousarray(np.asarray(inputs["L0_ffn_W2"], dtype=np.float32).T
                                    .reshape(64, 128, 8, 256).transpose(2, 1, 0, 3)),
    }
    shared.update(consts)
    in_maps = []
    for i in range(8):
        m = dict(shared)
        m["xT"] = np.ascontiguousarray(x[i].T)
        in_maps.append(m)
    return in_maps


def kernel(**inputs):
    nc, _ = _get_nc(taps=False)
    in_maps = make_in_maps(inputs)
    res = run_bass_kernel_spmd(nc, in_maps, core_ids=list(range(8)))
    out = np.empty((8, S, D), dtype=np.float32)
    for i in range(8):
        out[i] = res.results[i]["yT"].T
    return out


# version 18
# speedup vs baseline: 1.4578x; 1.2420x over previous
import numpy as np
import ml_dtypes
import concourse.bass as bass
import concourse.mybir as mybir
from concourse.bass_utils import run_bass_kernel_spmd

F32 = mybir.dt.float32
BF16 = mybir.dt.bfloat16
AF = mybir.ActivationFunctionType
ALU = mybir.AluOpType
AX = mybir.AxisListType

S, D, NH, NKV, HD, WIN, HF = 2048, 2048, 16, 4, 128, 1024, 8192
NDC = D // 128
NTS, TS = 4, 512
NQT = S // 128
QG, NQG = 512, 4
EPS = 1e-5
FW_SC = float(1.0 / np.sqrt(HF))
NEG = -1.0e30
ENGINES = ("tensor", "vector", "scalar", "sync", "gpsimd")


def _hadamard(n):
    h = np.array([[1.0]], dtype=np.float32)
    while h.shape[0] < n:
        h = np.block([[h, h], [h, -h]])
    return h.astype(np.float32)


def host_constants():
    t = np.arange(S, dtype=np.float32)
    inv = 1.0 / (10000.0 ** (np.arange(0, HD, 2, dtype=np.float32) / HD))
    fr = np.outer(t, inv)
    emb = np.concatenate([fr, fr], axis=-1)
    cos = np.ascontiguousarray(np.cos(emb).T)
    sin = np.ascontiguousarray(np.sin(emb).T)
    sgn = np.ones((HD, 1), dtype=np.float32)
    sgn[:64] = -1.0
    sc = np.float32(1.0 / np.sqrt(HD))
    qq = np.arange(512)[None, :]
    kk = np.arange(128)[:, None]
    m8 = np.concatenate(
        [(qq >= kk + d * 128).astype(np.float32) for d in range(4)]
        + [(qq <= kk + d * 128).astype(np.float32) for d in range(4)], axis=1)
    return {
        "cosq": (cos * sc).astype(np.float32),
        "sinq": (sin * sgn * sc).astype(np.float32),
        "cosk": cos.astype(np.float32),
        "sink": (sin * sgn).astype(np.float32),
        "h128": _hadamard(128),
        "h64x2": np.kron(np.eye(2, dtype=np.float32), _hadamard(64)),
        "onescol": np.ones((128, 1), dtype=np.float32),
        "onesrow": np.ones((1, 128), dtype=np.float32),
        "negrow": -np.ones((1, 128), dtype=np.float32),
        "cvec": np.tile(np.array([[EPS, -0.5, FW_SC, 1.0 / D]], dtype=np.float32), (128, 1)),
        "mask8": np.ascontiguousarray(m8),
    }


class Sem:
    def __init__(self, handle, name):
        self.h, self.name, self.n = handle, name, 0


class Prog:
    def __init__(self):
        self.ops = {e: [] for e in ENGINES}
        self.serial_sem = None
        self.serial_dma = False

    def op(self, eng, fn, waits=(), incs=(), tag=""):
        waits = list(waits)
        incs = list(incs)
        serial_followup = None
        if self.serial_sem is not None and fn is not None:
            so = self.serial_sem
            waits.append((so, so.n))
            if not incs:
                incs.append((so, 1))
            else:
                serial_followup = (eng, so)
        self.serial_dma = False
        rec = []
        for sem, amt in incs:
            sem.n += amt
            rec.append((sem, amt))
        self.ops[eng].append((fn, tuple(waits), tuple(rec), tag))
        if serial_followup is not None:
            eng2, so = serial_followup
            s0, _ = rec[0]
            self.ops[eng2].append((lambda e: e.nop(), ((s0, s0.n),), ((so, 1),), tag + ".serord"))
            so.n += 1

    def barrier(self, sbar, k):
        for e in ENGINES:
            self.op(e, lambda eng: eng.nop(), incs=[(sbar, 1)], tag=f"bar{k}.inc")
        for e in ENGINES:
            self.op(e, None, waits=[(sbar, 5 * k)], tag=f"bar{k}.wait")

    def check_deadlock(self):
        idx = {e: 0 for e in ENGINES}
        vals = {}
        progress = True
        while progress:
            progress = False
            for e in ENGINES:
                while idx[e] < len(self.ops[e]):
                    fn, waits, incs, tag = self.ops[e][idx[e]]
                    if any(vals.get(s, 0) < v for s, v in waits):
                        break
                    for s, amt in incs:
                        vals[s] = vals.get(s, 0) + amt
                    idx[e] += 1
                    progress = True
        stuck = {e: i for e, i in idx.items() if i < len(self.ops[e])}
        if stuck:
            lines = []
            for e, i in stuck.items():
                fn, waits, incs, tag = self.ops[e][i]
                wd = [(s.name, v, vals.get(s, 0)) for s, v in waits]
                lines.append(f"  {e}[{i}/{len(self.ops[e])}] tag={tag} waits={wd}")
            raise RuntimeError("DEADLOCK in emitted program:\n" + "\n".join(lines))

    def replay(self, name, eng):
        for fn, waits, incs, tag in self.ops[name]:
            for s, v in waits:
                eng.wait_ge(s.h, v)
            if fn is None:
                continue
            inst = fn(eng)
            for s, amt in incs:
                inst = inst.then_inc(s.h, amt)


class DmaPool:

    def __init__(self, P, eng, mksem, name, K=8):
        self.P, self.eng, self.K = P, eng, K
        self.sems = [mksem(f"{name}{i}") for i in range(K)]
        self.uses = [0] * K
        self.i = 0

    def dma(self, fn, waits=(), tag=""):
        j = self.i % self.K
        s = self.sems[j]
        w = list(waits)
        if self.uses[j]:
            w.append((s, 16 * self.uses[j]))
        self.P.serial_dma = True
        self.P.op(self.eng, fn, waits=w, incs=[(s, 16)], tag=tag)
        self.uses[j] += 1
        self.i += 1
        return [(s, 16 * self.uses[j])]

    def snap(self):
        return [(s, 16 * u) for s, u in zip(self.sems, self.uses) if u]


def build_nc(taps=False, serial=False):
    import contextlib
    nc = bass.Bass("TRN2", target_bir_lowering=False, dynamic_dma_scratch_size=16384)

    def din(name, shape):
        return nc.dram_tensor(name, list(shape), F32, kind="ExternalInput").ap()

    xT = din("xT", (D, S))
    wq, wk, wv, wo = din("wq", (D, D)), din("wk", (D, 512)), din("wv", (D, 512)), din("wo", (D, D))
    w1t = din("w1t", (16, 128, 16, 512))
    w3t = din("w3t", (16, 128, 16, 512))
    w2t = din("w2t", (8, 128, 64, 256))
    cosqd, sinqd = din("cosq", (128, S)), din("sinq", (128, S))
    coskd, sinkd = din("cosk", (128, S)), din("sink", (128, S))
    h128d, h64d = din("h128", (128, 128)), din("h64x2", (128, 128))
    onescold, onesrowd, negrowd = din("onescol", (128, 1)), din("onesrow", (1, 128)), din("negrow", (1, 128))
    mask8d = din("mask8", (128, 4096))
    cvecd = din("cvec", (128, 4))

    yT = nc.dram_tensor("yT", [D, S], F32, kind="ExternalOutput").ap()
    ikind = "ExternalOutput" if taps else "Internal"
    h2d = nc.dram_tensor("h2d", [64, NTS, 128, TS], BF16, kind=ikind).ap()
    h3d = nc.dram_tensor("h3d", [NTS, 128, 64, TS], BF16, kind=ikind).ap()
    x1d = nc.dram_tensor("x1d", [NDC, 128, S], F32, kind=ikind).ap()
    if taps:
        tqfm = nc.dram_tensor("tqfm", [128, NH * S], BF16, kind="ExternalOutput").ap()
        tkfm = nc.dram_tensor("tkfm", [128, NKV * S], BF16, kind="ExternalOutput").ap()
        tvtm = nc.dram_tensor("tvtm", [128, 16 * 512], BF16, kind="ExternalOutput").ap()

    P = Prog()
    stack = contextlib.ExitStack()

    def sbuf(name, shape, dt):
        return stack.enter_context(nc.sbuf_tensor(name, list(shape), dt))

    def psum(name, shape, dt=F32):
        return stack.enter_context(nc.psum_tensor(name, list(shape), dt))

    def sem(name):
        return Sem(stack.enter_context(nc.semaphore(name)), name)

    with stack:
        A = sbuf("A", (128, 32768), BF16)
        B = sbuf("B", (128, 32768), BF16)
        C = sbuf("C", (128, 8192), BF16)
        Dv = sbuf("Dv", (128, 8192), BF16)
        W0 = sbuf("W0", (128, 8192), BF16)
        W1b = sbuf("W1b", (128, 8192), BF16)
        SCR = sbuf("SCR", (128, 3584), F32)

        AF32 = A[:].bitcast(F32)
        CF32 = C[:].bitcast(F32)
        DF32 = Dv[:].bitcast(F32)
        SCRb = SCR[:].bitcast(BF16)

        h64b = sbuf("h64b", (128, 128), BF16)
        onescolb = sbuf("onescolb", (128, 1), BF16)
        onescolf = sbuf("onescolf", (128, 1), F32)
        onesrow_s = sbuf("onesrow_s", (1, 128), F32)
        negrow_s = sbuf("negrow_s", (1, 128), F32)
        cvec_s = sbuf("cvec_s", (128, 4), F32)

        PSA = psum("PSA", (128, 1536), F32)
        PSB = psum("PSB", (128, 1536), F32)
        PTO = psum("PTO", (128, 1024), F32)

        h128b = A[:, 0:128]
        mrow1 = SCR[0:1, 2560:3072]
        vrow1 = SCR[0:1, 3072:3584]
        trow = SCR[0:1, 2048:2560]

        sBar = sem("sBar")
        sPre, sRpV, sRpA = sem("sPre"), sem("sRpV"), sem("sRpA")
        sQK, sExp, sMsk = sem("sQK"), sem("sExp"), sem("sMsk")
        sSum, sRcp, sBc2, sNrm, sPV, sOcp = (sem("sSum"), sem("sRcp"), sem("sBc2"),
                                             sem("sNrm"), sem("sPV"), sem("sOcp"))
        sWo, sZ, sZb, sSq, sSt = sem("sWo"), sem("sZ"), sem("sZb"), sem("sSq"), sem("sSt")
        sFinV, sFinA, sBc, sApl = sem("sFinV"), sem("sFinA"), sem("sBc"), sem("sApl")
        sUp, sSil, sGat, sH2e = sem("sUp"), sem("sSil"), sem("sGat"), sem("sH2e")
        sH64, sH64e, sDn, sDnE = sem("sH64"), sem("sH64e"), sem("sDn"), sem("sDnE")
        gp = DmaPool(P, "gpsimd", sem, "dG", 8)
        sy = DmaPool(P, "sync", sem, "dS", 8)
        sc = DmaPool(P, "scalar", sem, "dC", 8)
        if serial:
            P.serial_sem = sem("sOrd")

        sy.dma(lambda e: e.dma_start(cvec_s[:], cvecd), tag="ld.cvec")
        sy.dma(lambda e: e.dma_start(onesrow_s[:], onesrowd), tag="ld.onesrow")
        consts_sy = sy.dma(lambda e: e.dma_start(negrow_s[:], negrowd), tag="ld.negrow") + sy.snap()
        gp.dma(lambda e: e.dma_start(h64b[:], h64d), tag="ld.h64")
        gp.dma(lambda e: e.dma_start(onescolb[:], onescold), tag="ld.onescol")
        sy.dma(lambda e: e.dma_start(onescolf[:], onescold), tag="ld.onescolf")
        xv = xT.rearrange("(dc p) t -> p dc t", p=128)
        for i in range(4):
            gp.dma(lambda e, i=i: e.dma_start(A[:, i * 8192:(i + 1) * 8192], xv[:, 4 * i:4 * (i + 1), :]),
                   tag=f"ld.x{i}")

        t1f = PTO[:, 0:512]
        swpf = PSA[:, 512:1024]
        t2f = SCR[:, 2048:2560]
        cos_sl = lambda par: SCR[:, par * 1024: par * 1024 + 512]
        sin_sl = lambda par: SCR[:, par * 1024 + 512: par * 1024 + 1024]

        def pp(par):
            return (PSA if par == 0 else PSB)[:, 0:512]

        rope_done = []

        def emit_rope(par, dst, cs, sn, tag):
            mmw = (sPre, sPre.n)
            P.op("vector", lambda e, par=par, cs=cs: e.tensor_tensor(t1f, pp(par), cs, ALU.mult),
                 waits=[mmw], incs=[(sRpV, 1)], tag=f"{tag}.t1")
            swq = [(sRpV, sRpV.n)]
            P.op("scalar", lambda e, par=par: e.activation(
                swpf[0:64, :], (PSA if par == 0 else PSB)[64:128, 0:512], AF.Copy),
                waits=[mmw] + swq, incs=[(sRpA, 1)], tag=f"{tag}.sw0")
            P.op("scalar", lambda e, par=par: e.activation(
                swpf[64:128, :], (PSA if par == 0 else PSB)[0:64, 0:512], AF.Copy),
                incs=[(sRpA, 1)], tag=f"{tag}.sw1")
            P.op("vector", lambda e, sn=sn: e.tensor_tensor(t2f, swpf[:], sn, ALU.mult),
                 waits=[(sRpA, sRpA.n)], incs=[(sRpV, 1)], tag=f"{tag}.t2")
            P.op("vector", lambda e, dst=dst: e.tensor_tensor(dst, t2f, t1f, ALU.add),
                 incs=[(sRpV, 1)], tag=f"{tag}.add")
            rope_done.append(sRpV.n)

        wkv = wk.rearrange("(kc p) o -> p kc o", p=128)
        gp.dma(lambda e: e.dma_start(W0[:], wkv), tag="ld.wk")
        k_ld = gp.snap()
        nunit = 0
        tab_user = {0: 0, 1: 0}
        tab_ready = {0: [], 1: []}

        def load_tab(par, ts, cosd, sind, tag):
            t1 = sy.dma(lambda e, par=par, ts=ts, cosd=cosd: e.dma_start(
                cos_sl(par), cosd[:, ts * 512:(ts + 1) * 512]),
                waits=[(sRpV, tab_user[par])], tag=f"{tag}.cos")
            t2 = sy.dma(lambda e, par=par, ts=ts, sind=sind: e.dma_start(
                sin_sl(par), sind[:, ts * 512:(ts + 1) * 512]), tag=f"{tag}.sin")
            tab_ready[par] = t1 + t2

        for ts in range(NTS):
            tpar = ts % 2
            load_tab(tpar, ts, coskd, sinkd, f"ktab.{ts}")
            for g in range(NKV):
                par = nunit % 2
                w0 = k_ld + tab_ready[tpar]
                if nunit >= 2:
                    w0 = w0 + [(sRpV, rope_done[nunit - 2])]
                for kc in range(NDC):
                    P.op("tensor", lambda e, g=g, ts=ts, kc=kc, par=par: e.matmul(
                        pp(par), W0[:, kc * 512 + g * 128: kc * 512 + (g + 1) * 128],
                        A[:, kc * 2048 + ts * 512: kc * 2048 + (ts + 1) * 512],
                        start=(kc == 0), stop=(kc == NDC - 1)),
                        waits=(w0 if kc == 0 else ()),
                        incs=([(sPre, 1)] if kc == NDC - 1 else ()), tag=f"k.mm.{g}.{ts}")
                emit_rope(par, C[:, g * 2048 + ts * 512: g * 2048 + (ts + 1) * 512],
                          cos_sl(tpar), sin_sl(tpar), f"k.rope.{g}.{ts}")
                tab_user[tpar] = sRpV.n
                nunit += 1

        wvv = wv.rearrange("(kc p) o -> p kc o", p=128)
        gp.dma(lambda e: e.dma_start(W0[:], wvv), waits=[(sPre, sPre.n)], tag="ld.wv")
        v_ld = gp.snap()
        for tc in range(16):
            par = nunit % 2
            w0 = list(v_ld)
            if nunit >= 2:
                w0.append((sRpV, rope_done[nunit - 2]))
            for kc in range(NDC):
                P.op("tensor", lambda e, tc=tc, kc=kc, par=par: e.matmul(
                    pp(par), A[:, kc * 2048 + tc * 128: kc * 2048 + (tc + 1) * 128],
                    W0[:, kc * 512:(kc + 1) * 512],
                    start=(kc == 0), stop=(kc == NDC - 1)),
                    waits=(w0 if kc == 0 else ()),
                    incs=([(sPre, 1)] if kc == NDC - 1 else ()), tag=f"v.mm.{tc}")
            P.op("vector", lambda e, tc=tc, par=par: e.tensor_copy(
                Dv[:, tc * 512:(tc + 1) * 512], pp(par)),
                waits=[(sPre, sPre.n)], incs=[(sRpV, 1)], tag=f"v.ev.{tc}")
            rope_done.append(sRpV.n)
            nunit += 1

        wqv = wq.rearrange("(kc p) o -> p kc o", p=128)
        q_mm_at_eighth = {}
        for ei in range(8):
            hb = ei % 2
            ldw = [] if ei < 2 else [(sPre, q_mm_at_eighth[ei - 2])]
            q_tok = gp.dma(lambda e, ei=ei, hb=hb: e.dma_start(
                W1b[:, hb * 4096:(hb + 1) * 4096], wqv[:, :, ei * 256:(ei + 1) * 256]),
                waits=ldw, tag=f"ld.wq{ei}")
            for ts in range(NTS):
                tpar = ts % 2
                load_tab(tpar, ts, cosqd, sinqd, f"qtab.{ei}.{ts}")
                for qc2 in range(2):
                    h = ei * 2 + qc2
                    par = nunit % 2
                    w0 = v_ld + q_tok + tab_ready[tpar]
                    if nunit >= 2:
                        w0 = w0 + [(sRpV, rope_done[nunit - 2])]
                    for kc in range(NDC):
                        P.op("tensor", lambda e, h=h, ts=ts, kc=kc, par=par, hb=hb, qc2=qc2: e.matmul(
                            pp(par),
                            W1b[:, hb * 4096 + kc * 256 + qc2 * 128: hb * 4096 + kc * 256 + (qc2 + 1) * 128],
                            A[:, kc * 2048 + ts * 512: kc * 2048 + (ts + 1) * 512],
                            start=(kc == 0), stop=(kc == NDC - 1)),
                            waits=(w0 if kc == 0 else ()),
                            incs=([(sPre, 1)] if kc == NDC - 1 else ()), tag=f"q.mm.{h}.{ts}")
                    emit_rope(par, B[:, h * 2048 + ts * 512: h * 2048 + (ts + 1) * 512],
                              cos_sl(tpar), sin_sl(tpar), f"q.rope.{h}.{ts}")
                    tab_user[tpar] = sRpV.n
                    nunit += 1
            q_mm_at_eighth[ei] = sPre.n

        P.barrier(sBar, 1)
        mask_tok = gp.dma(lambda e: e.dma_start(SCRb[:, 0:4096], mask8d), tag="ld.mask8")
        if taps:
            sy.dma(lambda e: e.dma_start(tqfm, B[:]), tag="tap.q")
            sy.dma(lambda e: e.dma_start(tkfm, C[:]), tag="tap.k")
            sy.dma(lambda e: e.dma_start(tvtm, Dv[:]), tag="tap.v")

        attn_col = A[:, 24576:32768]
        pt_slot = lambda par, s: A[:, 20480 + par * 1536 + s * 512: 20480 + par * 1536 + (s + 1) * 512]
        srow_s = AF32[0:1, 11776:12288]
        mask_sl = lambda mid: SCRb[:, mid * 512:(mid + 1) * 512]

        def S_bk(par, gi):
            return (PSA if par == 0 else PSB)[:, (gi % 2) * 512:(gi % 2) * 512 + 512]

        def O_bk(par):
            return (PSA if par == 0 else PSB)[:, 1024:1536]

        sumrow = lambda par: PTO[0:1, par * 512:(par + 1) * 512]
        Rbc = lambda par: PTO[:, par * 512:(par + 1) * 512]
        statS_a = PSA[0:1, 1024:1536]
        statS2_a = PSB[0:1, 1024:1536]

        z1_t = lambda dc: AF32[:, dc * 512:(dc + 1) * 512]
        xres_t = lambda par: AF32[:, 8192 + par * 512: 8192 + (par + 1) * 512]
        zsq1_t = lambda par: AF32[:, 9216 + par * 512: 9216 + (par + 1) * 512]

        def qg_tiles(qg):
            if qg == 0:
                return [(kt, kt) for kt in range(4)]
            if qg == 1:
                return [(kt, None) for kt in range(4)] + [(kt, kt - 4) for kt in range(4, 8)]
            if qg == 2:
                return ([(kt, 4 + kt) for kt in range(4)] + [(kt, None) for kt in range(4, 8)]
                        + [(kt, kt - 8) for kt in range(8, 12)])
            return ([(kt, 4 + kt - 4) for kt in range(4, 8)] + [(kt, None) for kt in range(8, 12)]
                    + [(kt, kt - 12) for kt in range(12, 16)])

        unit = 0
        ptile_cnt = {0: 0, 1: 0}
        ptile_pv = {0: [], 1: []}
        bank_last_exp = {0: {0: 0, 1: 0}, 1: {0: 0, 1: 0}}
        bc_cnt, nrm_cnt, sum_stop, pv_stop, ocp_cnt = {}, {}, {}, {}, {}
        apl_gate = [0]
        fin_gate = [0]
        wo_all_prev = [0]
        z_all_prev = [0]
        first_mask = [True]
        pend_tail = []
        unit_tiles = {}

        def emit_sum_pv(u, t):
            par, h, g, nkt = u % 2, (u % 16), ((u % 16) // 4), len(unit_tiles[u])
            t_, gi, slot, kt, rdy = unit_tiles[u][t]
            sw = [rdy]
            if t == 0:
                if u >= 2:
                    sw.append((sNrm, nrm_cnt[u - 2]))
                if u % 16 in (0, 1) and apl_gate[0]:
                    sw.append((sApl, apl_gate[0]))
            P.op("tensor", lambda e, par=par, slot=slot, st=(t == 0), sp=(t == nkt - 1): e.matmul(
                sumrow(par), onescolb[:], pt_slot(par, slot), start=st, stop=sp),
                waits=sw, incs=[(sSum, 1)], tag=f"sm.{u}.{t}")
            if t == nkt - 1:
                sum_stop[u] = sSum.n
            pw = []
            if t == 0:
                if u >= 2:
                    pw.append((sOcp, ocp_cnt[u - 2]))
                if u % 16 in (0, 1) and fin_gate[0]:
                    pw.append((sFinV, fin_gate[0]))
            P.op("tensor", lambda e, par=par, g=g, kt=kt, slot=slot, st=(t == 0), sp=(t == nkt - 1): e.matmul(
                O_bk(par), Dv[:, kt * 512 + g * 128: kt * 512 + (g + 1) * 128],
                pt_slot(par, slot), start=st, stop=sp),
                waits=pw, incs=[(sPV, 1)], tag=f"pv.{u}.{t}")
            ptile_pv[par].append(sPV.n)
            if t == nkt - 1:
                pv_stop[u] = sPV.n

        def emit_tail():
            u, par, h = pend_tail.pop(0)
            cw = [(sPV, pv_stop[u])]
            if wo_all_prev[0]:
                cw.append((sWo, wo_all_prev[0]))
            P.op("scalar", lambda e, par=par, h=h: e.activation(
                attn_col[:, h * 512:(h + 1) * 512], O_bk(par), AF.Copy),
                waits=cw, incs=[(sOcp, 1)], tag=f"ocp.{u}")
            ocp_cnt[u] = sOcp.n
            rw = [(sSum, sum_stop[u])]
            if u >= 1:
                rw.append((sBc2, bc_cnt[u - 1]))
            P.op("vector", lambda e, par=par: e.reciprocal(srow_s, sumrow(par)),
                 waits=rw, incs=[(sRcp, 1)], tag=f"rcp.{u}")
            bw = [(sRcp, sRcp.n)]
            if u >= 2:
                bw.append((sNrm, nrm_cnt[u - 2]))
            if u % 16 in (0, 1) and apl_gate[0]:
                bw.append((sApl, apl_gate[0]))
            P.op("tensor", lambda e, par=par: e.matmul(
                Rbc(par), onesrow_s[:], srow_s, start=True, stop=True),
                waits=bw, incs=[(sBc2, 1)], tag=f"bc2.{u}")
            bc_cnt[u] = sBc2.n
            nw = [(sBc2, sBc2.n), (sOcp, ocp_cnt[u])]
            P.op("vector", lambda e, par=par, h=h: e.tensor_tensor(
                attn_col[:, h * 512:(h + 1) * 512],
                attn_col[:, h * 512:(h + 1) * 512], Rbc(par), ALU.mult),
                waits=nw, incs=[(sNrm, 1)], tag=f"nrm.{u}")
            nrm_cnt[u] = sNrm.n

        def emit_unit(h, qg):
            nonlocal unit
            u = unit
            par = u % 2
            g = h // 4
            tiles = qg_tiles(qg)
            nkt = len(tiles)
            unit_tiles[u] = []
            for t, (kt, mid) in enumerate(tiles):
                gi = ptile_cnt[par]
                qw = []
                le = bank_last_exp[par][gi % 2]
                if le:
                    qw.append((sExp, le))
                if u % 16 == 0 and t < 2 and z_all_prev[0]:
                    qw.append((sZ, z_all_prev[0]))
                P.op("tensor", lambda e, par=par, gi=gi, g=g, kt=kt, h=h, qg=qg: e.matmul(
                    S_bk(par, gi), C[:, g * 2048 + kt * 128: g * 2048 + (kt + 1) * 128],
                    B[:, h * 2048 + qg * 512: h * 2048 + (qg + 1) * 512],
                    start=True, stop=True),
                    waits=qw, incs=[(sQK, 1)], tag=f"qk.{u}.{t}")
                ew = [(sQK, sQK.n)]
                slot = gi % 3
                if gi >= 3:
                    ew.append((sPV, ptile_pv[par][gi - 3]))
                P.op("scalar", lambda e, par=par, gi=gi, slot=slot: e.activation(
                    pt_slot(par, slot), S_bk(par, gi), AF.Exp),
                    waits=ew, incs=[(sExp, 1)], tag=f"exp.{u}.{t}")
                bank_last_exp[par][gi % 2] = sExp.n
                if mid is not None:
                    mw = [(sExp, sExp.n)]
                    if first_mask[0]:
                        mw = mw + mask_tok
                        first_mask[0] = False
                    P.op("vector", lambda e, par=par, slot=slot, mid=mid: e.tensor_tensor(
                        pt_slot(par, slot), pt_slot(par, slot), mask_sl(mid), ALU.mult),
                        waits=mw, incs=[(sMsk, 1)], tag=f"msk.{u}.{t}")
                    rdy = (sMsk, sMsk.n)
                else:
                    rdy = (sExp, sExp.n)
                unit_tiles[u].append((t, gi, slot, kt, rdy))
                ptile_cnt[par] += 1
                if t >= 2:
                    emit_sum_pv(u, t - 2)
                if t == 1 and pend_tail:
                    emit_tail()
            emit_sum_pv(u, nkt - 2)
            emit_sum_pv(u, nkt - 1)
            pend_tail.append((u, par, h))
            unit += 1

        def emit_ln_finalize(statS, statS2, R_ps, B_ps, prev_bc, tag, mrow=None, vrow=None):
            if mrow is None:
                mrow, vrow = mrow1, vrow1
            P.op("vector", lambda e: e.tensor_scalar(mrow, statS, cvec_s[0:1, 3:4], None, ALU.mult),
                 waits=[(sSt, sSt.n), (sBc, prev_bc)], tag=f"{tag}.mean")
            P.op("vector", lambda e: e.tensor_scalar(trow, statS2, cvec_s[0:1, 3:4], None, ALU.mult),
                 tag=f"{tag}.e2")
            P.op("vector", lambda e: e.tensor_tensor(vrow, mrow, mrow, ALU.mult),
                 tag=f"{tag}.m2")
            P.op("vector", lambda e: e.tensor_tensor(trow, trow, vrow, ALU.subtract),
                 incs=[(sFinV, 1)], tag=f"{tag}.var")
            P.op("scalar", lambda e: e.activation(vrow, trow, AF.Ln, bias=cvec_s[0:1, 0:1]),
                 waits=[(sFinV, sFinV.n), (sBc, prev_bc)], tag=f"{tag}.lnf")
            P.op("scalar", lambda e: e.activation(vrow, vrow, AF.Exp, scale=cvec_s[0:1, 1:2]),
                 incs=[(sFinA, 1)], tag=f"{tag}.expf")
            P.op("tensor", lambda e: e.matmul(R_ps, onesrow_s[:], vrow, start=True, stop=True),
                 waits=[(sFinA, sFinA.n)], incs=[(sBc, 1)], tag=f"{tag}.bcR")
            P.op("vector", lambda e: e.tensor_tensor(mrow, mrow, vrow, ALU.mult),
                 waits=[(sFinA, sFinA.n)], incs=[(sFinV, 1)], tag=f"{tag}.brow")
            P.op("tensor", lambda e: e.matmul(B_ps, negrow_s[:], mrow, start=True, stop=True),
                 waits=[(sFinV, sFinV.n)], incs=[(sBc, 1)], tag=f"{tag}.bcB")

        wov = wo.rearrange("(kc p) o -> p kc o", p=128)
        wo_user_done = {0: None, 1: None}
        x1_pend = []
        x1_tok = {}

        def flush_x1():
            qg_, dc_, apl = x1_pend.pop(0)
            x1_tok[(qg_, dc_)] = sy.dma(lambda e, dc=dc_, qg0_=qg_ * QG: e.dma_start(
                x1d[dc, :, qg0_:qg0_ + QG], z1_t(dc)),
                waits=[(sApl, apl)], tag=f"x1w.{qg_}.{dc_}")
        R1 = PSA[:, 1024:1536]
        B1 = PSB[:, 1024:1536]
        R1a = PTO[:, 0:512]
        B1a = PTO[:, 512:1024]
        statS = PSB[0:1, 0:512]
        statS2 = PSB[0:1, 512:1024]

        for qg in range(NQG):
            qg0 = qg * QG
            for h in range(NH):
                emit_unit(h, qg)
            while pend_tail:
                emit_tail()
            nrm_all = sNrm.n
            z_done, st_done = {}, {}
            stat_pend = []

            def emit_stats(z_t, zsq_t, which):
                dc, zpar, zc, sqc = stat_pend.pop(0)
                P.op("tensor", lambda e, dc=dc, zpar=zpar: e.matmul(
                    statS_a, onescolf[:], z_t(dc),
                    start=(dc == 0), stop=(dc == NDC - 1)),
                    waits=[(sZ, zc)], tag=f"st1.{which}.{dc}")
                P.op("tensor", lambda e, dc=dc, zpar=zpar: e.matmul(
                    statS2_a, onescolf[:], zsq_t(zpar),
                    start=(dc == 0), stop=(dc == NDC - 1)),
                    waits=[(sSq, sqc)],
                    incs=([(sSt, 1)] if dc == NDC - 1 else ()), tag=f"st2.{which}.{dc}")
                st_done[dc] = sSt.n

            prev_bc = sBc.n
            for dcg in range(4):
                wpar = dcg % 2
                wobuf = W0 if wpar == 0 else W1b
                ldw = []
                if wo_user_done[wpar] is not None:
                    ldw.append((sWo, wo_user_done[wpar]))
                elif qg == 0 and dcg < 2:
                    ldw.append((sPre, sPre.n))
                wo_ld = gp.dma(lambda e, dcg=dcg, wobuf=wobuf: e.dma_start(
                    wobuf[:], wov[:, :, dcg * 512:(dcg + 1) * 512]),
                    waits=ldw, tag=f"ld.wo.{qg}.{dcg}")
                for _ in range(4):
                    if x1_pend:
                        flush_x1()
                for dc_l in range(4):
                    dc = dcg * 4 + dc_l
                    zpar = dc % 2
                    woPS = PSA[:, zpar * 512:(zpar + 1) * 512]
                    xw = [] if dc < 2 else [(sZ, z_done[dc - 2])]
                    xres_tok = sy.dma(lambda e, dc=dc, zpar=zpar, qg0=qg0: e.dma_start(
                        xres_t(zpar), xv[:, dc, qg0:qg0 + QG]),
                        waits=xw, tag=f"xres.{qg}.{dc}")
                    w0 = wo_ld + [(sNrm, nrm_all)]
                    if dc >= 2:
                        w0.append((sZ, z_done[dc - 2]))
                    for kc in range(NDC):
                        P.op("tensor", lambda e, kc=kc, dc_l=dc_l, wobuf=wobuf, woPS=woPS: e.matmul(
                            woPS, wobuf[:, kc * 512 + dc_l * 128: kc * 512 + (dc_l + 1) * 128],
                            attn_col[:, kc * 512:(kc + 1) * 512],
                            start=(kc == 0), stop=(kc == NDC - 1)),
                            waits=(w0 if kc == 0 else ()),
                            incs=([(sWo, 1)] if kc == NDC - 1 else ()), tag=f"wo.{qg}.{dc}")
                    zaw = [(sWo, sWo.n)] + xres_tok
                    if (qg - 1, dc) in x1_tok:
                        zaw += x1_tok[(qg - 1, dc)]
                    P.op("vector", lambda e, dc=dc, zpar=zpar, woPS=woPS: e.tensor_tensor(
                        z1_t(dc), woPS, xres_t(zpar), ALU.add),
                        waits=zaw, incs=[(sZ, 1)], tag=f"z1.{qg}.{dc}")
                    z_done[dc] = sZ.n
                    zw = [(sZ, sZ.n)] if dc < 2 else [(sZ, sZ.n), (sSt, st_done.get(dc - 2, 0))]
                    P.op("vector", lambda e, dc=dc, zpar=zpar: e.tensor_tensor(
                        zsq1_t(zpar), z1_t(dc), z1_t(dc), ALU.mult),
                        waits=zw, incs=[(sSq, 1)], tag=f"zsq.{qg}.{dc}")
                    stat_pend.append((dc, zpar, z_done[dc], sSq.n))
                    if len(stat_pend) >= 2:
                        emit_stats(z1_t, zsq1_t, 1)
                wo_user_done[wpar] = sWo.n
            while stat_pend:
                emit_stats(z1_t, zsq1_t, 1)
            emit_ln_finalize(statS_a, statS2_a, R1a, B1a, prev_bc, f"ln1.{qg}")
            for dc in range(NDC):
                aw = [(sBc, sBc.n)]
                P.op("vector", lambda e, dc=dc: e.tensor_tensor(
                    z1_t(dc), z1_t(dc), R1a, ALU.mult),
                    waits=aw, tag=f"ap1.{qg}.{dc}")
                P.op("vector", lambda e, dc=dc: e.tensor_tensor(
                    z1_t(dc), z1_t(dc), B1a, ALU.add),
                    incs=[(sApl, 1)], tag=f"ap2.{qg}.{dc}")
                x1_pend.append((qg, dc, sApl.n))
            apl_gate[0] = sApl.n
            fin_gate[0] = sFinV.n
            wo_all_prev[0] = sWo.n
            z_all_prev[0] = sZ.n
            if qg == NQG - 1:
                while x1_pend:
                    flush_x1()

        x1w_all = sy.snap()
        qk_all = sQK.n
        pv_all = sPV.n
        wo_all = sWo.n

        x1v = x1d.rearrange("dc p t -> p dc t")
        for i in range(4):
            gp.dma(lambda e, i=i: e.dma_start(
                B[:, i * 8192:(i + 1) * 8192], x1v[:, 4 * i:4 * (i + 1), :]),
                waits=x1w_all + [(sQK, qk_all)], tag=f"ld.x1bf{i}")

        P.barrier(sBar, 2)
        h128_tok = gp.dma(lambda e: e.dma_start(h128b, h128d), tag="ld.h128")


        sg_t = lambda par: SCR[:, par * 512:(par + 1) * 512]
        hbf_t = lambda par: SCRb[:, 2048 + par * 512: 2048 + (par + 1) * 512]
        h2st_t = lambda par: SCRb[:, 3072 + par * 512: 3072 + (par + 1) * 512]

        def up_ps(par, k):
            return (PSA if par == 0 else PSB)[:, k * 512:(k + 1) * 512]

        up_idx = 0
        up_mm_done = {}
        sil_done, gat_done, h2e_done, h2w_done = {}, {}, {}, {}
        pend_up = []

        def up_tail(i):
            c, ts, par = pend_up.pop(0)
            hw = [(sGat, gat_done[i])]
            if i == 0:
                hw += h128_tok
            if i >= 2:
                hw.append((sH2e, h2e_done[i - 2]))
            P.op("tensor", lambda e, par=par: e.matmul(
                up_ps(par, 2), h128b, hbf_t(par), start=True, stop=True),
                waits=hw, incs=[(sUp, 1)], tag=f"h128.{i}")
            ew = [(sUp, sUp.n)]
            if i >= 2:
                ew += h2w_done[i - 2]
            P.op("scalar", lambda e, par=par: e.activation(
                h2st_t(par), up_ps(par, 2), AF.Copy, scale=cvec_s[:, 2:3]),
                waits=ew, incs=[(sH2e, 1)], tag=f"h2e.{i}")
            h2e_done[i] = sH2e.n
            h2w_done[i] = sy.dma(lambda e, c=c, ts=ts, par=par: e.dma_start(h2d[c, ts], h2st_t(par)),
                                 waits=[(sH2e, sH2e.n)], tag=f"h2w.{i}")

        for cg in range(16):
            wpar = cg % 2
            w1buf = C if wpar == 0 else W0
            w3buf = Dv if wpar == 0 else W1b
            ldw = []
            if cg >= 2:
                ldw.append((sUp, up_mm_done[(cg - 2) * 16 + 15]))
            else:
                ldw += [(sQK, qk_all), (sPV, pv_all), (sWo, wo_all)]
            gp.dma(lambda e, cg=cg, w1buf=w1buf: e.dma_start(w1buf[:], w1t[cg]),
                waits=ldw, tag=f"ld.w1.{cg}")
            gp.dma(lambda e, cg=cg, w3buf=w3buf: e.dma_start(w3buf[:], w3t[cg]), tag=f"ld.w3.{cg}")
            ld_done = gp.snap()
            for cl in range(4):
                c = cg * 4 + cl
                for ts in range(NTS):
                    i = up_idx
                    par = i % 2
                    w0 = list(ld_done)
                    if i >= 2:
                        w0.append((sSil, sil_done[i - 2]))
                    for dc in range(NDC):
                        P.op("tensor", lambda e, dc=dc, cl=cl, ts=ts, par=par, w1buf=w1buf: e.matmul(
                            up_ps(par, 0), w1buf[:, dc * 512 + cl * 128: dc * 512 + (cl + 1) * 128],
                            B[:, dc * 2048 + ts * 512: dc * 2048 + (ts + 1) * 512],
                            start=(dc == 0), stop=(dc == NDC - 1)),
                            waits=(w0 if dc == 0 else ()),
                            incs=([(sUp, 1)] if dc == NDC - 1 else ()), tag=f"upA.{i}")
                    w0b = [(sGat, gat_done[i - 2])] if i >= 2 else []
                    for dc in range(NDC):
                        P.op("tensor", lambda e, dc=dc, cl=cl, ts=ts, par=par, w3buf=w3buf: e.matmul(
                            up_ps(par, 1), w3buf[:, dc * 512 + cl * 128: dc * 512 + (cl + 1) * 128],
                            B[:, dc * 2048 + ts * 512: dc * 2048 + (ts + 1) * 512],
                            start=(dc == 0), stop=(dc == NDC - 1)),
                            waits=(w0b if dc == 0 else ()),
                            incs=([(sUp, 1)] if dc == NDC - 1 else ()), tag=f"upB.{i}")
                    up_mm_done[i] = sUp.n
                    P.op("scalar", lambda e, par=par: e.activation(sg_t(par), up_ps(par, 0), AF.Silu),
                         waits=[(sUp, sUp.n)], incs=[(sSil, 1)], tag=f"sil.{i}")
                    sil_done[i] = sSil.n
                    P.op("vector", lambda e, par=par: e.tensor_tensor(
                        hbf_t(par), sg_t(par), up_ps(par, 1), ALU.mult),
                        waits=[(sSil, sSil.n)], incs=[(sGat, 1)], tag=f"gat.{i}")
                    gat_done[i] = sGat.n
                    pend_up.append((c, ts, par))
                    if i >= 1:
                        up_tail(i - 1)
                    up_idx += 1
        up_tail(up_idx - 1)
        up_all = sUp.n
        h2w_all = sy.snap()

        h2r = h2d.rearrange("c s (js jl) t -> s js c jl t", js=2)
        h3r4 = h3d.rearrange("s (js cp) jl t -> s js jl cp t", js=2)
        hin_t = lambda sl: W1b[:, sl * 1024:(sl + 1) * 1024]
        h3st_t = lambda sl: W0[:, sl * 1024:(sl + 1) * 1024]
        x1r_t = lambda par: SCR[:, par * 512:(par + 1) * 512]
        zsq2_t = lambda par: SCR[:, 1024 + par * 512: 1024 + (par + 1) * 512]

        def z2_t(dc):
            return CF32[:, dc * 512:(dc + 1) * 512] if dc < 8 else DF32[:, (dc - 8) * 512:(dc - 7) * 512]

        h64_units = [(ts, p2) for ts in range(NTS) for p2 in range(32)]
        h64_idx = 0
        h64_mm_done, h64_ev_done, h64_w_done = {}, {}, {}
        h3w_at_ts = {}
        h64_store_pend = []

        def flush_h64_store():
            i, ts_, p2_, sl_, ev = h64_store_pend.pop(0)
            h64_w_done[i] = gp.dma(lambda e, ts=ts_, p2=p2_, sl=sl_: e.dma_start(
                h3d[ts, :, 2 * p2:2 * p2 + 2, :], h3st_t(sl)),
                waits=[(sH64e, ev)], tag=f"h64w.{i}")

        def emit_h64(n):
            nonlocal h64_idx
            for _ in range(n):
                if h64_idx >= len(h64_units):
                    return
                i = h64_idx
                ts, p2 = h64_units[i]
                sl = i % 8
                lw = list(h2w_all)
                if i >= 8:
                    lw.append((sH64, h64_mm_done[i - 8]))
                ld_tok = gp.dma(lambda e, ts=ts, p2=p2, sl=sl: e.dma_start(
                    hin_t(sl), h2r[ts, :, :, 2 * p2:2 * p2 + 2, :]), waits=lw, tag=f"h64l.{i}")
                mw = list(ld_tok)
                if i >= 1:
                    mw.append((sH64e, h64_ev_done[i - 1]))
                for half in range(2):
                    P.op("tensor", lambda e, sl=sl, half=half: e.matmul(
                        PTO[:, half * 512:(half + 1) * 512], h64b[:],
                        hin_t(sl)[:, half * 512:(half + 1) * 512], start=True, stop=True),
                        waits=(mw if half == 0 else ()),
                        incs=([(sH64, 1)] if half == 1 else ()), tag=f"h64m.{i}")
                h64_mm_done[i] = sH64.n
                ew = [(sH64, sH64.n)]
                if (i - 8) in h64_w_done:
                    ew += h64_w_done[i - 8]
                elif i >= 8:
                    raise RuntimeError("h64 store lag exceeded staging depth")
                P.op("vector", lambda e, sl=sl: e.tensor_copy(h3st_t(sl), PTO[:, 0:1024]),
                     waits=ew, incs=[(sH64e, 1)], tag=f"h64e.{i}")
                h64_ev_done[i] = sH64e.n
                h64_store_pend.append((i, ts, p2, sl, sH64e.n))
                if len(h64_store_pend) > 4:
                    flush_h64_store()
                h64_idx += 1
                if (i + 1) % 32 == 0:
                    while h64_store_pend:
                        flush_h64_store()
                    h3w_at_ts[(i + 1) // 32 - 1] = gp.snap()

        emit_h64(32)

        yw = []
        y_pend = []
        y_tok = {}

        def flush_y():
            ts_, dc_, apl = y_pend.pop(0)
            y_tok[(ts_, dc_)] = sc.dma(lambda e, dc=dc_, ts=ts_: e.dma_start(
                yT[dc * 128:(dc + 1) * 128, ts * 512:(ts + 1) * 512], z2_t(dc)),
                waits=[(sApl, apl)], tag=f"yw.{ts_}.{dc_}")
            yw.append(y_tok[(ts_, dc_)])

        dn_done, dne_done = {}, {}
        st2_done = {}
        dn_mm_at_ts = {}
        gat_all = sGat.n
        for ts in range(NTS):
            lw = list(h3w_at_ts[ts])
            if ts == 0:
                lw += [(sApl, apl_gate[0])] + x1w_all
            else:
                lw.append((sDn, dn_mm_at_ts[ts - 1]))
            Av = A[:].rearrange("p (c t) -> p c t", t=512)
            for i8 in range(4):
                for js in range(2):
                    gp.dma(lambda e, ts=ts, i8=i8, js=js: e.dma_start(
                        Av[js * 64:(js + 1) * 64, i8 * 16:(i8 + 1) * 16, :],
                        h3r4[ts, js, :, i8 * 16:(i8 + 1) * 16, :]),
                        waits=(lw if (i8 == 0 and js == 0) else ()), tag=f"h3l.{ts}.{i8}.{js}")
            h3l_done = gp.snap()
            prev_bc2 = sBc.n
            stat_pend2 = []
            st_done2 = {}

            def emit_stats2(ts=ts):
                dc, zpar, zc, sqc = stat_pend2.pop(0)
                P.op("tensor", lambda e, dc=dc, zpar=zpar: e.matmul(
                    statS, onescolf[:], z2_t(dc), start=(dc == 0), stop=(dc == NDC - 1)),
                    waits=[(sDnE, zc)], tag=f"st1b.{ts}.{dc}")
                P.op("tensor", lambda e, dc=dc, zpar=zpar: e.matmul(
                    statS2, onescolf[:], zsq2_t(zpar), start=(dc == 0), stop=(dc == NDC - 1)),
                    waits=[(sSq, sqc)],
                    incs=([(sSt, 1)] if dc == NDC - 1 else ()), tag=f"st2b.{ts}.{dc}")
                st2_done[(ts, dc)] = sSt.n

            for dg in range(8):
                wpar = dg % 2
                ldw = []
                if ts == 0 and dg < 2:
                    ldw.append((sUp, up_all))
                else:
                    key = (ts, dg - 2) if dg >= 2 else (ts - 1, dg + 6)
                    ldw.append((sDn, dn_done[key]))
                gp.dma(lambda e, dg=dg, wpar=wpar: e.dma_start(
                    B[:, wpar * 16384:(wpar + 1) * 16384], w2t[dg]),
                    waits=ldw, tag=f"w2l.{ts}.{dg}")
                w2_ld = gp.snap()
                for _ in range(2):
                    if y_pend:
                        flush_y()
                for dc2 in range(2):
                    dc = dg * 2 + dc2
                    dps = PSA[:, dc2 * 512:(dc2 + 1) * 512]
                    zpar = dc % 2
                    xw = []
                    if ts > 0 or dc >= 1:
                        xw.append((sDnE, dne_done[(ts, dc - 1) if dc >= 1 else (ts - 1, 15)]))
                    else:
                        xw.append((sGat, gat_all))
                    x1r_tok = sy.dma(lambda e, dc=dc, ts=ts, zpar=zpar: e.dma_start(
                        x1r_t(zpar), x1d[dc, :, ts * 512:(ts + 1) * 512]),
                        waits=xw + (x1w_all if ts == 0 and dc == 0 else []), tag=f"x1r.{ts}.{dc}")
                    mw = w2_ld + h3l_done
                    if ts > 0 or dc >= 2:
                        mw.append((sDnE, dne_done[(ts, dc - 2) if dc >= 2 else (ts - 1, dc + 14)]))
                    for cc in range(64):
                        P.op("tensor", lambda e, cc=cc, dc2=dc2, dps=dps, wpar=wpar: e.matmul(
                            dps, B[:, wpar * 16384 + cc * 256 + dc2 * 128: wpar * 16384 + cc * 256 + (dc2 + 1) * 128],
                            A[:, cc * 512:(cc + 1) * 512],
                            start=(cc == 0), stop=(cc == 63)),
                            waits=(mw if cc == 0 else ()),
                            incs=([(sDn, 1)] if cc == 63 else ()), tag=f"dn.{ts}.{dc}")
                    z2w = [(sDn, sDn.n)] + x1r_tok
                    if (ts - 1, dc) in y_tok:
                        z2w += y_tok[(ts - 1, dc)]
                    P.op("vector", lambda e, dc=dc, dps=dps, zpar=zpar: e.tensor_tensor(
                        z2_t(dc), dps, x1r_t(zpar), ALU.add),
                        waits=z2w, incs=[(sDnE, 1)], tag=f"z2.{ts}.{dc}")
                    dne_done[(ts, dc)] = sDnE.n
                    zw = [(sDnE, sDnE.n)]
                    if ts > 0 or dc >= 2:
                        zw.append((sSt, st2_done.get((ts, dc - 2) if dc >= 2 else (ts - 1, dc + 14), 0)))
                    P.op("vector", lambda e, dc=dc, zpar=zpar: e.tensor_tensor(
                        zsq2_t(zpar), z2_t(dc), z2_t(dc), ALU.mult),
                        waits=zw, incs=[(sSq, 1)], tag=f"zsq2.{ts}.{dc}")
                    stat_pend2.append((dc, zpar, dne_done[(ts, dc)], sSq.n))
                    if len(stat_pend2) >= 2:
                        emit_stats2()
                dn_done[(ts, dg)] = sDn.n
                emit_h64(4)
            while stat_pend2:
                emit_stats2()
            dn_mm_at_ts[ts] = sDn.n
            emit_ln_finalize(statS, statS2, R1, B1, prev_bc2, f"ln2.{ts}")
            for dc in range(NDC):
                aw = [(sBc, sBc.n)]
                P.op("vector", lambda e, dc=dc: e.tensor_tensor(
                    z2_t(dc), z2_t(dc), R1, ALU.mult),
                    waits=aw, tag=f"apy1.{ts}.{dc}")
                P.op("vector", lambda e, dc=dc: e.tensor_tensor(
                    z2_t(dc), z2_t(dc), B1, ALU.add),
                    incs=[(sApl, 1)], tag=f"apy2.{ts}.{dc}")
                y_pend.append((ts, dc, sApl.n))
            if ts == NTS - 1:
                while y_pend:
                    flush_y()

        P.op("sync", None, waits=sy.snap(), tag="final.sy")
        P.op("gpsimd", None, waits=gp.snap(), tag="final.gp")
        P.op("scalar", None, waits=sc.snap(), tag="final.sc")

        P.check_deadlock()
        nops = {e: len(P.ops[e]) for e in ENGINES}

        with nc.Block() as block:
            @block.tensor
            def _(eng):
                P.replay("tensor", eng)

            @block.vector
            def _(eng):
                P.replay("vector", eng)

            @block.scalar
            def _(eng):
                P.replay("scalar", eng)

            @block.sync
            def _(eng):
                P.replay("sync", eng)

            @block.gpsimd
            def _(eng):
                P.replay("gpsimd", eng)

    return nc, nops


_CACHE = {}


def _get_nc(taps=False):
    key = bool(taps)
    if key not in _CACHE:
        _CACHE[key] = build_nc(taps=taps)
    return _CACHE[key]


def make_in_maps(inputs):
    x = np.asarray(inputs["L0_attn_hidden_states"], dtype=np.float32)
    consts = host_constants()
    shared = {
        "wq": np.ascontiguousarray(inputs["L0_attn_w_q"], dtype=np.float32),
        "wk": np.ascontiguousarray(inputs["L0_attn_w_k"], dtype=np.float32),
        "wv": np.ascontiguousarray(inputs["L0_attn_w_v"], dtype=np.float32),
        "wo": np.ascontiguousarray(inputs["L0_attn_w_o"], dtype=np.float32),
        "w1t": np.ascontiguousarray(
            np.asarray(inputs["L0_ffn_W1"], dtype=np.float32).T.reshape(16, 128, 16, 512)
            .transpose(3, 1, 0, 2).reshape(16, 128, 16, 512)) if False else
            np.ascontiguousarray(np.asarray(inputs["L0_ffn_W1"], dtype=np.float32).T
                                 .reshape(16, 128, 16, 512).transpose(2, 1, 0, 3)),
        "w3t": np.ascontiguousarray(np.asarray(inputs["L0_ffn_W3"], dtype=np.float32).T
                                    .reshape(16, 128, 16, 512).transpose(2, 1, 0, 3)),
        "w2t": np.ascontiguousarray(np.asarray(inputs["L0_ffn_W2"], dtype=np.float32).T
                                    .reshape(64, 128, 8, 256).transpose(2, 1, 0, 3)),
    }
    shared.update(consts)
    in_maps = []
    for i in range(8):
        m = dict(shared)
        m["xT"] = np.ascontiguousarray(x[i].T)
        in_maps.append(m)
    return in_maps


def kernel(**inputs):
    nc, _ = _get_nc(taps=False)
    in_maps = make_in_maps(inputs)
    res = run_bass_kernel_spmd(nc, in_maps, core_ids=list(range(8)))
    out = np.empty((8, S, D), dtype=np.float32)
    for i in range(8):
        out[i] = res.results[i]["yT"].T
    return out


# version 19
# speedup vs baseline: 1.4863x; 1.0195x over previous
import numpy as np
import ml_dtypes
import concourse.bass as bass
import concourse.mybir as mybir
from concourse.bass_utils import run_bass_kernel_spmd

F32 = mybir.dt.float32
BF16 = mybir.dt.bfloat16
AF = mybir.ActivationFunctionType
ALU = mybir.AluOpType
AX = mybir.AxisListType

S, D, NH, NKV, HD, WIN, HF = 2048, 2048, 16, 4, 128, 1024, 8192
NDC = D // 128
NTS, TS = 4, 512
NQT = S // 128
QG, NQG = 512, 4
EPS = 1e-5
FW_SC = float(1.0 / np.sqrt(HF))
NEG = -1.0e30
ENGINES = ("tensor", "vector", "scalar", "sync", "gpsimd")


def _hadamard(n):
    h = np.array([[1.0]], dtype=np.float32)
    while h.shape[0] < n:
        h = np.block([[h, h], [h, -h]])
    return h.astype(np.float32)


def host_constants():
    t = np.arange(S, dtype=np.float32)
    inv = 1.0 / (10000.0 ** (np.arange(0, HD, 2, dtype=np.float32) / HD))
    fr = np.outer(t, inv)
    emb = np.concatenate([fr, fr], axis=-1)
    cos = np.ascontiguousarray(np.cos(emb).T)
    sin = np.ascontiguousarray(np.sin(emb).T)
    sgn = np.ones((HD, 1), dtype=np.float32)
    sgn[:64] = -1.0
    sc = np.float32(1.0 / np.sqrt(HD))
    qq = np.arange(512)[None, :]
    kk = np.arange(128)[:, None]
    m8 = np.concatenate(
        [(qq >= kk + d * 128).astype(np.float32) for d in range(4)]
        + [(qq <= kk + d * 128).astype(np.float32) for d in range(4)], axis=1)
    return {
        "cosq": (cos * sc).astype(np.float32),
        "sinq": (sin * sgn * sc).astype(np.float32),
        "cosk": cos.astype(np.float32),
        "sink": (sin * sgn).astype(np.float32),
        "h128": _hadamard(128),
        "h64x2": np.kron(np.eye(2, dtype=np.float32), _hadamard(64)),
        "onescol": np.ones((128, 1), dtype=np.float32),
        "onesrow": np.ones((1, 128), dtype=np.float32),
        "negrow": -np.ones((1, 128), dtype=np.float32),
        "cvec": np.tile(np.array([[EPS, -0.5, FW_SC, 1.0 / D]], dtype=np.float32), (128, 1)),
        "mask8": np.ascontiguousarray(m8),
    }


class Sem:
    def __init__(self, handle, name):
        self.h, self.name, self.n = handle, name, 0


class Prog:
    def __init__(self):
        self.ops = {e: [] for e in ENGINES}
        self.serial_sem = None
        self.serial_dma = False

    def op(self, eng, fn, waits=(), incs=(), tag=""):
        waits = list(waits)
        incs = list(incs)
        serial_followup = None
        if self.serial_sem is not None and fn is not None:
            so = self.serial_sem
            waits.append((so, so.n))
            if not incs:
                incs.append((so, 1))
            else:
                serial_followup = (eng, so)
        self.serial_dma = False
        rec = []
        for sem, amt in incs:
            sem.n += amt
            rec.append((sem, amt))
        self.ops[eng].append((fn, tuple(waits), tuple(rec), tag))
        if serial_followup is not None:
            eng2, so = serial_followup
            s0, _ = rec[0]
            self.ops[eng2].append((lambda e: e.nop(), ((s0, s0.n),), ((so, 1),), tag + ".serord"))
            so.n += 1

    def barrier(self, sbar, k):
        for e in ENGINES:
            self.op(e, lambda eng: eng.nop(), incs=[(sbar, 1)], tag=f"bar{k}.inc")
        for e in ENGINES:
            self.op(e, None, waits=[(sbar, 5 * k)], tag=f"bar{k}.wait")

    def check_deadlock(self):
        idx = {e: 0 for e in ENGINES}
        vals = {}
        progress = True
        while progress:
            progress = False
            for e in ENGINES:
                while idx[e] < len(self.ops[e]):
                    fn, waits, incs, tag = self.ops[e][idx[e]]
                    if any(vals.get(s, 0) < v for s, v in waits):
                        break
                    for s, amt in incs:
                        vals[s] = vals.get(s, 0) + amt
                    idx[e] += 1
                    progress = True
        stuck = {e: i for e, i in idx.items() if i < len(self.ops[e])}
        if stuck:
            lines = []
            for e, i in stuck.items():
                fn, waits, incs, tag = self.ops[e][i]
                wd = [(s.name, v, vals.get(s, 0)) for s, v in waits]
                lines.append(f"  {e}[{i}/{len(self.ops[e])}] tag={tag} waits={wd}")
            raise RuntimeError("DEADLOCK in emitted program:\n" + "\n".join(lines))

    def replay(self, name, eng):
        for fn, waits, incs, tag in self.ops[name]:
            for s, v in waits:
                eng.wait_ge(s.h, v)
            if fn is None:
                continue
            inst = fn(eng)
            for s, amt in incs:
                inst = inst.then_inc(s.h, amt)


class DmaPool:

    def __init__(self, P, eng, mksem, name, K=8):
        self.P, self.eng, self.K = P, eng, K
        self.sems = [mksem(f"{name}{i}") for i in range(K)]
        self.uses = [0] * K
        self.i = 0

    def dma(self, fn, waits=(), tag=""):
        j = self.i % self.K
        s = self.sems[j]
        w = list(waits)
        if self.uses[j]:
            w.append((s, 16 * self.uses[j]))
        self.P.serial_dma = True
        self.P.op(self.eng, fn, waits=w, incs=[(s, 16)], tag=tag)
        self.uses[j] += 1
        self.i += 1
        return [(s, 16 * self.uses[j])]

    def snap(self):
        return [(s, 16 * u) for s, u in zip(self.sems, self.uses) if u]


def build_nc(taps=False, serial=False):
    import contextlib
    nc = bass.Bass("TRN2", target_bir_lowering=False, dynamic_dma_scratch_size=16384)

    def din(name, shape):
        return nc.dram_tensor(name, list(shape), F32, kind="ExternalInput").ap()

    xT = din("xT", (D, S))
    wq, wk, wv, wo = din("wq", (D, D)), din("wk", (D, 512)), din("wv", (D, 512)), din("wo", (D, D))
    w1t = din("w1t", (16, 128, 16, 512))
    w3t = din("w3t", (16, 128, 16, 512))
    w2t = din("w2t", (8, 128, 64, 256))
    cosqd, sinqd = din("cosq", (128, S)), din("sinq", (128, S))
    coskd, sinkd = din("cosk", (128, S)), din("sink", (128, S))
    h128d, h64d = din("h128", (128, 128)), din("h64x2", (128, 128))
    onescold, onesrowd, negrowd = din("onescol", (128, 1)), din("onesrow", (1, 128)), din("negrow", (1, 128))
    mask8d = din("mask8", (128, 4096))
    cvecd = din("cvec", (128, 4))

    yT = nc.dram_tensor("yT", [D, S], F32, kind="ExternalOutput").ap()
    ikind = "ExternalOutput" if taps else "Internal"
    h2d = nc.dram_tensor("h2d", [64, NTS, 128, TS], BF16, kind=ikind).ap()
    h3d = nc.dram_tensor("h3d", [NTS, 128, 64, TS], BF16, kind=ikind).ap()
    x1d = nc.dram_tensor("x1d", [NDC, 128, S], F32, kind=ikind).ap()
    if taps:
        tqfm = nc.dram_tensor("tqfm", [128, NH * S], BF16, kind="ExternalOutput").ap()
        tkfm = nc.dram_tensor("tkfm", [128, NKV * S], BF16, kind="ExternalOutput").ap()
        tvtm = nc.dram_tensor("tvtm", [128, 16 * 512], BF16, kind="ExternalOutput").ap()

    P = Prog()
    stack = contextlib.ExitStack()

    def sbuf(name, shape, dt):
        return stack.enter_context(nc.sbuf_tensor(name, list(shape), dt))

    def psum(name, shape, dt=F32):
        return stack.enter_context(nc.psum_tensor(name, list(shape), dt))

    def sem(name):
        return Sem(stack.enter_context(nc.semaphore(name)), name)

    with stack:
        A = sbuf("A", (128, 32768), BF16)
        B = sbuf("B", (128, 32768), BF16)
        C = sbuf("C", (128, 8192), BF16)
        Dv = sbuf("Dv", (128, 8192), BF16)
        W0 = sbuf("W0", (128, 8192), BF16)
        W1b = sbuf("W1b", (128, 8192), BF16)
        SCR = sbuf("SCR", (128, 3584), F32)

        AF32 = A[:].bitcast(F32)
        CF32 = C[:].bitcast(F32)
        DF32 = Dv[:].bitcast(F32)
        SCRb = SCR[:].bitcast(BF16)

        h64b = sbuf("h64b", (128, 128), BF16)
        onescolb = sbuf("onescolb", (128, 1), BF16)
        onescolf = sbuf("onescolf", (128, 1), F32)
        onesrow_s = sbuf("onesrow_s", (1, 128), F32)
        negrow_s = sbuf("negrow_s", (1, 128), F32)
        cvec_s = sbuf("cvec_s", (128, 4), F32)

        PSA = psum("PSA", (128, 1536), F32)
        PSB = psum("PSB", (128, 1536), F32)
        PTO = psum("PTO", (128, 1024), F32)

        h128b = A[:, 0:128]
        mrow1 = SCR[0:1, 2560:3072]
        vrow1 = SCR[0:1, 3072:3584]
        trow = SCR[0:1, 2048:2560]

        sBar = sem("sBar")
        sPre, sRpV, sRpA = sem("sPre"), sem("sRpV"), sem("sRpA")
        sQK, sExp, sMsk = sem("sQK"), sem("sExp"), sem("sMsk")
        sSum, sRcp, sBc2, sNrm, sPV, sOcp = (sem("sSum"), sem("sRcp"), sem("sBc2"),
                                             sem("sNrm"), sem("sPV"), sem("sOcp"))
        sWo, sZ, sZb, sSq, sSt = sem("sWo"), sem("sZ"), sem("sZb"), sem("sSq"), sem("sSt")
        sFinV, sFinA, sBc, sApl = sem("sFinV"), sem("sFinA"), sem("sBc"), sem("sApl")
        sUp, sSil, sGat, sH2e = sem("sUp"), sem("sSil"), sem("sGat"), sem("sH2e")
        sH64, sH64e, sDn, sDnE = sem("sH64"), sem("sH64e"), sem("sDn"), sem("sDnE")
        gp = DmaPool(P, "gpsimd", sem, "dG", 8)
        sy = DmaPool(P, "sync", sem, "dS", 8)
        sc = DmaPool(P, "scalar", sem, "dC", 8)
        if serial:
            P.serial_sem = sem("sOrd")

        sy.dma(lambda e: e.dma_start(cvec_s[:], cvecd), tag="ld.cvec")
        sy.dma(lambda e: e.dma_start(onesrow_s[:], onesrowd), tag="ld.onesrow")
        consts_sy = sy.dma(lambda e: e.dma_start(negrow_s[:], negrowd), tag="ld.negrow") + sy.snap()
        gp.dma(lambda e: e.dma_start(h64b[:], h64d), tag="ld.h64")
        gp.dma(lambda e: e.dma_start(onescolb[:], onescold), tag="ld.onescol")
        sy.dma(lambda e: e.dma_start(onescolf[:], onescold), tag="ld.onescolf")
        xv = xT.rearrange("(dc p) t -> p dc t", p=128)
        for i in range(4):
            gp.dma(lambda e, i=i: e.dma_start(A[:, i * 8192:(i + 1) * 8192], xv[:, 4 * i:4 * (i + 1), :]),
                   tag=f"ld.x{i}")

        t1f = PTO[:, 0:512]
        swpf = PSA[:, 512:1024]
        t2f = SCR[:, 2048:2560]
        cos_sl = lambda par: SCR[:, par * 1024: par * 1024 + 512]
        sin_sl = lambda par: SCR[:, par * 1024 + 512: par * 1024 + 1024]

        def pp(par):
            return (PSA if par == 0 else PSB)[:, 0:512]

        rope_done = []

        def emit_rope(par, dst, cs, sn, tag):
            mmw = (sPre, sPre.n)
            P.op("vector", lambda e, par=par, cs=cs: e.tensor_tensor(t1f, pp(par), cs, ALU.mult),
                 waits=[mmw], incs=[(sRpV, 1)], tag=f"{tag}.t1")
            swq = [(sRpV, sRpV.n)]
            P.op("scalar", lambda e, par=par: e.activation(
                swpf[0:64, :], (PSA if par == 0 else PSB)[64:128, 0:512], AF.Copy),
                waits=[mmw] + swq, incs=[(sRpA, 1)], tag=f"{tag}.sw0")
            P.op("scalar", lambda e, par=par: e.activation(
                swpf[64:128, :], (PSA if par == 0 else PSB)[0:64, 0:512], AF.Copy),
                incs=[(sRpA, 1)], tag=f"{tag}.sw1")
            P.op("vector", lambda e, sn=sn: e.tensor_tensor(t2f, swpf[:], sn, ALU.mult),
                 waits=[(sRpA, sRpA.n)], incs=[(sRpV, 1)], tag=f"{tag}.t2")
            P.op("vector", lambda e, dst=dst: e.tensor_tensor(dst, t2f, t1f, ALU.add),
                 incs=[(sRpV, 1)], tag=f"{tag}.add")
            rope_done.append(sRpV.n)

        wkv = wk.rearrange("(kc p) o -> p kc o", p=128)
        gp.dma(lambda e: e.dma_start(W0[:], wkv), tag="ld.wk")
        k_ld = gp.snap()
        nunit = 0
        tab_user = {0: 0, 1: 0}
        tab_ready = {0: [], 1: []}

        def load_tab(par, ts, cosd, sind, tag):
            t1 = sy.dma(lambda e, par=par, ts=ts, cosd=cosd: e.dma_start(
                cos_sl(par), cosd[:, ts * 512:(ts + 1) * 512]),
                waits=[(sRpV, tab_user[par])], tag=f"{tag}.cos")
            t2 = sy.dma(lambda e, par=par, ts=ts, sind=sind: e.dma_start(
                sin_sl(par), sind[:, ts * 512:(ts + 1) * 512]), tag=f"{tag}.sin")
            tab_ready[par] = t1 + t2

        for ts in range(NTS):
            tpar = ts % 2
            load_tab(tpar, ts, coskd, sinkd, f"ktab.{ts}")
            for g in range(NKV):
                par = nunit % 2
                w0 = k_ld + tab_ready[tpar]
                if nunit >= 2:
                    w0 = w0 + [(sRpV, rope_done[nunit - 2])]
                for kc in range(NDC):
                    P.op("tensor", lambda e, g=g, ts=ts, kc=kc, par=par: e.matmul(
                        pp(par), W0[:, kc * 512 + g * 128: kc * 512 + (g + 1) * 128],
                        A[:, kc * 2048 + ts * 512: kc * 2048 + (ts + 1) * 512],
                        start=(kc == 0), stop=(kc == NDC - 1)),
                        waits=(w0 if kc == 0 else ()),
                        incs=([(sPre, 1)] if kc == NDC - 1 else ()), tag=f"k.mm.{g}.{ts}")
                emit_rope(par, C[:, g * 2048 + ts * 512: g * 2048 + (ts + 1) * 512],
                          cos_sl(tpar), sin_sl(tpar), f"k.rope.{g}.{ts}")
                tab_user[tpar] = sRpV.n
                nunit += 1

        wvv = wv.rearrange("(kc p) o -> p kc o", p=128)
        gp.dma(lambda e: e.dma_start(W0[:], wvv), waits=[(sPre, sPre.n)], tag="ld.wv")
        v_ld = gp.snap()
        for tc in range(16):
            par = nunit % 2
            w0 = list(v_ld)
            if nunit >= 2:
                w0.append((sRpV, rope_done[nunit - 2]))
            for kc in range(NDC):
                P.op("tensor", lambda e, tc=tc, kc=kc, par=par: e.matmul(
                    pp(par), A[:, kc * 2048 + tc * 128: kc * 2048 + (tc + 1) * 128],
                    W0[:, kc * 512:(kc + 1) * 512],
                    start=(kc == 0), stop=(kc == NDC - 1)),
                    waits=(w0 if kc == 0 else ()),
                    incs=([(sPre, 1)] if kc == NDC - 1 else ()), tag=f"v.mm.{tc}")
            P.op("vector", lambda e, tc=tc, par=par: e.tensor_copy(
                Dv[:, tc * 512:(tc + 1) * 512], pp(par)),
                waits=[(sPre, sPre.n)], incs=[(sRpV, 1)], tag=f"v.ev.{tc}")
            rope_done.append(sRpV.n)
            nunit += 1

        wqv = wq.rearrange("(kc p) o -> p kc o", p=128)
        q_mm_at_eighth = {}
        for ei in range(8):
            hb = ei % 2
            ldw = [] if ei < 2 else [(sPre, q_mm_at_eighth[ei - 2])]
            q_tok = gp.dma(lambda e, ei=ei, hb=hb: e.dma_start(
                W1b[:, hb * 4096:(hb + 1) * 4096], wqv[:, :, ei * 256:(ei + 1) * 256]),
                waits=ldw, tag=f"ld.wq{ei}")
            for ts in range(NTS):
                tpar = ts % 2
                load_tab(tpar, ts, cosqd, sinqd, f"qtab.{ei}.{ts}")
                for qc2 in range(2):
                    h = ei * 2 + qc2
                    par = nunit % 2
                    w0 = v_ld + q_tok + tab_ready[tpar]
                    if nunit >= 2:
                        w0 = w0 + [(sRpV, rope_done[nunit - 2])]
                    for kc in range(NDC):
                        P.op("tensor", lambda e, h=h, ts=ts, kc=kc, par=par, hb=hb, qc2=qc2: e.matmul(
                            pp(par),
                            W1b[:, hb * 4096 + kc * 256 + qc2 * 128: hb * 4096 + kc * 256 + (qc2 + 1) * 128],
                            A[:, kc * 2048 + ts * 512: kc * 2048 + (ts + 1) * 512],
                            start=(kc == 0), stop=(kc == NDC - 1)),
                            waits=(w0 if kc == 0 else ()),
                            incs=([(sPre, 1)] if kc == NDC - 1 else ()), tag=f"q.mm.{h}.{ts}")
                    emit_rope(par, B[:, h * 2048 + ts * 512: h * 2048 + (ts + 1) * 512],
                              cos_sl(tpar), sin_sl(tpar), f"q.rope.{h}.{ts}")
                    tab_user[tpar] = sRpV.n
                    nunit += 1
            q_mm_at_eighth[ei] = sPre.n

        P.barrier(sBar, 1)
        mask_tok = gp.dma(lambda e: e.dma_start(SCRb[:, 0:4096], mask8d), tag="ld.mask8")
        if taps:
            sy.dma(lambda e: e.dma_start(tqfm, B[:]), tag="tap.q")
            sy.dma(lambda e: e.dma_start(tkfm, C[:]), tag="tap.k")
            sy.dma(lambda e: e.dma_start(tvtm, Dv[:]), tag="tap.v")

        attn_col = A[:, 24576:32768]
        pt_slot = lambda par, s: A[:, 20480 + par * 1536 + s * 512: 20480 + par * 1536 + (s + 1) * 512]
        srow_s = AF32[0:1, 11776:12288]
        mask_sl = lambda mid: SCRb[:, mid * 512:(mid + 1) * 512]

        def S_bk(par, gi):
            return (PSA if par == 0 else PSB)[:, (gi % 2) * 512:(gi % 2) * 512 + 512]

        def O_bk(par):
            return (PSA if par == 0 else PSB)[:, 1024:1536]

        sumrow = lambda par: PTO[0:1, par * 512:(par + 1) * 512]
        Rbc = lambda par: PTO[:, par * 512:(par + 1) * 512]
        statS_a = PSA[0:1, 1024:1536]
        statS2_a = PSB[0:1, 1024:1536]

        z1_t = lambda dc: AF32[:, dc * 512:(dc + 1) * 512]
        xres_t = lambda par: AF32[:, 8192 + par * 512: 8192 + (par + 1) * 512]
        zsq1_t = lambda par: AF32[:, 9216 + par * 512: 9216 + (par + 1) * 512]

        def qg_tiles(qg):
            if qg == 0:
                return [(kt, kt) for kt in range(4)]
            if qg == 1:
                return [(kt, None) for kt in range(4)] + [(kt, kt - 4) for kt in range(4, 8)]
            if qg == 2:
                return ([(kt, 4 + kt) for kt in range(4)] + [(kt, None) for kt in range(4, 8)]
                        + [(kt, kt - 8) for kt in range(8, 12)])
            return ([(kt, 4 + kt - 4) for kt in range(4, 8)] + [(kt, None) for kt in range(8, 12)]
                    + [(kt, kt - 12) for kt in range(12, 16)])

        unit = 0
        ptile_cnt = {0: 0, 1: 0}
        ptile_pv = {0: [], 1: []}
        bank_last_exp = {0: {0: 0, 1: 0}, 1: {0: 0, 1: 0}}
        bc_cnt, nrm_cnt, sum_stop, pv_stop, ocp_cnt = {}, {}, {}, {}, {}
        apl_gate = [0]
        fin_gate = [0]
        wo_all_prev = [0]
        z_all_prev = [0]
        first_mask = [True]
        pend_tail = []
        unit_tiles = {}

        def emit_sum_pv(u, t):
            par, h, g, nkt = u % 2, (u % 16), ((u % 16) // 4), len(unit_tiles[u])
            t_, gi, slot, kt, rdy = unit_tiles[u][t]
            sw = [rdy]
            if t == 0:
                if u >= 2:
                    sw.append((sNrm, nrm_cnt[u - 2]))
                if u % 16 in (0, 1) and apl_gate[0]:
                    sw.append((sApl, apl_gate[0]))
            P.op("tensor", lambda e, par=par, slot=slot, st=(t == 0), sp=(t == nkt - 1): e.matmul(
                sumrow(par), onescolb[:], pt_slot(par, slot), start=st, stop=sp),
                waits=sw, incs=[(sSum, 1)], tag=f"sm.{u}.{t}")
            if t == nkt - 1:
                sum_stop[u] = sSum.n
            pw = []
            if t == 0:
                if u >= 2:
                    pw.append((sOcp, ocp_cnt[u - 2]))
                if u % 16 in (0, 1) and fin_gate[0]:
                    pw.append((sFinV, fin_gate[0]))
            P.op("tensor", lambda e, par=par, g=g, kt=kt, slot=slot, st=(t == 0), sp=(t == nkt - 1): e.matmul(
                O_bk(par), Dv[:, kt * 512 + g * 128: kt * 512 + (g + 1) * 128],
                pt_slot(par, slot), start=st, stop=sp),
                waits=pw, incs=[(sPV, 1)], tag=f"pv.{u}.{t}")
            ptile_pv[par].append(sPV.n)
            if t == nkt - 1:
                pv_stop[u] = sPV.n

        def emit_tail():
            u, par, h = pend_tail.pop(0)
            cw = [(sPV, pv_stop[u])]
            if wo_all_prev[0]:
                cw.append((sWo, wo_all_prev[0]))
            P.op("scalar", lambda e, par=par, h=h: e.activation(
                attn_col[:, h * 512:(h + 1) * 512], O_bk(par), AF.Copy),
                waits=cw, incs=[(sOcp, 1)], tag=f"ocp.{u}")
            ocp_cnt[u] = sOcp.n
            rw = [(sSum, sum_stop[u])]
            if u >= 1:
                rw.append((sBc2, bc_cnt[u - 1]))
            P.op("vector", lambda e, par=par: e.reciprocal(srow_s, sumrow(par)),
                 waits=rw, incs=[(sRcp, 1)], tag=f"rcp.{u}")
            bw = [(sRcp, sRcp.n)]
            if u >= 2:
                bw.append((sNrm, nrm_cnt[u - 2]))
            if u % 16 in (0, 1) and apl_gate[0]:
                bw.append((sApl, apl_gate[0]))
            P.op("tensor", lambda e, par=par: e.matmul(
                Rbc(par), onesrow_s[:], srow_s, start=True, stop=True),
                waits=bw, incs=[(sBc2, 1)], tag=f"bc2.{u}")
            bc_cnt[u] = sBc2.n
            nw = [(sBc2, sBc2.n), (sOcp, ocp_cnt[u])]
            P.op("vector", lambda e, par=par, h=h: e.tensor_tensor(
                attn_col[:, h * 512:(h + 1) * 512],
                attn_col[:, h * 512:(h + 1) * 512], Rbc(par), ALU.mult),
                waits=nw, incs=[(sNrm, 1)], tag=f"nrm.{u}")
            nrm_cnt[u] = sNrm.n

        def emit_unit(h, qg):
            nonlocal unit
            u = unit
            par = u % 2
            g = h // 4
            tiles = qg_tiles(qg)
            nkt = len(tiles)
            unit_tiles[u] = []
            for t, (kt, mid) in enumerate(tiles):
                gi = ptile_cnt[par]
                qw = []
                le = bank_last_exp[par][gi % 2]
                if le:
                    qw.append((sExp, le))
                if u % 16 == 0 and t < 2 and z_all_prev[0]:
                    qw.append((sZ, z_all_prev[0]))
                P.op("tensor", lambda e, par=par, gi=gi, g=g, kt=kt, h=h, qg=qg: e.matmul(
                    S_bk(par, gi), C[:, g * 2048 + kt * 128: g * 2048 + (kt + 1) * 128],
                    B[:, h * 2048 + qg * 512: h * 2048 + (qg + 1) * 512],
                    start=True, stop=True),
                    waits=qw, incs=[(sQK, 1)], tag=f"qk.{u}.{t}")
                ew = [(sQK, sQK.n)]
                slot = gi % 3
                if gi >= 3:
                    ew.append((sPV, ptile_pv[par][gi - 3]))
                P.op("scalar", lambda e, par=par, gi=gi, slot=slot: e.activation(
                    pt_slot(par, slot), S_bk(par, gi), AF.Exp),
                    waits=ew, incs=[(sExp, 1)], tag=f"exp.{u}.{t}")
                bank_last_exp[par][gi % 2] = sExp.n
                if mid is not None:
                    mw = [(sExp, sExp.n)]
                    if first_mask[0]:
                        mw = mw + mask_tok
                        first_mask[0] = False
                    P.op("vector", lambda e, par=par, slot=slot, mid=mid: e.tensor_tensor(
                        pt_slot(par, slot), pt_slot(par, slot), mask_sl(mid), ALU.mult),
                        waits=mw, incs=[(sMsk, 1)], tag=f"msk.{u}.{t}")
                    rdy = (sMsk, sMsk.n)
                else:
                    rdy = (sExp, sExp.n)
                unit_tiles[u].append((t, gi, slot, kt, rdy))
                ptile_cnt[par] += 1
                if t >= 2:
                    emit_sum_pv(u, t - 2)
                if t == min(6, nkt - 2) and pend_tail:
                    emit_tail()
            emit_sum_pv(u, nkt - 2)
            emit_sum_pv(u, nkt - 1)
            pend_tail.append((u, par, h))
            unit += 1

        def emit_ln_finalize(statS, statS2, R_ps, B_ps, prev_bc, tag, mrow=None, vrow=None):
            if mrow is None:
                mrow, vrow = mrow1, vrow1
            P.op("vector", lambda e: e.tensor_scalar(mrow, statS, cvec_s[0:1, 3:4], None, ALU.mult),
                 waits=[(sSt, sSt.n), (sBc, prev_bc)], tag=f"{tag}.mean")
            P.op("vector", lambda e: e.tensor_scalar(trow, statS2, cvec_s[0:1, 3:4], None, ALU.mult),
                 tag=f"{tag}.e2")
            P.op("vector", lambda e: e.tensor_tensor(vrow, mrow, mrow, ALU.mult),
                 tag=f"{tag}.m2")
            P.op("vector", lambda e: e.tensor_tensor(trow, trow, vrow, ALU.subtract),
                 incs=[(sFinV, 1)], tag=f"{tag}.var")
            P.op("scalar", lambda e: e.activation(vrow, trow, AF.Ln, bias=cvec_s[0:1, 0:1]),
                 waits=[(sFinV, sFinV.n), (sBc, prev_bc)], tag=f"{tag}.lnf")
            P.op("scalar", lambda e: e.activation(vrow, vrow, AF.Exp, scale=cvec_s[0:1, 1:2]),
                 incs=[(sFinA, 1)], tag=f"{tag}.expf")
            P.op("tensor", lambda e: e.matmul(R_ps, onesrow_s[:], vrow, start=True, stop=True),
                 waits=[(sFinA, sFinA.n)], incs=[(sBc, 1)], tag=f"{tag}.bcR")
            P.op("vector", lambda e: e.tensor_tensor(mrow, mrow, vrow, ALU.mult),
                 waits=[(sFinA, sFinA.n)], incs=[(sFinV, 1)], tag=f"{tag}.brow")
            P.op("tensor", lambda e: e.matmul(B_ps, negrow_s[:], mrow, start=True, stop=True),
                 waits=[(sFinV, sFinV.n)], incs=[(sBc, 1)], tag=f"{tag}.bcB")

        wov = wo.rearrange("(kc p) o -> p kc o", p=128)
        wo_user_done = {0: None, 1: None}
        x1_pend = []
        x1_tok = {}

        def flush_x1():
            qg_, dc_, apl = x1_pend.pop(0)
            x1_tok[(qg_, dc_)] = sy.dma(lambda e, dc=dc_, qg0_=qg_ * QG: e.dma_start(
                x1d[dc, :, qg0_:qg0_ + QG], z1_t(dc)),
                waits=[(sApl, apl)], tag=f"x1w.{qg_}.{dc_}")
        R1 = PSA[:, 1024:1536]
        B1 = PSB[:, 1024:1536]
        R1a = PTO[:, 0:512]
        B1a = PTO[:, 512:1024]
        statS = PSB[0:1, 0:512]
        statS2 = PSB[0:1, 512:1024]

        for qg in range(NQG):
            qg0 = qg * QG
            for h in range(NH):
                emit_unit(h, qg)
            while pend_tail:
                emit_tail()
            nrm_all = sNrm.n
            z_done, st_done = {}, {}
            stat_pend = []

            def emit_stats(z_t, zsq_t, which):
                dc, zpar, zc, sqc = stat_pend.pop(0)
                P.op("tensor", lambda e, dc=dc, zpar=zpar: e.matmul(
                    statS_a, onescolf[:], z_t(dc),
                    start=(dc == 0), stop=(dc == NDC - 1)),
                    waits=[(sZ, zc)], tag=f"st1.{which}.{dc}")
                P.op("tensor", lambda e, dc=dc, zpar=zpar: e.matmul(
                    statS2_a, onescolf[:], zsq_t(zpar),
                    start=(dc == 0), stop=(dc == NDC - 1)),
                    waits=[(sSq, sqc)],
                    incs=([(sSt, 1)] if dc == NDC - 1 else ()), tag=f"st2.{which}.{dc}")
                st_done[dc] = sSt.n

            prev_bc = sBc.n
            for dcg in range(4):
                wpar = dcg % 2
                wobuf = W0 if wpar == 0 else W1b
                ldw = []
                if wo_user_done[wpar] is not None:
                    ldw.append((sWo, wo_user_done[wpar]))
                elif qg == 0 and dcg < 2:
                    ldw.append((sPre, sPre.n))
                wo_ld = gp.dma(lambda e, dcg=dcg, wobuf=wobuf: e.dma_start(
                    wobuf[:], wov[:, :, dcg * 512:(dcg + 1) * 512]),
                    waits=ldw, tag=f"ld.wo.{qg}.{dcg}")
                for _ in range(4):
                    if x1_pend:
                        flush_x1()
                for dc_l in range(4):
                    dc = dcg * 4 + dc_l
                    zpar = dc % 2
                    woPS = PSA[:, zpar * 512:(zpar + 1) * 512]
                    xw = [] if dc < 2 else [(sZ, z_done[dc - 2])]
                    xres_tok = sy.dma(lambda e, dc=dc, zpar=zpar, qg0=qg0: e.dma_start(
                        xres_t(zpar), xv[:, dc, qg0:qg0 + QG]),
                        waits=xw, tag=f"xres.{qg}.{dc}")
                    w0 = wo_ld
                    if dc >= 2:
                        w0 = w0 + [(sZ, z_done[dc - 2])]
                    for kc in range(NDC):
                        kw = list(w0) if kc == 0 else []
                        if dc == 0:
                            kw.append((sNrm, nrm_cnt[qg * 16 + kc]))
                        elif kc == 0:
                            kw.append((sNrm, nrm_all))
                        P.op("tensor", lambda e, kc=kc, dc_l=dc_l, wobuf=wobuf, woPS=woPS: e.matmul(
                            woPS, wobuf[:, kc * 512 + dc_l * 128: kc * 512 + (dc_l + 1) * 128],
                            attn_col[:, kc * 512:(kc + 1) * 512],
                            start=(kc == 0), stop=(kc == NDC - 1)),
                            waits=kw,
                            incs=([(sWo, 1)] if kc == NDC - 1 else ()), tag=f"wo.{qg}.{dc}")
                    zaw = [(sWo, sWo.n)] + xres_tok
                    if (qg - 1, dc) in x1_tok:
                        zaw += x1_tok[(qg - 1, dc)]
                    P.op("vector", lambda e, dc=dc, zpar=zpar, woPS=woPS: e.tensor_tensor(
                        z1_t(dc), woPS, xres_t(zpar), ALU.add),
                        waits=zaw, incs=[(sZ, 1)], tag=f"z1.{qg}.{dc}")
                    z_done[dc] = sZ.n
                    zw = [(sZ, sZ.n)] if dc < 2 else [(sZ, sZ.n), (sSt, st_done.get(dc - 2, 0))]
                    P.op("vector", lambda e, dc=dc, zpar=zpar: e.tensor_tensor(
                        zsq1_t(zpar), z1_t(dc), z1_t(dc), ALU.mult),
                        waits=zw, incs=[(sSq, 1)], tag=f"zsq.{qg}.{dc}")
                    stat_pend.append((dc, zpar, z_done[dc], sSq.n))
                    if len(stat_pend) >= 2:
                        emit_stats(z1_t, zsq1_t, 1)
                wo_user_done[wpar] = sWo.n
            while stat_pend:
                emit_stats(z1_t, zsq1_t, 1)
            emit_ln_finalize(statS_a, statS2_a, R1a, B1a, prev_bc, f"ln1.{qg}")
            for dc in range(NDC):
                aw = [(sBc, sBc.n)]
                P.op("vector", lambda e, dc=dc: e.tensor_tensor(
                    z1_t(dc), z1_t(dc), R1a, ALU.mult),
                    waits=aw, tag=f"ap1.{qg}.{dc}")
                P.op("vector", lambda e, dc=dc: e.tensor_tensor(
                    z1_t(dc), z1_t(dc), B1a, ALU.add),
                    incs=[(sApl, 1)], tag=f"ap2.{qg}.{dc}")
                x1_pend.append((qg, dc, sApl.n))
            apl_gate[0] = sApl.n
            fin_gate[0] = sFinV.n
            wo_all_prev[0] = sWo.n
            z_all_prev[0] = sZ.n
            if qg == NQG - 1:
                while x1_pend:
                    flush_x1()

        x1w_all = sy.snap()
        qk_all = sQK.n
        pv_all = sPV.n
        wo_all = sWo.n

        x1v = x1d.rearrange("dc p t -> p dc t")
        for i in range(4):
            gp.dma(lambda e, i=i: e.dma_start(
                B[:, i * 8192:(i + 1) * 8192], x1v[:, 4 * i:4 * (i + 1), :]),
                waits=x1w_all + [(sQK, qk_all)], tag=f"ld.x1bf{i}")

        P.barrier(sBar, 2)
        h128_tok = gp.dma(lambda e: e.dma_start(h128b, h128d), tag="ld.h128")


        sg_t = lambda par: SCR[:, par * 512:(par + 1) * 512]
        hbf_t = lambda par: SCRb[:, 2048 + par * 512: 2048 + (par + 1) * 512]
        h2st_t = lambda par: SCRb[:, 3072 + par * 512: 3072 + (par + 1) * 512]

        def up_ps(par, k):
            return (PSA if par == 0 else PSB)[:, k * 512:(k + 1) * 512]

        up_idx = 0
        up_mm_done = {}
        sil_done, gat_done, h2e_done, h2w_done = {}, {}, {}, {}
        pend_up = []

        def up_tail(i):
            c, ts, par = pend_up.pop(0)
            hw = [(sGat, gat_done[i])]
            if i == 0:
                hw += h128_tok
            if i >= 2:
                hw.append((sH2e, h2e_done[i - 2]))
            P.op("tensor", lambda e, par=par: e.matmul(
                up_ps(par, 2), h128b, hbf_t(par), start=True, stop=True),
                waits=hw, incs=[(sUp, 1)], tag=f"h128.{i}")
            ew = [(sUp, sUp.n)]
            if i >= 2:
                ew += h2w_done[i - 2]
            P.op("scalar", lambda e, par=par: e.activation(
                h2st_t(par), up_ps(par, 2), AF.Copy, scale=cvec_s[:, 2:3]),
                waits=ew, incs=[(sH2e, 1)], tag=f"h2e.{i}")
            h2e_done[i] = sH2e.n
            h2w_done[i] = sy.dma(lambda e, c=c, ts=ts, par=par: e.dma_start(h2d[c, ts], h2st_t(par)),
                                 waits=[(sH2e, sH2e.n)], tag=f"h2w.{i}")

        for cg in range(16):
            wpar = cg % 2
            w1buf = C if wpar == 0 else W0
            w3buf = Dv if wpar == 0 else W1b
            ldw = []
            if cg >= 2:
                ldw.append((sUp, up_mm_done[(cg - 2) * 16 + 15]))
            else:
                ldw += [(sQK, qk_all), (sPV, pv_all), (sWo, wo_all)]
            gp.dma(lambda e, cg=cg, w1buf=w1buf: e.dma_start(w1buf[:], w1t[cg]),
                waits=ldw, tag=f"ld.w1.{cg}")
            gp.dma(lambda e, cg=cg, w3buf=w3buf: e.dma_start(w3buf[:], w3t[cg]), tag=f"ld.w3.{cg}")
            ld_done = gp.snap()
            for cl in range(4):
                c = cg * 4 + cl
                for ts in range(NTS):
                    i = up_idx
                    par = i % 2
                    w0 = list(ld_done)
                    if i >= 2:
                        w0.append((sSil, sil_done[i - 2]))
                    for dc in range(NDC):
                        P.op("tensor", lambda e, dc=dc, cl=cl, ts=ts, par=par, w1buf=w1buf: e.matmul(
                            up_ps(par, 0), w1buf[:, dc * 512 + cl * 128: dc * 512 + (cl + 1) * 128],
                            B[:, dc * 2048 + ts * 512: dc * 2048 + (ts + 1) * 512],
                            start=(dc == 0), stop=(dc == NDC - 1)),
                            waits=(w0 if dc == 0 else ()),
                            incs=([(sUp, 1)] if dc == NDC - 1 else ()), tag=f"upA.{i}")
                    w0b = [(sGat, gat_done[i - 2])] if i >= 2 else []
                    for dc in range(NDC):
                        P.op("tensor", lambda e, dc=dc, cl=cl, ts=ts, par=par, w3buf=w3buf: e.matmul(
                            up_ps(par, 1), w3buf[:, dc * 512 + cl * 128: dc * 512 + (cl + 1) * 128],
                            B[:, dc * 2048 + ts * 512: dc * 2048 + (ts + 1) * 512],
                            start=(dc == 0), stop=(dc == NDC - 1)),
                            waits=(w0b if dc == 0 else ()),
                            incs=([(sUp, 1)] if dc == NDC - 1 else ()), tag=f"upB.{i}")
                    up_mm_done[i] = sUp.n
                    P.op("scalar", lambda e, par=par: e.activation(sg_t(par), up_ps(par, 0), AF.Silu),
                         waits=[(sUp, sUp.n)], incs=[(sSil, 1)], tag=f"sil.{i}")
                    sil_done[i] = sSil.n
                    P.op("vector", lambda e, par=par: e.tensor_tensor(
                        hbf_t(par), sg_t(par), up_ps(par, 1), ALU.mult),
                        waits=[(sSil, sSil.n)], incs=[(sGat, 1)], tag=f"gat.{i}")
                    gat_done[i] = sGat.n
                    pend_up.append((c, ts, par))
                    if i >= 1:
                        up_tail(i - 1)
                    up_idx += 1
        up_tail(up_idx - 1)
        up_all = sUp.n
        h2w_all = sy.snap()

        h2r = h2d.rearrange("c s (js jl) t -> s js c jl t", js=2)
        h3r4 = h3d.rearrange("s (js cp) jl t -> s js jl cp t", js=2)
        hin_t = lambda sl: W1b[:, sl * 1024:(sl + 1) * 1024]
        h3st_t = lambda sl: W0[:, sl * 1024:(sl + 1) * 1024]
        x1r_t = lambda par: SCR[:, par * 512:(par + 1) * 512]
        zsq2_t = lambda par: SCR[:, 1024 + par * 512: 1024 + (par + 1) * 512]

        def z2_t(dc):
            return CF32[:, dc * 512:(dc + 1) * 512] if dc < 8 else DF32[:, (dc - 8) * 512:(dc - 7) * 512]

        h64_units = [(ts, p2) for ts in range(NTS) for p2 in range(32)]
        h64_idx = 0
        h64_mm_done, h64_ev_done, h64_w_done = {}, {}, {}
        h3w_at_ts = {}
        h64_store_pend = []

        def flush_h64_store():
            i, ts_, p2_, sl_, ev = h64_store_pend.pop(0)
            h64_w_done[i] = gp.dma(lambda e, ts=ts_, p2=p2_, sl=sl_: e.dma_start(
                h3d[ts, :, 2 * p2:2 * p2 + 2, :], h3st_t(sl)),
                waits=[(sH64e, ev)], tag=f"h64w.{i}")

        def emit_h64(n):
            nonlocal h64_idx
            for _ in range(n):
                if h64_idx >= len(h64_units):
                    return
                i = h64_idx
                ts, p2 = h64_units[i]
                sl = i % 8
                lw = list(h2w_all)
                if i >= 8:
                    lw.append((sH64, h64_mm_done[i - 8]))
                ld_tok = gp.dma(lambda e, ts=ts, p2=p2, sl=sl: e.dma_start(
                    hin_t(sl), h2r[ts, :, :, 2 * p2:2 * p2 + 2, :]), waits=lw, tag=f"h64l.{i}")
                mw = list(ld_tok)
                if i >= 1:
                    mw.append((sH64e, h64_ev_done[i - 1]))
                for half in range(2):
                    P.op("tensor", lambda e, sl=sl, half=half: e.matmul(
                        PTO[:, half * 512:(half + 1) * 512], h64b[:],
                        hin_t(sl)[:, half * 512:(half + 1) * 512], start=True, stop=True),
                        waits=(mw if half == 0 else ()),
                        incs=([(sH64, 1)] if half == 1 else ()), tag=f"h64m.{i}")
                h64_mm_done[i] = sH64.n
                ew = [(sH64, sH64.n)]
                if (i - 8) in h64_w_done:
                    ew += h64_w_done[i - 8]
                elif i >= 8:
                    raise RuntimeError("h64 store lag exceeded staging depth")
                P.op("vector", lambda e, sl=sl: e.tensor_copy(h3st_t(sl), PTO[:, 0:1024]),
                     waits=ew, incs=[(sH64e, 1)], tag=f"h64e.{i}")
                h64_ev_done[i] = sH64e.n
                h64_store_pend.append((i, ts, p2, sl, sH64e.n))
                if len(h64_store_pend) > 4:
                    flush_h64_store()
                h64_idx += 1
                if (i + 1) % 32 == 0:
                    while h64_store_pend:
                        flush_h64_store()
                    h3w_at_ts[(i + 1) // 32 - 1] = gp.snap()

        emit_h64(32)

        yw = []
        y_pend = []
        y_tok = {}

        def flush_y():
            ts_, dc_, apl = y_pend.pop(0)
            y_tok[(ts_, dc_)] = sc.dma(lambda e, dc=dc_, ts=ts_: e.dma_start(
                yT[dc * 128:(dc + 1) * 128, ts * 512:(ts + 1) * 512], z2_t(dc)),
                waits=[(sApl, apl)], tag=f"yw.{ts_}.{dc_}")
            yw.append(y_tok[(ts_, dc_)])

        dn_done, dne_done = {}, {}
        st2_done = {}
        dn_mm_at_ts = {}
        gat_all = sGat.n
        for ts in range(NTS):
            lw = list(h3w_at_ts[ts])
            if ts == 0:
                lw += [(sApl, apl_gate[0])] + x1w_all
            else:
                lw.append((sDn, dn_mm_at_ts[ts - 1]))
            Av = A[:].rearrange("p (c t) -> p c t", t=512)
            for i8 in range(4):
                for js in range(2):
                    gp.dma(lambda e, ts=ts, i8=i8, js=js: e.dma_start(
                        Av[js * 64:(js + 1) * 64, i8 * 16:(i8 + 1) * 16, :],
                        h3r4[ts, js, :, i8 * 16:(i8 + 1) * 16, :]),
                        waits=(lw if (i8 == 0 and js == 0) else ()), tag=f"h3l.{ts}.{i8}.{js}")
            h3l_done = gp.snap()
            prev_bc2 = sBc.n
            stat_pend2 = []
            st_done2 = {}

            def emit_stats2(ts=ts):
                dc, zpar, zc, sqc = stat_pend2.pop(0)
                P.op("tensor", lambda e, dc=dc, zpar=zpar: e.matmul(
                    statS, onescolf[:], z2_t(dc), start=(dc == 0), stop=(dc == NDC - 1)),
                    waits=[(sDnE, zc)], tag=f"st1b.{ts}.{dc}")
                P.op("tensor", lambda e, dc=dc, zpar=zpar: e.matmul(
                    statS2, onescolf[:], zsq2_t(zpar), start=(dc == 0), stop=(dc == NDC - 1)),
                    waits=[(sSq, sqc)],
                    incs=([(sSt, 1)] if dc == NDC - 1 else ()), tag=f"st2b.{ts}.{dc}")
                st2_done[(ts, dc)] = sSt.n

            for dg in range(8):
                wpar = dg % 2
                ldw = []
                if ts == 0 and dg < 2:
                    ldw.append((sUp, up_all))
                else:
                    key = (ts, dg - 2) if dg >= 2 else (ts - 1, dg + 6)
                    ldw.append((sDn, dn_done[key]))
                gp.dma(lambda e, dg=dg, wpar=wpar: e.dma_start(
                    B[:, wpar * 16384:(wpar + 1) * 16384], w2t[dg]),
                    waits=ldw, tag=f"w2l.{ts}.{dg}")
                w2_ld = gp.snap()
                for _ in range(2):
                    if y_pend:
                        flush_y()
                for dc2 in range(2):
                    dc = dg * 2 + dc2
                    dps = PSA[:, dc2 * 512:(dc2 + 1) * 512]
                    zpar = dc % 2
                    xw = []
                    if ts > 0 or dc >= 1:
                        xw.append((sDnE, dne_done[(ts, dc - 1) if dc >= 1 else (ts - 1, 15)]))
                    else:
                        xw.append((sGat, gat_all))
                    x1r_tok = sy.dma(lambda e, dc=dc, ts=ts, zpar=zpar: e.dma_start(
                        x1r_t(zpar), x1d[dc, :, ts * 512:(ts + 1) * 512]),
                        waits=xw + (x1w_all if ts == 0 and dc == 0 else []), tag=f"x1r.{ts}.{dc}")
                    mw = w2_ld + h3l_done
                    if ts > 0 or dc >= 2:
                        mw.append((sDnE, dne_done[(ts, dc - 2) if dc >= 2 else (ts - 1, dc + 14)]))
                    for cc in range(64):
                        P.op("tensor", lambda e, cc=cc, dc2=dc2, dps=dps, wpar=wpar: e.matmul(
                            dps, B[:, wpar * 16384 + cc * 256 + dc2 * 128: wpar * 16384 + cc * 256 + (dc2 + 1) * 128],
                            A[:, cc * 512:(cc + 1) * 512],
                            start=(cc == 0), stop=(cc == 63)),
                            waits=(mw if cc == 0 else ()),
                            incs=([(sDn, 1)] if cc == 63 else ()), tag=f"dn.{ts}.{dc}")
                    z2w = [(sDn, sDn.n)] + x1r_tok
                    if (ts - 1, dc) in y_tok:
                        z2w += y_tok[(ts - 1, dc)]
                    P.op("vector", lambda e, dc=dc, dps=dps, zpar=zpar: e.tensor_tensor(
                        z2_t(dc), dps, x1r_t(zpar), ALU.add),
                        waits=z2w, incs=[(sDnE, 1)], tag=f"z2.{ts}.{dc}")
                    dne_done[(ts, dc)] = sDnE.n
                    zw = [(sDnE, sDnE.n)]
                    if ts > 0 or dc >= 2:
                        zw.append((sSt, st2_done.get((ts, dc - 2) if dc >= 2 else (ts - 1, dc + 14), 0)))
                    P.op("vector", lambda e, dc=dc, zpar=zpar: e.tensor_tensor(
                        zsq2_t(zpar), z2_t(dc), z2_t(dc), ALU.mult),
                        waits=zw, incs=[(sSq, 1)], tag=f"zsq2.{ts}.{dc}")
                    stat_pend2.append((dc, zpar, dne_done[(ts, dc)], sSq.n))
                    if len(stat_pend2) >= 2:
                        emit_stats2()
                dn_done[(ts, dg)] = sDn.n
                emit_h64(4)
            while stat_pend2:
                emit_stats2()
            dn_mm_at_ts[ts] = sDn.n
            emit_ln_finalize(statS, statS2, R1, B1, prev_bc2, f"ln2.{ts}")
            for dc in range(NDC):
                aw = [(sBc, sBc.n)]
                P.op("vector", lambda e, dc=dc: e.tensor_tensor(
                    z2_t(dc), z2_t(dc), R1, ALU.mult),
                    waits=aw, tag=f"apy1.{ts}.{dc}")
                P.op("vector", lambda e, dc=dc: e.tensor_tensor(
                    z2_t(dc), z2_t(dc), B1, ALU.add),
                    incs=[(sApl, 1)], tag=f"apy2.{ts}.{dc}")
                y_pend.append((ts, dc, sApl.n))
            if ts == NTS - 1:
                while y_pend:
                    flush_y()

        P.op("sync", None, waits=sy.snap(), tag="final.sy")
        P.op("gpsimd", None, waits=gp.snap(), tag="final.gp")
        P.op("scalar", None, waits=sc.snap(), tag="final.sc")

        P.check_deadlock()
        nops = {e: len(P.ops[e]) for e in ENGINES}

        with nc.Block() as block:
            @block.tensor
            def _(eng):
                P.replay("tensor", eng)

            @block.vector
            def _(eng):
                P.replay("vector", eng)

            @block.scalar
            def _(eng):
                P.replay("scalar", eng)

            @block.sync
            def _(eng):
                P.replay("sync", eng)

            @block.gpsimd
            def _(eng):
                P.replay("gpsimd", eng)

    return nc, nops


_CACHE = {}


def _get_nc(taps=False):
    key = bool(taps)
    if key not in _CACHE:
        _CACHE[key] = build_nc(taps=taps)
    return _CACHE[key]


def make_in_maps(inputs):
    x = np.asarray(inputs["L0_attn_hidden_states"], dtype=np.float32)
    consts = host_constants()
    shared = {
        "wq": np.ascontiguousarray(inputs["L0_attn_w_q"], dtype=np.float32),
        "wk": np.ascontiguousarray(inputs["L0_attn_w_k"], dtype=np.float32),
        "wv": np.ascontiguousarray(inputs["L0_attn_w_v"], dtype=np.float32),
        "wo": np.ascontiguousarray(inputs["L0_attn_w_o"], dtype=np.float32),
        "w1t": np.ascontiguousarray(
            np.asarray(inputs["L0_ffn_W1"], dtype=np.float32).T.reshape(16, 128, 16, 512)
            .transpose(3, 1, 0, 2).reshape(16, 128, 16, 512)) if False else
            np.ascontiguousarray(np.asarray(inputs["L0_ffn_W1"], dtype=np.float32).T
                                 .reshape(16, 128, 16, 512).transpose(2, 1, 0, 3)),
        "w3t": np.ascontiguousarray(np.asarray(inputs["L0_ffn_W3"], dtype=np.float32).T
                                    .reshape(16, 128, 16, 512).transpose(2, 1, 0, 3)),
        "w2t": np.ascontiguousarray(np.asarray(inputs["L0_ffn_W2"], dtype=np.float32).T
                                    .reshape(64, 128, 8, 256).transpose(2, 1, 0, 3)),
    }
    shared.update(consts)
    in_maps = []
    for i in range(8):
        m = dict(shared)
        m["xT"] = np.ascontiguousarray(x[i].T)
        in_maps.append(m)
    return in_maps


def kernel(**inputs):
    nc, _ = _get_nc(taps=False)
    in_maps = make_in_maps(inputs)
    res = run_bass_kernel_spmd(nc, in_maps, core_ids=list(range(8)))
    out = np.empty((8, S, D), dtype=np.float32)
    for i in range(8):
        out[i] = res.results[i]["yT"].T
    return out


# version 20
# speedup vs baseline: 1.4931x; 1.0046x over previous
import numpy as np
import ml_dtypes
import concourse.bass as bass
import concourse.mybir as mybir
from concourse.bass_utils import run_bass_kernel_spmd

F32 = mybir.dt.float32
BF16 = mybir.dt.bfloat16
AF = mybir.ActivationFunctionType
ALU = mybir.AluOpType
AX = mybir.AxisListType

S, D, NH, NKV, HD, WIN, HF = 2048, 2048, 16, 4, 128, 1024, 8192
NDC = D // 128
NTS, TS = 4, 512
NQT = S // 128
QG, NQG = 512, 4
EPS = 1e-5
FW_SC = float(1.0 / np.sqrt(HF))
NEG = -1.0e30
ENGINES = ("tensor", "vector", "scalar", "sync", "gpsimd")


def _hadamard(n):
    h = np.array([[1.0]], dtype=np.float32)
    while h.shape[0] < n:
        h = np.block([[h, h], [h, -h]])
    return h.astype(np.float32)


def host_constants():
    t = np.arange(S, dtype=np.float32)
    inv = 1.0 / (10000.0 ** (np.arange(0, HD, 2, dtype=np.float32) / HD))
    fr = np.outer(t, inv)
    emb = np.concatenate([fr, fr], axis=-1)
    cos = np.ascontiguousarray(np.cos(emb).T)
    sin = np.ascontiguousarray(np.sin(emb).T)
    sgn = np.ones((HD, 1), dtype=np.float32)
    sgn[:64] = -1.0
    sc = np.float32(1.0 / np.sqrt(HD))
    qq = np.arange(512)[None, :]
    kk = np.arange(128)[:, None]
    m8 = np.concatenate(
        [(qq >= kk + d * 128).astype(np.float32) for d in range(4)]
        + [(qq <= kk + d * 128).astype(np.float32) for d in range(4)], axis=1)
    return {
        "cosq": (cos * sc).astype(np.float32),
        "sinq": (sin * sgn * sc).astype(np.float32),
        "cosk": cos.astype(np.float32),
        "sink": (sin * sgn).astype(np.float32),
        "h128": _hadamard(128),
        "h64x2": np.kron(np.eye(2, dtype=np.float32), _hadamard(64)),
        "onescol": np.ones((128, 1), dtype=np.float32),
        "onesrow": np.ones((1, 128), dtype=np.float32),
        "negrow": -np.ones((1, 128), dtype=np.float32),
        "cvec": np.tile(np.array([[EPS, -0.5, FW_SC, 1.0 / D]], dtype=np.float32), (128, 1)),
        "mask8": np.ascontiguousarray(m8),
    }


class Sem:
    def __init__(self, handle, name):
        self.h, self.name, self.n = handle, name, 0


class Prog:
    def __init__(self):
        self.ops = {e: [] for e in ENGINES}
        self.serial_sem = None
        self.serial_dma = False

    def op(self, eng, fn, waits=(), incs=(), tag=""):
        waits = list(waits)
        incs = list(incs)
        serial_followup = None
        if self.serial_sem is not None and fn is not None:
            so = self.serial_sem
            waits.append((so, so.n))
            if not incs:
                incs.append((so, 1))
            else:
                serial_followup = (eng, so)
        self.serial_dma = False
        rec = []
        for sem, amt in incs:
            sem.n += amt
            rec.append((sem, amt))
        self.ops[eng].append((fn, tuple(waits), tuple(rec), tag))
        if serial_followup is not None:
            eng2, so = serial_followup
            s0, _ = rec[0]
            self.ops[eng2].append((lambda e: e.nop(), ((s0, s0.n),), ((so, 1),), tag + ".serord"))
            so.n += 1

    def barrier(self, sbar, k):
        for e in ENGINES:
            self.op(e, lambda eng: eng.nop(), incs=[(sbar, 1)], tag=f"bar{k}.inc")
        for e in ENGINES:
            self.op(e, None, waits=[(sbar, 5 * k)], tag=f"bar{k}.wait")

    def check_deadlock(self):
        idx = {e: 0 for e in ENGINES}
        vals = {}
        progress = True
        while progress:
            progress = False
            for e in ENGINES:
                while idx[e] < len(self.ops[e]):
                    fn, waits, incs, tag = self.ops[e][idx[e]]
                    if any(vals.get(s, 0) < v for s, v in waits):
                        break
                    for s, amt in incs:
                        vals[s] = vals.get(s, 0) + amt
                    idx[e] += 1
                    progress = True
        stuck = {e: i for e, i in idx.items() if i < len(self.ops[e])}
        if stuck:
            lines = []
            for e, i in stuck.items():
                fn, waits, incs, tag = self.ops[e][i]
                wd = [(s.name, v, vals.get(s, 0)) for s, v in waits]
                lines.append(f"  {e}[{i}/{len(self.ops[e])}] tag={tag} waits={wd}")
            raise RuntimeError("DEADLOCK in emitted program:\n" + "\n".join(lines))

    def replay(self, name, eng):
        for fn, waits, incs, tag in self.ops[name]:
            for s, v in waits:
                eng.wait_ge(s.h, v)
            if fn is None:
                continue
            inst = fn(eng)
            for s, amt in incs:
                inst = inst.then_inc(s.h, amt)


class DmaPool:

    def __init__(self, P, eng, mksem, name, K=8):
        self.P, self.eng, self.K = P, eng, K
        self.sems = [mksem(f"{name}{i}") for i in range(K)]
        self.uses = [0] * K
        self.i = 0

    def dma(self, fn, waits=(), tag=""):
        j = self.i % self.K
        s = self.sems[j]
        w = list(waits)
        if self.uses[j]:
            w.append((s, 16 * self.uses[j]))
        self.P.serial_dma = True
        self.P.op(self.eng, fn, waits=w, incs=[(s, 16)], tag=tag)
        self.uses[j] += 1
        self.i += 1
        return [(s, 16 * self.uses[j])]

    def snap(self):
        return [(s, 16 * u) for s, u in zip(self.sems, self.uses) if u]


def build_nc(taps=False, serial=False):
    import contextlib
    nc = bass.Bass("TRN2", target_bir_lowering=False, dynamic_dma_scratch_size=16384)

    def din(name, shape):
        return nc.dram_tensor(name, list(shape), F32, kind="ExternalInput").ap()

    xT = din("xT", (D, S))
    wq, wk, wv, wo = din("wq", (D, D)), din("wk", (D, 512)), din("wv", (D, 512)), din("wo", (D, D))
    w1t = din("w1t", (16, 128, 16, 512))
    w3t = din("w3t", (16, 128, 16, 512))
    w2pd = din("w2p", (64, 128, 2048))
    cosqd, sinqd = din("cosq", (128, S)), din("sinq", (128, S))
    coskd, sinkd = din("cosk", (128, S)), din("sink", (128, S))
    h128d, h64d = din("h128", (128, 128)), din("h64x2", (128, 128))
    onescold, onesrowd, negrowd = din("onescol", (128, 1)), din("onesrow", (1, 128)), din("negrow", (1, 128))
    mask8d = din("mask8", (128, 4096))
    cvecd = din("cvec", (128, 4))

    yT = nc.dram_tensor("yT", [D, S], F32, kind="ExternalOutput").ap()
    ikind = "ExternalOutput" if taps else "Internal"
    h2d = nc.dram_tensor("h2d", [64, NTS, 128, TS], BF16, kind=ikind).ap()
    w2bd = nc.dram_tensor("w2bd", [8, 128, 64, 256], BF16, kind="Internal").ap()
    x1d = nc.dram_tensor("x1d", [NDC, 128, S], F32, kind=ikind).ap()
    if taps:
        tqfm = nc.dram_tensor("tqfm", [128, NH * S], BF16, kind="ExternalOutput").ap()
        tkfm = nc.dram_tensor("tkfm", [128, NKV * S], BF16, kind="ExternalOutput").ap()
        tvtm = nc.dram_tensor("tvtm", [128, 16 * 512], BF16, kind="ExternalOutput").ap()

    P = Prog()
    stack = contextlib.ExitStack()

    def sbuf(name, shape, dt):
        return stack.enter_context(nc.sbuf_tensor(name, list(shape), dt))

    def psum(name, shape, dt=F32):
        return stack.enter_context(nc.psum_tensor(name, list(shape), dt))

    def sem(name):
        return Sem(stack.enter_context(nc.semaphore(name)), name)

    with stack:
        A = sbuf("A", (128, 32768), BF16)
        B = sbuf("B", (128, 32768), BF16)
        C = sbuf("C", (128, 8192), BF16)
        Dv = sbuf("Dv", (128, 8192), BF16)
        W0 = sbuf("W0", (128, 8192), BF16)
        W1b = sbuf("W1b", (128, 8192), BF16)
        SCR = sbuf("SCR", (128, 3584), F32)

        AF32 = A[:].bitcast(F32)
        CF32 = C[:].bitcast(F32)
        DF32 = Dv[:].bitcast(F32)
        SCRb = SCR[:].bitcast(BF16)

        h64b = sbuf("h64b", (128, 128), BF16)
        onescolb = sbuf("onescolb", (128, 1), BF16)
        onescolf = sbuf("onescolf", (128, 1), F32)
        onesrow_s = sbuf("onesrow_s", (1, 128), F32)
        negrow_s = sbuf("negrow_s", (1, 128), F32)
        cvec_s = sbuf("cvec_s", (128, 4), F32)

        PSA = psum("PSA", (128, 1536), F32)
        PSB = psum("PSB", (128, 1536), F32)
        PTO = psum("PTO", (128, 1024), F32)

        h128b = A[:, 0:128]
        mrow1 = SCR[0:1, 2560:3072]
        vrow1 = SCR[0:1, 3072:3584]
        trow = SCR[0:1, 2048:2560]

        sBar = sem("sBar")
        sPre, sRpV, sRpA = sem("sPre"), sem("sRpV"), sem("sRpA")
        sQK, sExp, sMsk = sem("sQK"), sem("sExp"), sem("sMsk")
        sSum, sRcp, sBc2, sNrm, sPV, sOcp = (sem("sSum"), sem("sRcp"), sem("sBc2"),
                                             sem("sNrm"), sem("sPV"), sem("sOcp"))
        sWo, sZ, sZb, sSq, sSt = sem("sWo"), sem("sZ"), sem("sZb"), sem("sSq"), sem("sSt")
        sFinV, sFinA, sBc, sApl = sem("sFinV"), sem("sFinA"), sem("sBc"), sem("sApl")
        sUp, sSil, sGat, sH2e = sem("sUp"), sem("sSil"), sem("sGat"), sem("sH2e")
        sH64, sH64e, sDn, sDnE, sTw = sem("sH64"), sem("sH64e"), sem("sDn"), sem("sDnE"), sem("sTw")
        gp = DmaPool(P, "gpsimd", sem, "dG", 8)
        sy = DmaPool(P, "sync", sem, "dS", 8)
        sc = DmaPool(P, "scalar", sem, "dC", 8)
        if serial:
            P.serial_sem = sem("sOrd")

        sy.dma(lambda e: e.dma_start(cvec_s[:], cvecd), tag="ld.cvec")
        sy.dma(lambda e: e.dma_start(onesrow_s[:], onesrowd), tag="ld.onesrow")
        consts_sy = sy.dma(lambda e: e.dma_start(negrow_s[:], negrowd), tag="ld.negrow") + sy.snap()
        gp.dma(lambda e: e.dma_start(h64b[:], h64d), tag="ld.h64")
        gp.dma(lambda e: e.dma_start(onescolb[:], onescold), tag="ld.onescol")
        sy.dma(lambda e: e.dma_start(onescolf[:], onescold), tag="ld.onescolf")
        xv = xT.rearrange("(dc p) t -> p dc t", p=128)
        for i in range(4):
            gp.dma(lambda e, i=i: e.dma_start(A[:, i * 8192:(i + 1) * 8192], xv[:, 4 * i:4 * (i + 1), :]),
                   tag=f"ld.x{i}")

        t1f = PTO[:, 0:512]
        swpf = PSA[:, 512:1024]
        t2f = SCR[:, 2048:2560]
        cos_sl = lambda par: SCR[:, par * 1024: par * 1024 + 512]
        sin_sl = lambda par: SCR[:, par * 1024 + 512: par * 1024 + 1024]

        def pp(par):
            return (PSA if par == 0 else PSB)[:, 0:512]

        rope_done = []

        def emit_rope(par, dst, cs, sn, tag):
            mmw = (sPre, sPre.n)
            P.op("vector", lambda e, par=par, cs=cs: e.tensor_tensor(t1f, pp(par), cs, ALU.mult),
                 waits=[mmw], incs=[(sRpV, 1)], tag=f"{tag}.t1")
            swq = [(sRpV, sRpV.n)]
            P.op("scalar", lambda e, par=par: e.activation(
                swpf[0:64, :], (PSA if par == 0 else PSB)[64:128, 0:512], AF.Copy),
                waits=[mmw] + swq, incs=[(sRpA, 1)], tag=f"{tag}.sw0")
            P.op("scalar", lambda e, par=par: e.activation(
                swpf[64:128, :], (PSA if par == 0 else PSB)[0:64, 0:512], AF.Copy),
                incs=[(sRpA, 1)], tag=f"{tag}.sw1")
            P.op("vector", lambda e, sn=sn: e.tensor_tensor(t2f, swpf[:], sn, ALU.mult),
                 waits=[(sRpA, sRpA.n)], incs=[(sRpV, 1)], tag=f"{tag}.t2")
            P.op("vector", lambda e, dst=dst: e.tensor_tensor(dst, t2f, t1f, ALU.add),
                 incs=[(sRpV, 1)], tag=f"{tag}.add")
            rope_done.append(sRpV.n)

        wkv = wk.rearrange("(kc p) o -> p kc o", p=128)
        gp.dma(lambda e: e.dma_start(W0[:], wkv), tag="ld.wk")
        k_ld = gp.snap()
        nunit = 0
        tab_user = {0: 0, 1: 0}
        tab_ready = {0: [], 1: []}

        def load_tab(par, ts, cosd, sind, tag):
            t1 = sy.dma(lambda e, par=par, ts=ts, cosd=cosd: e.dma_start(
                cos_sl(par), cosd[:, ts * 512:(ts + 1) * 512]),
                waits=[(sRpV, tab_user[par])], tag=f"{tag}.cos")
            t2 = sy.dma(lambda e, par=par, ts=ts, sind=sind: e.dma_start(
                sin_sl(par), sind[:, ts * 512:(ts + 1) * 512]), tag=f"{tag}.sin")
            tab_ready[par] = t1 + t2

        for ts in range(NTS):
            tpar = ts % 2
            load_tab(tpar, ts, coskd, sinkd, f"ktab.{ts}")
            for g in range(NKV):
                par = nunit % 2
                w0 = k_ld + tab_ready[tpar]
                if nunit >= 2:
                    w0 = w0 + [(sRpV, rope_done[nunit - 2])]
                for kc in range(NDC):
                    P.op("tensor", lambda e, g=g, ts=ts, kc=kc, par=par: e.matmul(
                        pp(par), W0[:, kc * 512 + g * 128: kc * 512 + (g + 1) * 128],
                        A[:, kc * 2048 + ts * 512: kc * 2048 + (ts + 1) * 512],
                        start=(kc == 0), stop=(kc == NDC - 1)),
                        waits=(w0 if kc == 0 else ()),
                        incs=([(sPre, 1)] if kc == NDC - 1 else ()), tag=f"k.mm.{g}.{ts}")
                emit_rope(par, C[:, g * 2048 + ts * 512: g * 2048 + (ts + 1) * 512],
                          cos_sl(tpar), sin_sl(tpar), f"k.rope.{g}.{ts}")
                tab_user[tpar] = sRpV.n
                nunit += 1

        wvv = wv.rearrange("(kc p) o -> p kc o", p=128)
        gp.dma(lambda e: e.dma_start(W0[:], wvv), waits=[(sPre, sPre.n)], tag="ld.wv")
        v_ld = gp.snap()
        for tc in range(16):
            par = nunit % 2
            w0 = list(v_ld)
            if nunit >= 2:
                w0.append((sRpV, rope_done[nunit - 2]))
            for kc in range(NDC):
                P.op("tensor", lambda e, tc=tc, kc=kc, par=par: e.matmul(
                    pp(par), A[:, kc * 2048 + tc * 128: kc * 2048 + (tc + 1) * 128],
                    W0[:, kc * 512:(kc + 1) * 512],
                    start=(kc == 0), stop=(kc == NDC - 1)),
                    waits=(w0 if kc == 0 else ()),
                    incs=([(sPre, 1)] if kc == NDC - 1 else ()), tag=f"v.mm.{tc}")
            P.op("vector", lambda e, tc=tc, par=par: e.tensor_copy(
                Dv[:, tc * 512:(tc + 1) * 512], pp(par)),
                waits=[(sPre, sPre.n)], incs=[(sRpV, 1)], tag=f"v.ev.{tc}")
            rope_done.append(sRpV.n)
            nunit += 1

        w2b_st = w2bd.rearrange("dg (jp js) c dl -> jp js c dg dl", js=2)
        v_mm_all = sPre.n
        tw_in = lambda sl: W0[:, sl * 2048:(sl + 1) * 2048]
        tw_ev = lambda sl: W0[:, 4096 + sl * 2048: 4096 + (sl + 1) * 2048]
        tw_ps = lambda k: PSA[:, 1024:1536] if k == 0 else PSB[:, 1024:1536]
        tw_mm_done, tw_ev_done, tw_st_tok = {}, {}, {}

        def emit_tw(jp):
            sl = jp % 2
            lw = [(sPre, v_mm_all)] if jp < 2 else [(sTw, tw_mm_done[jp - 2])]
            ld = gp.dma(lambda e, jp=jp, sl=sl: e.dma_start(tw_in(sl), w2pd[jp]),
                        waits=lw, tag=f"twl.{jp}")
            for ch in range(4):
                gi = jp * 4 + ch
                mw = list(ld) if ch == 0 else []
                if gi >= 2:
                    mw.append(tw_ev_done[gi - 2])
                P.op("tensor", lambda e, sl=sl, ch=ch, k=ch % 2: e.matmul(
                    tw_ps(k), h64b[:], tw_in(sl)[:, ch * 512:(ch + 1) * 512],
                    start=True, stop=True),
                    waits=mw, incs=[(sTw, 1)], tag=f"twm.{jp}.{ch}")
                ew = [(sTw, sTw.n)]
                if ch < 2 and jp >= 2:
                    ew += tw_st_tok[jp - 2]
                if ch % 2 == 0:
                    P.op("vector", lambda e, sl=sl, ch=ch: e.tensor_copy(
                        tw_ev(sl)[:, ch * 512:(ch + 1) * 512], tw_ps(0)),
                        waits=ew, incs=[(sH64, 1)], tag=f"twe.{jp}.{ch}")
                    tw_ev_done[gi] = (sH64, sH64.n)
                else:
                    P.op("scalar", lambda e, sl=sl, ch=ch: e.activation(
                        tw_ev(sl)[:, ch * 512:(ch + 1) * 512], tw_ps(1), AF.Copy),
                        waits=ew, incs=[(sH64e, 1)], tag=f"twe.{jp}.{ch}")
                    tw_ev_done[gi] = (sH64e, sH64e.n)
            tw_mm_done[jp] = sTw.n
            tw_st_tok[jp] = sc.dma(lambda e, jp=jp, sl=sl: e.dma_start(w2b_st[jp], tw_ev(sl)),
                                   waits=[tw_ev_done[jp * 4 + 2], tw_ev_done[jp * 4 + 3]],
                                   tag=f"tws.{jp}")

        wqv = wq.rearrange("(kc p) o -> p kc o", p=128)
        q_mm_at_eighth = {}
        for ei in range(8):
            hb = ei % 2
            ldw = [] if ei < 2 else [(sPre, q_mm_at_eighth[ei - 2])]
            q_tok = gp.dma(lambda e, ei=ei, hb=hb: e.dma_start(
                W1b[:, hb * 4096:(hb + 1) * 4096], wqv[:, :, ei * 256:(ei + 1) * 256]),
                waits=ldw, tag=f"ld.wq{ei}")
            for ts in range(NTS):
                tpar = ts % 2
                load_tab(tpar, ts, cosqd, sinqd, f"qtab.{ei}.{ts}")
                for qc2 in range(2):
                    h = ei * 2 + qc2
                    par = nunit % 2
                    w0 = v_ld + q_tok + tab_ready[tpar]
                    if nunit >= 2:
                        w0 = w0 + [(sRpV, rope_done[nunit - 2])]
                    for kc in range(NDC):
                        P.op("tensor", lambda e, h=h, ts=ts, kc=kc, par=par, hb=hb, qc2=qc2: e.matmul(
                            pp(par),
                            W1b[:, hb * 4096 + kc * 256 + qc2 * 128: hb * 4096 + kc * 256 + (qc2 + 1) * 128],
                            A[:, kc * 2048 + ts * 512: kc * 2048 + (ts + 1) * 512],
                            start=(kc == 0), stop=(kc == NDC - 1)),
                            waits=(w0 if kc == 0 else ()),
                            incs=([(sPre, 1)] if kc == NDC - 1 else ()), tag=f"q.mm.{h}.{ts}")
                    emit_rope(par, B[:, h * 2048 + ts * 512: h * 2048 + (ts + 1) * 512],
                              cos_sl(tpar), sin_sl(tpar), f"q.rope.{h}.{ts}")
                    tab_user[tpar] = sRpV.n
                    emit_tw(ei * 8 + ts * 2 + qc2)
                    nunit += 1
            q_mm_at_eighth[ei] = sPre.n
        tw_mm_all = sTw.n
        tw_st_all = sc.snap()

        P.barrier(sBar, 1)
        mask_tok = gp.dma(lambda e: e.dma_start(SCRb[:, 0:4096], mask8d), tag="ld.mask8")
        if taps:
            sy.dma(lambda e: e.dma_start(tqfm, B[:]), tag="tap.q")
            sy.dma(lambda e: e.dma_start(tkfm, C[:]), tag="tap.k")
            sy.dma(lambda e: e.dma_start(tvtm, Dv[:]), tag="tap.v")

        attn_col = A[:, 24576:32768]
        pt_slot = lambda par, s: A[:, 20480 + par * 1536 + s * 512: 20480 + par * 1536 + (s + 1) * 512]
        srow_s = AF32[0:1, 11776:12288]
        mask_sl = lambda mid: SCRb[:, mid * 512:(mid + 1) * 512]

        def S_bk(par, gi):
            return (PSA if par == 0 else PSB)[:, (gi % 2) * 512:(gi % 2) * 512 + 512]

        def O_bk(par):
            return (PSA if par == 0 else PSB)[:, 1024:1536]

        sumrow = lambda par: PTO[0:1, par * 512:(par + 1) * 512]
        Rbc = lambda par: PTO[:, par * 512:(par + 1) * 512]
        statS_a = PSA[0:1, 1024:1536]
        statS2_a = PSB[0:1, 1024:1536]

        z1_t = lambda dc: AF32[:, dc * 512:(dc + 1) * 512]
        xres_t = lambda par: AF32[:, 8192 + par * 512: 8192 + (par + 1) * 512]
        zsq1_t = lambda par: AF32[:, 9216 + par * 512: 9216 + (par + 1) * 512]

        def qg_tiles(qg):
            if qg == 0:
                return [(kt, kt) for kt in range(4)]
            if qg == 1:
                return [(kt, None) for kt in range(4)] + [(kt, kt - 4) for kt in range(4, 8)]
            if qg == 2:
                return ([(kt, 4 + kt) for kt in range(4)] + [(kt, None) for kt in range(4, 8)]
                        + [(kt, kt - 8) for kt in range(8, 12)])
            return ([(kt, 4 + kt - 4) for kt in range(4, 8)] + [(kt, None) for kt in range(8, 12)]
                    + [(kt, kt - 12) for kt in range(12, 16)])

        unit = 0
        ptile_cnt = {0: 0, 1: 0}
        ptile_pv = {0: [], 1: []}
        bank_last_exp = {0: {0: 0, 1: 0}, 1: {0: 0, 1: 0}}
        bc_cnt, nrm_cnt, sum_stop, pv_stop, ocp_cnt = {}, {}, {}, {}, {}
        apl_gate = [0]
        fin_gate = [0]
        wo_all_prev = [0]
        z_all_prev = [0]
        first_mask = [True]
        pend_tail = []
        unit_tiles = {}

        def emit_sum_pv(u, t):
            par, h, g, nkt = u % 2, (u % 16), ((u % 16) // 4), len(unit_tiles[u])
            t_, gi, slot, kt, rdy = unit_tiles[u][t]
            sw = [rdy]
            if t == 0:
                if u >= 2:
                    sw.append((sNrm, nrm_cnt[u - 2]))
                if u % 16 in (0, 1) and apl_gate[0]:
                    sw.append((sApl, apl_gate[0]))
            P.op("tensor", lambda e, par=par, slot=slot, st=(t == 0), sp=(t == nkt - 1): e.matmul(
                sumrow(par), onescolb[:], pt_slot(par, slot), start=st, stop=sp),
                waits=sw, incs=[(sSum, 1)], tag=f"sm.{u}.{t}")
            if t == nkt - 1:
                sum_stop[u] = sSum.n
            pw = []
            if t == 0:
                if u >= 2:
                    pw.append((sOcp, ocp_cnt[u - 2]))
                if u % 16 in (0, 1) and fin_gate[0]:
                    pw.append((sFinV, fin_gate[0]))
            P.op("tensor", lambda e, par=par, g=g, kt=kt, slot=slot, st=(t == 0), sp=(t == nkt - 1): e.matmul(
                O_bk(par), Dv[:, kt * 512 + g * 128: kt * 512 + (g + 1) * 128],
                pt_slot(par, slot), start=st, stop=sp),
                waits=pw, incs=[(sPV, 1)], tag=f"pv.{u}.{t}")
            ptile_pv[par].append(sPV.n)
            if t == nkt - 1:
                pv_stop[u] = sPV.n

        def emit_tail():
            u, par, h = pend_tail.pop(0)
            cw = [(sPV, pv_stop[u])]
            if wo_all_prev[0]:
                cw.append((sWo, wo_all_prev[0]))
            P.op("scalar", lambda e, par=par, h=h: e.activation(
                attn_col[:, h * 512:(h + 1) * 512], O_bk(par), AF.Copy),
                waits=cw, incs=[(sOcp, 1)], tag=f"ocp.{u}")
            ocp_cnt[u] = sOcp.n
            rw = [(sSum, sum_stop[u])]
            if u >= 1:
                rw.append((sBc2, bc_cnt[u - 1]))
            P.op("vector", lambda e, par=par: e.reciprocal(srow_s, sumrow(par)),
                 waits=rw, incs=[(sRcp, 1)], tag=f"rcp.{u}")
            bw = [(sRcp, sRcp.n)]
            if u >= 2:
                bw.append((sNrm, nrm_cnt[u - 2]))
            if u % 16 in (0, 1) and apl_gate[0]:
                bw.append((sApl, apl_gate[0]))
            P.op("tensor", lambda e, par=par: e.matmul(
                Rbc(par), onesrow_s[:], srow_s, start=True, stop=True),
                waits=bw, incs=[(sBc2, 1)], tag=f"bc2.{u}")
            bc_cnt[u] = sBc2.n
            nw = [(sBc2, sBc2.n), (sOcp, ocp_cnt[u])]
            P.op("vector", lambda e, par=par, h=h: e.tensor_tensor(
                attn_col[:, h * 512:(h + 1) * 512],
                attn_col[:, h * 512:(h + 1) * 512], Rbc(par), ALU.mult),
                waits=nw, incs=[(sNrm, 1)], tag=f"nrm.{u}")
            nrm_cnt[u] = sNrm.n

        def emit_unit(h, qg):
            nonlocal unit
            u = unit
            par = u % 2
            g = h // 4
            tiles = qg_tiles(qg)
            nkt = len(tiles)
            unit_tiles[u] = []
            for t, (kt, mid) in enumerate(tiles):
                gi = ptile_cnt[par]
                qw = []
                le = bank_last_exp[par][gi % 2]
                if le:
                    qw.append((sExp, le))
                if u % 16 == 0 and t < 2 and z_all_prev[0]:
                    qw.append((sZ, z_all_prev[0]))
                P.op("tensor", lambda e, par=par, gi=gi, g=g, kt=kt, h=h, qg=qg: e.matmul(
                    S_bk(par, gi), C[:, g * 2048 + kt * 128: g * 2048 + (kt + 1) * 128],
                    B[:, h * 2048 + qg * 512: h * 2048 + (qg + 1) * 512],
                    start=True, stop=True),
                    waits=qw, incs=[(sQK, 1)], tag=f"qk.{u}.{t}")
                ew = [(sQK, sQK.n)]
                slot = gi % 3
                if gi >= 3:
                    ew.append((sPV, ptile_pv[par][gi - 3]))
                P.op("scalar", lambda e, par=par, gi=gi, slot=slot: e.activation(
                    pt_slot(par, slot), S_bk(par, gi), AF.Exp),
                    waits=ew, incs=[(sExp, 1)], tag=f"exp.{u}.{t}")
                bank_last_exp[par][gi % 2] = sExp.n
                if mid is not None:
                    mw = [(sExp, sExp.n)]
                    if first_mask[0]:
                        mw = mw + mask_tok
                        first_mask[0] = False
                    P.op("vector", lambda e, par=par, slot=slot, mid=mid: e.tensor_tensor(
                        pt_slot(par, slot), pt_slot(par, slot), mask_sl(mid), ALU.mult),
                        waits=mw, incs=[(sMsk, 1)], tag=f"msk.{u}.{t}")
                    rdy = (sMsk, sMsk.n)
                else:
                    rdy = (sExp, sExp.n)
                unit_tiles[u].append((t, gi, slot, kt, rdy))
                ptile_cnt[par] += 1
                if t >= 2:
                    emit_sum_pv(u, t - 2)
                if t == min(6, nkt - 2) and pend_tail:
                    emit_tail()
            emit_sum_pv(u, nkt - 2)
            emit_sum_pv(u, nkt - 1)
            pend_tail.append((u, par, h))
            unit += 1

        def emit_ln_finalize(statS, statS2, R_ps, B_ps, prev_bc, tag, mrow=None, vrow=None):
            if mrow is None:
                mrow, vrow = mrow1, vrow1
            P.op("vector", lambda e: e.tensor_scalar(mrow, statS, cvec_s[0:1, 3:4], None, ALU.mult),
                 waits=[(sSt, sSt.n), (sBc, prev_bc)], tag=f"{tag}.mean")
            P.op("vector", lambda e: e.tensor_scalar(trow, statS2, cvec_s[0:1, 3:4], None, ALU.mult),
                 tag=f"{tag}.e2")
            P.op("vector", lambda e: e.tensor_tensor(vrow, mrow, mrow, ALU.mult),
                 tag=f"{tag}.m2")
            P.op("vector", lambda e: e.tensor_tensor(trow, trow, vrow, ALU.subtract),
                 incs=[(sFinV, 1)], tag=f"{tag}.var")
            P.op("scalar", lambda e: e.activation(vrow, trow, AF.Ln, bias=cvec_s[0:1, 0:1]),
                 waits=[(sFinV, sFinV.n), (sBc, prev_bc)], tag=f"{tag}.lnf")
            P.op("scalar", lambda e: e.activation(vrow, vrow, AF.Exp, scale=cvec_s[0:1, 1:2]),
                 incs=[(sFinA, 1)], tag=f"{tag}.expf")
            P.op("tensor", lambda e: e.matmul(R_ps, onesrow_s[:], vrow, start=True, stop=True),
                 waits=[(sFinA, sFinA.n)], incs=[(sBc, 1)], tag=f"{tag}.bcR")
            P.op("vector", lambda e: e.tensor_tensor(mrow, mrow, vrow, ALU.mult),
                 waits=[(sFinA, sFinA.n)], incs=[(sFinV, 1)], tag=f"{tag}.brow")
            P.op("tensor", lambda e: e.matmul(B_ps, negrow_s[:], mrow, start=True, stop=True),
                 waits=[(sFinV, sFinV.n)], incs=[(sBc, 1)], tag=f"{tag}.bcB")

        wov = wo.rearrange("(kc p) o -> p kc o", p=128)
        wo_user_done = {0: None, 1: None}
        x1_pend = []
        x1_tok = {}

        def flush_x1():
            qg_, dc_, apl = x1_pend.pop(0)
            x1_tok[(qg_, dc_)] = sy.dma(lambda e, dc=dc_, qg0_=qg_ * QG: e.dma_start(
                x1d[dc, :, qg0_:qg0_ + QG], z1_t(dc)),
                waits=[(sApl, apl)], tag=f"x1w.{qg_}.{dc_}")
        R1 = PSA[:, 1024:1536]
        B1 = PSB[:, 1024:1536]
        R1a = PTO[:, 0:512]
        B1a = PTO[:, 512:1024]
        statS = PSB[0:1, 0:512]
        statS2 = PSB[0:1, 512:1024]

        for qg in range(NQG):
            qg0 = qg * QG
            for h in range(NH):
                emit_unit(h, qg)
            while pend_tail:
                emit_tail()
            nrm_all = sNrm.n
            z_done, st_done = {}, {}
            stat_pend = []

            def emit_stats(z_t, zsq_t, which):
                dc, zpar, zc, sqc = stat_pend.pop(0)
                P.op("tensor", lambda e, dc=dc, zpar=zpar: e.matmul(
                    statS_a, onescolf[:], z_t(dc),
                    start=(dc == 0), stop=(dc == NDC - 1)),
                    waits=[(sZ, zc)], tag=f"st1.{which}.{dc}")
                P.op("tensor", lambda e, dc=dc, zpar=zpar: e.matmul(
                    statS2_a, onescolf[:], zsq_t(zpar),
                    start=(dc == 0), stop=(dc == NDC - 1)),
                    waits=[(sSq, sqc)],
                    incs=([(sSt, 1)] if dc == NDC - 1 else ()), tag=f"st2.{which}.{dc}")
                st_done[dc] = sSt.n

            prev_bc = sBc.n
            for dcg in range(4):
                wpar = dcg % 2
                wobuf = W0 if wpar == 0 else W1b
                ldw = []
                if wo_user_done[wpar] is not None:
                    ldw.append((sWo, wo_user_done[wpar]))
                elif qg == 0 and dcg == 0:
                    ldw += [(sTw, tw_mm_all)] + tw_st_tok[62] + tw_st_tok[63]
                elif qg == 0 and dcg == 1:
                    ldw.append((sPre, sPre.n))
                wo_ld = gp.dma(lambda e, dcg=dcg, wobuf=wobuf: e.dma_start(
                    wobuf[:], wov[:, :, dcg * 512:(dcg + 1) * 512]),
                    waits=ldw, tag=f"ld.wo.{qg}.{dcg}")
                for _ in range(4):
                    if x1_pend:
                        flush_x1()
                for dc_l in range(4):
                    dc = dcg * 4 + dc_l
                    zpar = dc % 2
                    woPS = PSA[:, zpar * 512:(zpar + 1) * 512]
                    xw = [] if dc < 2 else [(sZ, z_done[dc - 2])]
                    xres_tok = sy.dma(lambda e, dc=dc, zpar=zpar, qg0=qg0: e.dma_start(
                        xres_t(zpar), xv[:, dc, qg0:qg0 + QG]),
                        waits=xw, tag=f"xres.{qg}.{dc}")
                    w0 = wo_ld
                    if dc >= 2:
                        w0 = w0 + [(sZ, z_done[dc - 2])]
                    for kc in range(NDC):
                        kw = list(w0) if kc == 0 else []
                        if dc == 0:
                            kw.append((sNrm, nrm_cnt[qg * 16 + kc]))
                        elif kc == 0:
                            kw.append((sNrm, nrm_all))
                        P.op("tensor", lambda e, kc=kc, dc_l=dc_l, wobuf=wobuf, woPS=woPS: e.matmul(
                            woPS, wobuf[:, kc * 512 + dc_l * 128: kc * 512 + (dc_l + 1) * 128],
                            attn_col[:, kc * 512:(kc + 1) * 512],
                            start=(kc == 0), stop=(kc == NDC - 1)),
                            waits=kw,
                            incs=([(sWo, 1)] if kc == NDC - 1 else ()), tag=f"wo.{qg}.{dc}")
                    zaw = [(sWo, sWo.n)] + xres_tok
                    if (qg - 1, dc) in x1_tok:
                        zaw += x1_tok[(qg - 1, dc)]
                    P.op("vector", lambda e, dc=dc, zpar=zpar, woPS=woPS: e.tensor_tensor(
                        z1_t(dc), woPS, xres_t(zpar), ALU.add),
                        waits=zaw, incs=[(sZ, 1)], tag=f"z1.{qg}.{dc}")
                    z_done[dc] = sZ.n
                    zw = [(sZ, sZ.n)] if dc < 2 else [(sZ, sZ.n), (sSt, st_done.get(dc - 2, 0))]
                    P.op("vector", lambda e, dc=dc, zpar=zpar: e.tensor_tensor(
                        zsq1_t(zpar), z1_t(dc), z1_t(dc), ALU.mult),
                        waits=zw, incs=[(sSq, 1)], tag=f"zsq.{qg}.{dc}")
                    stat_pend.append((dc, zpar, z_done[dc], sSq.n))
                    if len(stat_pend) >= 2:
                        emit_stats(z1_t, zsq1_t, 1)
                wo_user_done[wpar] = sWo.n
            while stat_pend:
                emit_stats(z1_t, zsq1_t, 1)
            emit_ln_finalize(statS_a, statS2_a, R1a, B1a, prev_bc, f"ln1.{qg}")
            for dc in range(NDC):
                aw = [(sBc, sBc.n)]
                P.op("vector", lambda e, dc=dc: e.tensor_tensor(
                    z1_t(dc), z1_t(dc), R1a, ALU.mult),
                    waits=aw, tag=f"ap1.{qg}.{dc}")
                P.op("vector", lambda e, dc=dc: e.tensor_tensor(
                    z1_t(dc), z1_t(dc), B1a, ALU.add),
                    incs=[(sApl, 1)], tag=f"ap2.{qg}.{dc}")
                x1_pend.append((qg, dc, sApl.n))
            apl_gate[0] = sApl.n
            fin_gate[0] = sFinV.n
            wo_all_prev[0] = sWo.n
            z_all_prev[0] = sZ.n
            if qg == NQG - 1:
                while x1_pend:
                    flush_x1()

        x1w_all = sy.snap()
        qk_all = sQK.n
        pv_all = sPV.n
        wo_all = sWo.n

        x1v = x1d.rearrange("dc p t -> p dc t")
        for i in range(4):
            gp.dma(lambda e, i=i: e.dma_start(
                B[:, i * 8192:(i + 1) * 8192], x1v[:, 4 * i:4 * (i + 1), :]),
                waits=x1w_all + [(sQK, qk_all)], tag=f"ld.x1bf{i}")

        P.barrier(sBar, 2)
        h128_tok = gp.dma(lambda e: e.dma_start(h128b, h128d), tag="ld.h128")


        sg_t = lambda par: SCR[:, par * 512:(par + 1) * 512]
        hbf_t = lambda par: SCRb[:, 2048 + par * 512: 2048 + (par + 1) * 512]
        h2st_t = lambda par: SCRb[:, 3072 + par * 512: 3072 + (par + 1) * 512]

        def up_ps(par, k):
            return (PSA if par == 0 else PSB)[:, k * 512:(k + 1) * 512]

        up_idx = 0
        up_mm_done = {}
        sil_done, gat_done, h2e_done, h2w_done = {}, {}, {}, {}
        pend_up = []

        def up_tail(i):
            c, ts, par = pend_up.pop(0)
            hw = [(sGat, gat_done[i])]
            if i == 0:
                hw += h128_tok
            if i >= 2:
                hw.append((sH2e, h2e_done[i - 2]))
            P.op("tensor", lambda e, par=par: e.matmul(
                up_ps(par, 2), h128b, hbf_t(par), start=True, stop=True),
                waits=hw, incs=[(sUp, 1)], tag=f"h128.{i}")
            ew = [(sUp, sUp.n)]
            if i >= 2:
                ew += h2w_done[i - 2]
            P.op("scalar", lambda e, par=par: e.activation(
                h2st_t(par), up_ps(par, 2), AF.Copy, scale=cvec_s[:, 2:3]),
                waits=ew, incs=[(sH2e, 1)], tag=f"h2e.{i}")
            h2e_done[i] = sH2e.n
            h2w_done[i] = sy.dma(lambda e, c=c, ts=ts, par=par: e.dma_start(h2d[c, ts], h2st_t(par)),
                                 waits=[(sH2e, sH2e.n)], tag=f"h2w.{i}")

        for cg in range(16):
            wpar = cg % 2
            w1buf = C if wpar == 0 else W0
            w3buf = Dv if wpar == 0 else W1b
            ldw = []
            if cg >= 2:
                ldw.append((sUp, up_mm_done[(cg - 2) * 16 + 15]))
            else:
                ldw += [(sQK, qk_all), (sPV, pv_all), (sWo, wo_all)]
            gp.dma(lambda e, cg=cg, w1buf=w1buf: e.dma_start(w1buf[:], w1t[cg]),
                waits=ldw, tag=f"ld.w1.{cg}")
            gp.dma(lambda e, cg=cg, w3buf=w3buf: e.dma_start(w3buf[:], w3t[cg]), tag=f"ld.w3.{cg}")
            ld_done = gp.snap()
            for cl in range(4):
                c = cg * 4 + cl
                for ts in range(NTS):
                    i = up_idx
                    par = i % 2
                    w0 = list(ld_done)
                    if i >= 2:
                        w0.append((sSil, sil_done[i - 2]))
                    for dc in range(NDC):
                        P.op("tensor", lambda e, dc=dc, cl=cl, ts=ts, par=par, w1buf=w1buf: e.matmul(
                            up_ps(par, 0), w1buf[:, dc * 512 + cl * 128: dc * 512 + (cl + 1) * 128],
                            B[:, dc * 2048 + ts * 512: dc * 2048 + (ts + 1) * 512],
                            start=(dc == 0), stop=(dc == NDC - 1)),
                            waits=(w0 if dc == 0 else ()),
                            incs=([(sUp, 1)] if dc == NDC - 1 else ()), tag=f"upA.{i}")
                    w0b = [(sGat, gat_done[i - 2])] if i >= 2 else []
                    for dc in range(NDC):
                        P.op("tensor", lambda e, dc=dc, cl=cl, ts=ts, par=par, w3buf=w3buf: e.matmul(
                            up_ps(par, 1), w3buf[:, dc * 512 + cl * 128: dc * 512 + (cl + 1) * 128],
                            B[:, dc * 2048 + ts * 512: dc * 2048 + (ts + 1) * 512],
                            start=(dc == 0), stop=(dc == NDC - 1)),
                            waits=(w0b if dc == 0 else ()),
                            incs=([(sUp, 1)] if dc == NDC - 1 else ()), tag=f"upB.{i}")
                    up_mm_done[i] = sUp.n
                    P.op("scalar", lambda e, par=par: e.activation(sg_t(par), up_ps(par, 0), AF.Silu),
                         waits=[(sUp, sUp.n)], incs=[(sSil, 1)], tag=f"sil.{i}")
                    sil_done[i] = sSil.n
                    P.op("vector", lambda e, par=par: e.tensor_tensor(
                        hbf_t(par), sg_t(par), up_ps(par, 1), ALU.mult),
                        waits=[(sSil, sSil.n)], incs=[(sGat, 1)], tag=f"gat.{i}")
                    gat_done[i] = sGat.n
                    pend_up.append((c, ts, par))
                    if i >= 1:
                        up_tail(i - 1)
                    up_idx += 1
        up_tail(up_idx - 1)
        up_all = sUp.n
        h2w_all = sy.snap()

        h2rd = h2d.rearrange("c s p t -> s p c t")
        x1r_t = lambda par: SCR[:, par * 512:(par + 1) * 512]
        zsq2_t = lambda par: SCR[:, 1024 + par * 512: 1024 + (par + 1) * 512]

        def z2_t(dc):
            return CF32[:, dc * 512:(dc + 1) * 512] if dc < 8 else DF32[:, (dc - 8) * 512:(dc - 7) * 512]

        yw = []
        y_pend = []
        y_tok = {}

        def flush_y():
            ts_, dc_, apl = y_pend.pop(0)
            y_tok[(ts_, dc_)] = sc.dma(lambda e, dc=dc_, ts=ts_: e.dma_start(
                yT[dc * 128:(dc + 1) * 128, ts * 512:(ts + 1) * 512], z2_t(dc)),
                waits=[(sApl, apl)], tag=f"yw.{ts_}.{dc_}")
            yw.append(y_tok[(ts_, dc_)])

        dn_done, dne_done = {}, {}
        st2_done = {}
        dn_mm_at_ts = {}
        gat_all = sGat.n
        for ts in range(NTS):
            lw = list(h2w_all)
            if ts == 0:
                lw += [(sApl, apl_gate[0])] + x1w_all
            else:
                lw.append((sDn, dn_mm_at_ts[ts - 1]))
            Av = A[:].rearrange("p (c t) -> p c t", t=512)
            for i8 in range(4):
                gp.dma(lambda e, ts=ts, i8=i8: e.dma_start(
                    Av[:, i8 * 16:(i8 + 1) * 16, :],
                    h2rd[ts, :, i8 * 16:(i8 + 1) * 16, :]),
                    waits=(lw if i8 == 0 else ()), tag=f"h2l.{ts}.{i8}")
            h3l_done = gp.snap()
            prev_bc2 = sBc.n
            stat_pend2 = []
            st_done2 = {}

            def emit_stats2(ts=ts):
                dc, zpar, zc, sqc = stat_pend2.pop(0)
                P.op("tensor", lambda e, dc=dc, zpar=zpar: e.matmul(
                    statS, onescolf[:], z2_t(dc), start=(dc == 0), stop=(dc == NDC - 1)),
                    waits=[(sDnE, zc)], tag=f"st1b.{ts}.{dc}")
                P.op("tensor", lambda e, dc=dc, zpar=zpar: e.matmul(
                    statS2, onescolf[:], zsq2_t(zpar), start=(dc == 0), stop=(dc == NDC - 1)),
                    waits=[(sSq, sqc)],
                    incs=([(sSt, 1)] if dc == NDC - 1 else ()), tag=f"st2b.{ts}.{dc}")
                st2_done[(ts, dc)] = sSt.n

            for dg in range(8):
                wpar = dg % 2
                ldw = []
                if ts == 0 and dg < 2:
                    ldw = [(sUp, up_all)] + tw_st_all
                else:
                    key = (ts, dg - 2) if dg >= 2 else (ts - 1, dg + 6)
                    ldw.append((sDn, dn_done[key]))
                gp.dma(lambda e, dg=dg, wpar=wpar: e.dma_start(
                    B[:, wpar * 16384:(wpar + 1) * 16384], w2bd[dg]),
                    waits=ldw, tag=f"w2l.{ts}.{dg}")
                w2_ld = gp.snap()
                for _ in range(2):
                    if y_pend:
                        flush_y()
                for dc2 in range(2):
                    dc = dg * 2 + dc2
                    dps = PSA[:, dc2 * 512:(dc2 + 1) * 512]
                    zpar = dc % 2
                    xw = []
                    if ts > 0 or dc >= 1:
                        xw.append((sDnE, dne_done[(ts, dc - 1) if dc >= 1 else (ts - 1, 15)]))
                    else:
                        xw.append((sGat, gat_all))
                    x1r_tok = sy.dma(lambda e, dc=dc, ts=ts, zpar=zpar: e.dma_start(
                        x1r_t(zpar), x1d[dc, :, ts * 512:(ts + 1) * 512]),
                        waits=xw + (x1w_all if ts == 0 and dc == 0 else []), tag=f"x1r.{ts}.{dc}")
                    mw = w2_ld + h3l_done
                    if ts > 0 or dc >= 2:
                        mw.append((sDnE, dne_done[(ts, dc - 2) if dc >= 2 else (ts - 1, dc + 14)]))
                    for cc in range(64):
                        P.op("tensor", lambda e, cc=cc, dc2=dc2, dps=dps, wpar=wpar: e.matmul(
                            dps, B[:, wpar * 16384 + cc * 256 + dc2 * 128: wpar * 16384 + cc * 256 + (dc2 + 1) * 128],
                            A[:, cc * 512:(cc + 1) * 512],
                            start=(cc == 0), stop=(cc == 63)),
                            waits=(mw if cc == 0 else ()),
                            incs=([(sDn, 1)] if cc == 63 else ()), tag=f"dn.{ts}.{dc}")
                    z2w = [(sDn, sDn.n)] + x1r_tok
                    if (ts - 1, dc) in y_tok:
                        z2w += y_tok[(ts - 1, dc)]
                    P.op("vector", lambda e, dc=dc, dps=dps, zpar=zpar: e.tensor_tensor(
                        z2_t(dc), dps, x1r_t(zpar), ALU.add),
                        waits=z2w, incs=[(sDnE, 1)], tag=f"z2.{ts}.{dc}")
                    dne_done[(ts, dc)] = sDnE.n
                    zw = [(sDnE, sDnE.n)]
                    if ts > 0 or dc >= 2:
                        zw.append((sSt, st2_done.get((ts, dc - 2) if dc >= 2 else (ts - 1, dc + 14), 0)))
                    P.op("vector", lambda e, dc=dc, zpar=zpar: e.tensor_tensor(
                        zsq2_t(zpar), z2_t(dc), z2_t(dc), ALU.mult),
                        waits=zw, incs=[(sSq, 1)], tag=f"zsq2.{ts}.{dc}")
                    stat_pend2.append((dc, zpar, dne_done[(ts, dc)], sSq.n))
                    if len(stat_pend2) >= 2:
                        emit_stats2()
                dn_done[(ts, dg)] = sDn.n
            while stat_pend2:
                emit_stats2()
            dn_mm_at_ts[ts] = sDn.n
            emit_ln_finalize(statS, statS2, R1, B1, prev_bc2, f"ln2.{ts}")
            for dc in range(NDC):
                aw = [(sBc, sBc.n)]
                P.op("vector", lambda e, dc=dc: e.tensor_tensor(
                    z2_t(dc), z2_t(dc), R1, ALU.mult),
                    waits=aw, tag=f"apy1.{ts}.{dc}")
                P.op("vector", lambda e, dc=dc: e.tensor_tensor(
                    z2_t(dc), z2_t(dc), B1, ALU.add),
                    incs=[(sApl, 1)], tag=f"apy2.{ts}.{dc}")
                y_pend.append((ts, dc, sApl.n))
            if ts == NTS - 1:
                while y_pend:
                    flush_y()

        P.op("sync", None, waits=sy.snap(), tag="final.sy")
        P.op("gpsimd", None, waits=gp.snap(), tag="final.gp")
        P.op("scalar", None, waits=sc.snap(), tag="final.sc")

        P.check_deadlock()
        nops = {e: len(P.ops[e]) for e in ENGINES}

        with nc.Block() as block:
            @block.tensor
            def _(eng):
                P.replay("tensor", eng)

            @block.vector
            def _(eng):
                P.replay("vector", eng)

            @block.scalar
            def _(eng):
                P.replay("scalar", eng)

            @block.sync
            def _(eng):
                P.replay("sync", eng)

            @block.gpsimd
            def _(eng):
                P.replay("gpsimd", eng)

    return nc, nops


_CACHE = {}


def _get_nc(taps=False):
    key = bool(taps)
    if key not in _CACHE:
        _CACHE[key] = build_nc(taps=taps)
    return _CACHE[key]


def make_in_maps(inputs):
    x = np.asarray(inputs["L0_attn_hidden_states"], dtype=np.float32)
    consts = host_constants()
    shared = {
        "wq": np.ascontiguousarray(inputs["L0_attn_w_q"], dtype=np.float32),
        "wk": np.ascontiguousarray(inputs["L0_attn_w_k"], dtype=np.float32),
        "wv": np.ascontiguousarray(inputs["L0_attn_w_v"], dtype=np.float32),
        "wo": np.ascontiguousarray(inputs["L0_attn_w_o"], dtype=np.float32),
        "w1t": np.ascontiguousarray(
            np.asarray(inputs["L0_ffn_W1"], dtype=np.float32).T.reshape(16, 128, 16, 512)
            .transpose(3, 1, 0, 2).reshape(16, 128, 16, 512)) if False else
            np.ascontiguousarray(np.asarray(inputs["L0_ffn_W1"], dtype=np.float32).T
                                 .reshape(16, 128, 16, 512).transpose(2, 1, 0, 3)),
        "w3t": np.ascontiguousarray(np.asarray(inputs["L0_ffn_W3"], dtype=np.float32).T
                                    .reshape(16, 128, 16, 512).transpose(2, 1, 0, 3)),
        "w2p": np.ascontiguousarray(np.asarray(inputs["L0_ffn_W2"], dtype=np.float32).T
                                    .reshape(64, 64, 2, 2048).transpose(1, 2, 0, 3)
                                    .reshape(64, 128, 2048)),
    }
    shared.update(consts)
    in_maps = []
    for i in range(8):
        m = dict(shared)
        m["xT"] = np.ascontiguousarray(x[i].T)
        in_maps.append(m)
    return in_maps


def kernel(**inputs):
    nc, _ = _get_nc(taps=False)
    in_maps = make_in_maps(inputs)
    res = run_bass_kernel_spmd(nc, in_maps, core_ids=list(range(8)))
    out = np.empty((8, S, D), dtype=np.float32)
    for i in range(8):
        out[i] = res.results[i]["yT"].T
    return out


# version 21
# speedup vs baseline: 1.5849x; 1.0615x over previous
import numpy as np
import ml_dtypes
import concourse.bass as bass
import concourse.mybir as mybir
from concourse.bass_utils import run_bass_kernel_spmd

F32 = mybir.dt.float32
BF16 = mybir.dt.bfloat16
AF = mybir.ActivationFunctionType
ALU = mybir.AluOpType
AX = mybir.AxisListType

S, D, NH, NKV, HD, WIN, HF = 2048, 2048, 16, 4, 128, 1024, 8192
NDC = D // 128
NTS, TS = 4, 512
NQT = S // 128
QG, NQG = 512, 4
EPS = 1e-5
FW_SC = float(1.0 / np.sqrt(HF))
NEG = -1.0e30
ENGINES = ("tensor", "vector", "scalar", "sync", "gpsimd")


def _hadamard(n):
    h = np.array([[1.0]], dtype=np.float32)
    while h.shape[0] < n:
        h = np.block([[h, h], [h, -h]])
    return h.astype(np.float32)


def host_constants():
    t = np.arange(S, dtype=np.float32)
    inv = 1.0 / (10000.0 ** (np.arange(0, HD, 2, dtype=np.float32) / HD))
    fr = np.outer(t, inv)
    emb = np.concatenate([fr, fr], axis=-1)
    cos = np.ascontiguousarray(np.cos(emb).T)
    sin = np.ascontiguousarray(np.sin(emb).T)
    sgn = np.ones((HD, 1), dtype=np.float32)
    sgn[:64] = -1.0
    sc = np.float32(1.0 / np.sqrt(HD))
    qq = np.arange(512)[None, :]
    kk = np.arange(128)[:, None]
    m8 = np.concatenate(
        [(qq >= kk + d * 128).astype(np.float32) for d in range(4)]
        + [(qq <= kk + d * 128).astype(np.float32) for d in range(4)], axis=1)
    return {
        "cosq": (cos * sc).astype(np.float32),
        "sinq": (sin * sgn * sc).astype(np.float32),
        "cosk": cos.astype(np.float32),
        "sink": (sin * sgn).astype(np.float32),
        "h128": _hadamard(128),
        "h64x2": np.kron(np.eye(2, dtype=np.float32), _hadamard(64)),
        "onescol": np.ones((128, 1), dtype=np.float32),
        "onesrow": np.ones((1, 128), dtype=np.float32),
        "negrow": -np.ones((1, 128), dtype=np.float32),
        "cvec": np.tile(np.array([[EPS, -0.5, FW_SC, 1.0 / D]], dtype=np.float32), (128, 1)),
        "mask8": np.ascontiguousarray(m8),
    }


class Sem:
    def __init__(self, handle, name):
        self.h, self.name, self.n = handle, name, 0


class Prog:
    def __init__(self):
        self.ops = {e: [] for e in ENGINES}
        self.serial_sem = None
        self.serial_dma = False

    def op(self, eng, fn, waits=(), incs=(), tag=""):
        waits = list(waits)
        incs = list(incs)
        serial_followup = None
        if self.serial_sem is not None and fn is not None:
            so = self.serial_sem
            waits.append((so, so.n))
            if not incs:
                incs.append((so, 1))
            else:
                serial_followup = (eng, so)
        self.serial_dma = False
        rec = []
        for sem, amt in incs:
            sem.n += amt
            rec.append((sem, amt))
        self.ops[eng].append((fn, tuple(waits), tuple(rec), tag))
        if serial_followup is not None:
            eng2, so = serial_followup
            s0, _ = rec[0]
            self.ops[eng2].append((lambda e: e.nop(), ((s0, s0.n),), ((so, 1),), tag + ".serord"))
            so.n += 1

    def barrier(self, sbar, k):
        for e in ENGINES:
            self.op(e, lambda eng: eng.nop(), incs=[(sbar, 1)], tag=f"bar{k}.inc")
        for e in ENGINES:
            self.op(e, None, waits=[(sbar, 5 * k)], tag=f"bar{k}.wait")

    def check_deadlock(self):
        idx = {e: 0 for e in ENGINES}
        vals = {}
        progress = True
        while progress:
            progress = False
            for e in ENGINES:
                while idx[e] < len(self.ops[e]):
                    fn, waits, incs, tag = self.ops[e][idx[e]]
                    if any(vals.get(s, 0) < v for s, v in waits):
                        break
                    for s, amt in incs:
                        vals[s] = vals.get(s, 0) + amt
                    idx[e] += 1
                    progress = True
        stuck = {e: i for e, i in idx.items() if i < len(self.ops[e])}
        if stuck:
            lines = []
            for e, i in stuck.items():
                fn, waits, incs, tag = self.ops[e][i]
                wd = [(s.name, v, vals.get(s, 0)) for s, v in waits]
                lines.append(f"  {e}[{i}/{len(self.ops[e])}] tag={tag} waits={wd}")
            raise RuntimeError("DEADLOCK in emitted program:\n" + "\n".join(lines))

    def replay(self, name, eng):
        for fn, waits, incs, tag in self.ops[name]:
            for s, v in waits:
                eng.wait_ge(s.h, v)
            if fn is None:
                continue
            inst = fn(eng)
            for s, amt in incs:
                inst = inst.then_inc(s.h, amt)


class DmaPool:

    def __init__(self, P, eng, mksem, name, K=8):
        self.P, self.eng, self.K = P, eng, K
        self.sems = [mksem(f"{name}{i}") for i in range(K)]
        self.uses = [0] * K
        self.i = 0

    def dma(self, fn, waits=(), tag=""):
        j = self.i % self.K
        s = self.sems[j]
        w = list(waits)
        if self.uses[j]:
            w.append((s, 16 * self.uses[j]))
        self.P.serial_dma = True
        self.P.op(self.eng, fn, waits=w, incs=[(s, 16)], tag=tag)
        self.uses[j] += 1
        self.i += 1
        return [(s, 16 * self.uses[j])]

    def snap(self):
        return [(s, 16 * u) for s, u in zip(self.sems, self.uses) if u]


def build_nc(taps=False, serial=False):
    import contextlib
    nc = bass.Bass("TRN2", target_bir_lowering=False, dynamic_dma_scratch_size=16384)

    def din(name, shape):
        return nc.dram_tensor(name, list(shape), F32, kind="ExternalInput").ap()

    xT = din("xT", (D, S))
    wq, wk, wv, wo = din("wq", (D, D)), din("wk", (D, 512)), din("wv", (D, 512)), din("wo", (D, D))
    w1t = din("w1t", (16, 128, 16, 512))
    w3t = din("w3t", (16, 128, 16, 512))
    w2pd = din("w2p", (64, 128, 2048))
    cosqd, sinqd = din("cosq", (128, S)), din("sinq", (128, S))
    coskd, sinkd = din("cosk", (128, S)), din("sink", (128, S))
    h128d, h64d = din("h128", (128, 128)), din("h64x2", (128, 128))
    onescold, onesrowd, negrowd = din("onescol", (128, 1)), din("onesrow", (1, 128)), din("negrow", (1, 128))
    mask8d = din("mask8", (128, 4096))
    cvecd = din("cvec", (128, 4))

    yT = nc.dram_tensor("yT", [D, S], F32, kind="ExternalOutput").ap()
    ikind = "ExternalOutput" if taps else "Internal"
    h2d = nc.dram_tensor("h2d", [64, NTS, 128, TS], BF16, kind=ikind).ap()
    w2bd = nc.dram_tensor("w2bd", [8, 128, 64, 256], BF16, kind="Internal").ap()
    x1d = nc.dram_tensor("x1d", [NDC, 128, S], F32, kind=ikind).ap()
    if taps:
        tqfm = nc.dram_tensor("tqfm", [128, NH * S], BF16, kind="ExternalOutput").ap()
        tkfm = nc.dram_tensor("tkfm", [128, NKV * S], BF16, kind="ExternalOutput").ap()
        tvtm = nc.dram_tensor("tvtm", [128, 16 * 512], BF16, kind="ExternalOutput").ap()

    P = Prog()
    stack = contextlib.ExitStack()

    def sbuf(name, shape, dt):
        return stack.enter_context(nc.sbuf_tensor(name, list(shape), dt))

    def psum(name, shape, dt=F32):
        return stack.enter_context(nc.psum_tensor(name, list(shape), dt))

    def sem(name):
        return Sem(stack.enter_context(nc.semaphore(name)), name)

    with stack:
        A = sbuf("A", (128, 32768), BF16)
        B = sbuf("B", (128, 32768), BF16)
        C = sbuf("C", (128, 8192), BF16)
        Dv = sbuf("Dv", (128, 8192), BF16)
        W0 = sbuf("W0", (128, 8192), BF16)
        W1b = sbuf("W1b", (128, 8192), BF16)
        SCR = sbuf("SCR", (128, 3584), F32)

        AF32 = A[:].bitcast(F32)
        CF32 = C[:].bitcast(F32)
        DF32 = Dv[:].bitcast(F32)
        SCRb = SCR[:].bitcast(BF16)

        h64b = sbuf("h64b", (128, 128), BF16)
        onescolb = sbuf("onescolb", (128, 1), BF16)
        onescolf = sbuf("onescolf", (128, 1), F32)
        onesrow_s = sbuf("onesrow_s", (1, 128), F32)
        negrow_s = sbuf("negrow_s", (1, 128), F32)
        cvec_s = sbuf("cvec_s", (128, 4), F32)

        PSA = psum("PSA", (128, 1536), F32)
        PSB = psum("PSB", (128, 1536), F32)
        PTO = psum("PTO", (128, 1024), F32)

        h128b = A[:, 0:128]
        mrow1 = SCR[0:1, 2560:3072]
        vrow1 = SCR[0:1, 3072:3584]
        trow = SCR[0:1, 2048:2560]

        sBar = sem("sBar")
        sPre, sRpV, sRpA = sem("sPre"), sem("sRpV"), sem("sRpA")
        sQK, sExp, sMsk = sem("sQK"), sem("sExp"), sem("sMsk")
        sSum, sRcp, sBc2, sNrm, sPV, sOcp = (sem("sSum"), sem("sRcp"), sem("sBc2"),
                                             sem("sNrm"), sem("sPV"), sem("sOcp"))
        sWo, sZ, sZb, sSq, sSt = sem("sWo"), sem("sZ"), sem("sZb"), sem("sSq"), sem("sSt")
        sFinV, sFinA, sBc, sApl = sem("sFinV"), sem("sFinA"), sem("sBc"), sem("sApl")
        sUp, sSil, sGat, sH2e = sem("sUp"), sem("sSil"), sem("sGat"), sem("sH2e")
        sH64, sH64e, sDn, sDnE, sTw = sem("sH64"), sem("sH64e"), sem("sDn"), sem("sDnE"), sem("sTw")
        gp = DmaPool(P, "gpsimd", sem, "dG", 8)
        sy = DmaPool(P, "sync", sem, "dS", 8)
        sc = DmaPool(P, "scalar", sem, "dC", 8)
        if serial:
            P.serial_sem = sem("sOrd")

        sy.dma(lambda e: e.dma_start(cvec_s[:], cvecd), tag="ld.cvec")
        sy.dma(lambda e: e.dma_start(onesrow_s[:], onesrowd), tag="ld.onesrow")
        consts_sy = sy.dma(lambda e: e.dma_start(negrow_s[:], negrowd), tag="ld.negrow") + sy.snap()
        gp.dma(lambda e: e.dma_start(h64b[:], h64d), tag="ld.h64")
        gp.dma(lambda e: e.dma_start(onescolb[:], onescold), tag="ld.onescol")
        sy.dma(lambda e: e.dma_start(onescolf[:], onescold), tag="ld.onescolf")
        xv = xT.rearrange("(dc p) t -> p dc t", p=128)
        for i in range(4):
            gp.dma(lambda e, i=i: e.dma_start(A[:, i * 8192:(i + 1) * 8192], xv[:, 4 * i:4 * (i + 1), :]),
                   tag=f"ld.x{i}")

        t1f = PTO[:, 0:512]
        swpf = PSA[:, 512:1024]
        t2f = SCR[:, 2048:2560]
        cos_sl = lambda par: SCR[:, par * 1024: par * 1024 + 512]
        sin_sl = lambda par: SCR[:, par * 1024 + 512: par * 1024 + 1024]

        def pp(par):
            return (PSA if par == 0 else PSB)[:, 0:512]

        rope_done = []

        def emit_rope(par, dst, cs, sn, tag):
            mmw = (sPre, sPre.n)
            P.op("vector", lambda e, par=par, cs=cs: e.tensor_tensor(t1f, pp(par), cs, ALU.mult),
                 waits=[mmw], incs=[(sRpV, 1)], tag=f"{tag}.t1")
            swq = [(sRpV, sRpV.n)]
            P.op("scalar", lambda e, par=par: e.activation(
                swpf[0:64, :], (PSA if par == 0 else PSB)[64:128, 0:512], AF.Copy),
                waits=[mmw] + swq, incs=[(sRpA, 1)], tag=f"{tag}.sw0")
            P.op("scalar", lambda e, par=par: e.activation(
                swpf[64:128, :], (PSA if par == 0 else PSB)[0:64, 0:512], AF.Copy),
                incs=[(sRpA, 1)], tag=f"{tag}.sw1")
            P.op("vector", lambda e, sn=sn: e.tensor_tensor(t2f, swpf[:], sn, ALU.mult),
                 waits=[(sRpA, sRpA.n)], incs=[(sRpV, 1)], tag=f"{tag}.t2")
            P.op("vector", lambda e, dst=dst: e.tensor_tensor(dst, t2f, t1f, ALU.add),
                 incs=[(sRpV, 1)], tag=f"{tag}.add")
            rope_done.append(sRpV.n)

        wkv = wk.rearrange("(kc p) o -> p kc o", p=128)
        gp.dma(lambda e: e.dma_start(W0[:], wkv), tag="ld.wk")
        k_ld = gp.snap()
        nunit = 0
        tab_user = {0: 0, 1: 0}
        tab_ready = {0: [], 1: []}

        def load_tab(par, ts, cosd, sind, tag):
            t1 = sy.dma(lambda e, par=par, ts=ts, cosd=cosd: e.dma_start(
                cos_sl(par), cosd[:, ts * 512:(ts + 1) * 512]),
                waits=[(sRpV, tab_user[par])], tag=f"{tag}.cos")
            t2 = sy.dma(lambda e, par=par, ts=ts, sind=sind: e.dma_start(
                sin_sl(par), sind[:, ts * 512:(ts + 1) * 512]), tag=f"{tag}.sin")
            tab_ready[par] = t1 + t2

        for ts in range(NTS):
            tpar = ts % 2
            load_tab(tpar, ts, coskd, sinkd, f"ktab.{ts}")
            for g in range(NKV):
                par = nunit % 2
                w0 = k_ld + tab_ready[tpar]
                if nunit >= 2:
                    w0 = w0 + [(sRpV, rope_done[nunit - 2])]
                for kc in range(NDC):
                    P.op("tensor", lambda e, g=g, ts=ts, kc=kc, par=par: e.matmul(
                        pp(par), W0[:, kc * 512 + g * 128: kc * 512 + (g + 1) * 128],
                        A[:, kc * 2048 + ts * 512: kc * 2048 + (ts + 1) * 512],
                        start=(kc == 0), stop=(kc == NDC - 1)),
                        waits=(w0 if kc == 0 else ()),
                        incs=([(sPre, 1)] if kc == NDC - 1 else ()), tag=f"k.mm.{g}.{ts}")
                emit_rope(par, C[:, g * 2048 + ts * 512: g * 2048 + (ts + 1) * 512],
                          cos_sl(tpar), sin_sl(tpar), f"k.rope.{g}.{ts}")
                tab_user[tpar] = sRpV.n
                nunit += 1

        wvv = wv.rearrange("(kc p) o -> p kc o", p=128)
        gp.dma(lambda e: e.dma_start(W0[:], wvv), waits=[(sPre, sPre.n)], tag="ld.wv")
        v_ld = gp.snap()
        for tc in range(16):
            par = nunit % 2
            w0 = list(v_ld)
            if nunit >= 2:
                w0.append((sRpV, rope_done[nunit - 2]))
            for kc in range(NDC):
                P.op("tensor", lambda e, tc=tc, kc=kc, par=par: e.matmul(
                    pp(par), A[:, kc * 2048 + tc * 128: kc * 2048 + (tc + 1) * 128],
                    W0[:, kc * 512:(kc + 1) * 512],
                    start=(kc == 0), stop=(kc == NDC - 1)),
                    waits=(w0 if kc == 0 else ()),
                    incs=([(sPre, 1)] if kc == NDC - 1 else ()), tag=f"v.mm.{tc}")
            P.op("vector", lambda e, tc=tc, par=par: e.tensor_copy(
                Dv[:, tc * 512:(tc + 1) * 512], pp(par)),
                waits=[(sPre, sPre.n)], incs=[(sRpV, 1)], tag=f"v.ev.{tc}")
            rope_done.append(sRpV.n)
            nunit += 1

        w2b_st = w2bd.rearrange("dg (jp js) c dl -> jp js c dg dl", js=2)
        v_mm_all = sPre.n
        tw_in = lambda sl: W0[:, sl * 2048:(sl + 1) * 2048]
        tw_banks = [PSA[:, 1024:1536], PSB[:, 1024:1536], PSB[:, 512:1024], PTO[:, 512:1024]]
        tw_mm_done, tw_st_tok = {}, {}

        tw_ev_done = {}

        def emit_tw(jp):
            sl = jp % 4
            lw = [(sPre, v_mm_all)] if jp < 4 else tw_st_tok[jp - 4]
            ld = gp.dma(lambda e, jp=jp, sl=sl: e.dma_start(tw_in(sl), w2pd[jp]),
                        waits=lw, tag=f"twl.{jp}")
            for ch in range(4):
                mw = list(ld) if ch == 0 else []
                if jp >= 1:
                    mw.append(tw_ev_done[(jp - 1) * 4 + ch])
                P.op("tensor", lambda e, sl=sl, ch=ch: e.matmul(
                    tw_banks[ch], h64b[:], tw_in(sl)[:, ch * 512:(ch + 1) * 512],
                    start=True, stop=True),
                    waits=mw, incs=[(sTw, 1)], tag=f"twm.{jp}.{ch}")
                ew = [(sTw, sTw.n)]
                if ch % 2 == 0:
                    P.op("vector", lambda e, sl=sl, ch=ch: e.tensor_copy(
                        tw_in(sl)[:, ch * 512:(ch + 1) * 512], tw_banks[ch]),
                        waits=ew, incs=[(sH64, 1)], tag=f"twe.{jp}.{ch}")
                    tw_ev_done[jp * 4 + ch] = (sH64, sH64.n)
                else:
                    P.op("scalar", lambda e, sl=sl, ch=ch: e.activation(
                        tw_in(sl)[:, ch * 512:(ch + 1) * 512], tw_banks[ch], AF.Copy),
                        waits=ew, incs=[(sH64e, 1)], tag=f"twe.{jp}.{ch}")
                    tw_ev_done[jp * 4 + ch] = (sH64e, sH64e.n)
            tw_mm_done[jp] = sTw.n
            tw_st_tok[jp] = sc.dma(lambda e, jp=jp, sl=sl: e.dma_start(w2b_st[jp], tw_in(sl)),
                                   waits=[tw_ev_done[jp * 4 + 2], tw_ev_done[jp * 4 + 3]],
                                   tag=f"tws.{jp}")

        wqv = wq.rearrange("(kc p) o -> p kc o", p=128)
        q_mm_at_eighth = {}
        for ei in range(8):
            hb = ei % 2
            ldw = [] if ei < 2 else [(sPre, q_mm_at_eighth[ei - 2])]
            q_tok = gp.dma(lambda e, ei=ei, hb=hb: e.dma_start(
                W1b[:, hb * 4096:(hb + 1) * 4096], wqv[:, :, ei * 256:(ei + 1) * 256]),
                waits=ldw, tag=f"ld.wq{ei}")
            for ts in range(NTS):
                tpar = ts % 2
                load_tab(tpar, ts, cosqd, sinqd, f"qtab.{ei}.{ts}")
                for qc2 in range(2):
                    h = ei * 2 + qc2
                    par = nunit % 2
                    w0 = v_ld + q_tok + tab_ready[tpar]
                    if nunit >= 2:
                        w0 = w0 + [(sRpV, rope_done[nunit - 2])]
                    for kc in range(NDC):
                        P.op("tensor", lambda e, h=h, ts=ts, kc=kc, par=par, hb=hb, qc2=qc2: e.matmul(
                            pp(par),
                            W1b[:, hb * 4096 + kc * 256 + qc2 * 128: hb * 4096 + kc * 256 + (qc2 + 1) * 128],
                            A[:, kc * 2048 + ts * 512: kc * 2048 + (ts + 1) * 512],
                            start=(kc == 0), stop=(kc == NDC - 1)),
                            waits=(w0 if kc == 0 else ()),
                            incs=([(sPre, 1)] if kc == NDC - 1 else ()), tag=f"q.mm.{h}.{ts}")
                    emit_rope(par, B[:, h * 2048 + ts * 512: h * 2048 + (ts + 1) * 512],
                              cos_sl(tpar), sin_sl(tpar), f"q.rope.{h}.{ts}")
                    tab_user[tpar] = sRpV.n
                    emit_tw(ei * 8 + ts * 2 + qc2)
                    nunit += 1
            q_mm_at_eighth[ei] = sPre.n
        tw_mm_all = sTw.n
        tw_st_all = sc.snap()

        P.barrier(sBar, 1)
        mask_tok = gp.dma(lambda e: e.dma_start(SCRb[:, 0:4096], mask8d), tag="ld.mask8")
        if taps:
            sy.dma(lambda e: e.dma_start(tqfm, B[:]), tag="tap.q")
            sy.dma(lambda e: e.dma_start(tkfm, C[:]), tag="tap.k")
            sy.dma(lambda e: e.dma_start(tvtm, Dv[:]), tag="tap.v")

        attn_col = A[:, 24576:32768]
        pt_slot = lambda par, s: A[:, 20480 + par * 1536 + s * 512: 20480 + par * 1536 + (s + 1) * 512]
        srow_s = AF32[0:1, 11776:12288]
        mask_sl = lambda mid: SCRb[:, mid * 512:(mid + 1) * 512]

        def S_bk(par, gi):
            return (PSA if par == 0 else PSB)[:, (gi % 2) * 512:(gi % 2) * 512 + 512]

        def O_bk(par):
            return (PSA if par == 0 else PSB)[:, 1024:1536]

        sumrow = lambda par: PTO[0:1, par * 512:(par + 1) * 512]
        Rbc = lambda par: PTO[:, par * 512:(par + 1) * 512]
        statS_a = PSA[0:1, 1024:1536]
        statS2_a = PSB[0:1, 1024:1536]

        z1_t = lambda dc: AF32[:, dc * 512:(dc + 1) * 512]
        xres_t = lambda par: AF32[:, 8192 + par * 512: 8192 + (par + 1) * 512]
        zsq1_t = lambda par: AF32[:, 9216 + par * 512: 9216 + (par + 1) * 512]

        def qg_tiles(qg):
            if qg == 0:
                return [(kt, kt) for kt in range(4)]
            if qg == 1:
                return [(kt, None) for kt in range(4)] + [(kt, kt - 4) for kt in range(4, 8)]
            if qg == 2:
                return ([(kt, 4 + kt) for kt in range(4)] + [(kt, None) for kt in range(4, 8)]
                        + [(kt, kt - 8) for kt in range(8, 12)])
            return ([(kt, 4 + kt - 4) for kt in range(4, 8)] + [(kt, None) for kt in range(8, 12)]
                    + [(kt, kt - 12) for kt in range(12, 16)])

        unit = 0
        ptile_cnt = {0: 0, 1: 0}
        ptile_pv = {0: [], 1: []}
        bank_last_exp = {0: {0: 0, 1: 0}, 1: {0: 0, 1: 0}}
        bc_cnt, nrm_cnt, sum_stop, pv_stop, ocp_cnt = {}, {}, {}, {}, {}
        apl_gate = [0]
        fin_gate = [0]
        wo_all_prev = [0]
        z_all_prev = [0]
        first_mask = [True]
        pend_tail = []
        unit_tiles = {}

        def emit_sum_pv(u, t):
            par, h, g, nkt = u % 2, (u % 16), ((u % 16) // 4), len(unit_tiles[u])
            t_, gi, slot, kt, rdy = unit_tiles[u][t]
            sw = [rdy]
            if t == 0:
                if u >= 2:
                    sw.append((sNrm, nrm_cnt[u - 2]))
                if u % 16 in (0, 1) and apl_gate[0]:
                    sw.append((sApl, apl_gate[0]))
            P.op("tensor", lambda e, par=par, slot=slot, st=(t == 0), sp=(t == nkt - 1): e.matmul(
                sumrow(par), onescolb[:], pt_slot(par, slot), start=st, stop=sp),
                waits=sw, incs=[(sSum, 1)], tag=f"sm.{u}.{t}")
            if t == nkt - 1:
                sum_stop[u] = sSum.n
            pw = []
            if t == 0:
                if u >= 2:
                    pw.append((sOcp, ocp_cnt[u - 2]))
                if u % 16 in (0, 1) and fin_gate[0]:
                    pw.append((sFinV, fin_gate[0]))
            P.op("tensor", lambda e, par=par, g=g, kt=kt, slot=slot, st=(t == 0), sp=(t == nkt - 1): e.matmul(
                O_bk(par), Dv[:, kt * 512 + g * 128: kt * 512 + (g + 1) * 128],
                pt_slot(par, slot), start=st, stop=sp),
                waits=pw, incs=[(sPV, 1)], tag=f"pv.{u}.{t}")
            ptile_pv[par].append(sPV.n)
            if t == nkt - 1:
                pv_stop[u] = sPV.n

        def emit_tail():
            u, par, h = pend_tail.pop(0)
            cw = [(sPV, pv_stop[u])]
            if wo_all_prev[0]:
                cw.append((sWo, wo_all_prev[0]))
            P.op("scalar", lambda e, par=par, h=h: e.activation(
                attn_col[:, h * 512:(h + 1) * 512], O_bk(par), AF.Copy),
                waits=cw, incs=[(sOcp, 1)], tag=f"ocp.{u}")
            ocp_cnt[u] = sOcp.n
            rw = [(sSum, sum_stop[u])]
            if u >= 1:
                rw.append((sBc2, bc_cnt[u - 1]))
            P.op("vector", lambda e, par=par: e.reciprocal(srow_s, sumrow(par)),
                 waits=rw, incs=[(sRcp, 1)], tag=f"rcp.{u}")
            bw = [(sRcp, sRcp.n)]
            if u >= 2:
                bw.append((sNrm, nrm_cnt[u - 2]))
            if u % 16 in (0, 1) and apl_gate[0]:
                bw.append((sApl, apl_gate[0]))
            P.op("tensor", lambda e, par=par: e.matmul(
                Rbc(par), onesrow_s[:], srow_s, start=True, stop=True),
                waits=bw, incs=[(sBc2, 1)], tag=f"bc2.{u}")
            bc_cnt[u] = sBc2.n
            nw = [(sBc2, sBc2.n), (sOcp, ocp_cnt[u])]
            P.op("vector", lambda e, par=par, h=h: e.tensor_tensor(
                attn_col[:, h * 512:(h + 1) * 512],
                attn_col[:, h * 512:(h + 1) * 512], Rbc(par), ALU.mult),
                waits=nw, incs=[(sNrm, 1)], tag=f"nrm.{u}")
            nrm_cnt[u] = sNrm.n

        def emit_unit(h, qg):
            nonlocal unit
            u = unit
            par = u % 2
            g = h // 4
            tiles = qg_tiles(qg)
            nkt = len(tiles)
            unit_tiles[u] = []
            for t, (kt, mid) in enumerate(tiles):
                gi = ptile_cnt[par]
                qw = []
                le = bank_last_exp[par][gi % 2]
                if le:
                    qw.append((sExp, le))
                if u % 16 == 0 and t < 2 and z_all_prev[0]:
                    qw.append((sZ, z_all_prev[0]))
                P.op("tensor", lambda e, par=par, gi=gi, g=g, kt=kt, h=h, qg=qg: e.matmul(
                    S_bk(par, gi), C[:, g * 2048 + kt * 128: g * 2048 + (kt + 1) * 128],
                    B[:, h * 2048 + qg * 512: h * 2048 + (qg + 1) * 512],
                    start=True, stop=True),
                    waits=qw, incs=[(sQK, 1)], tag=f"qk.{u}.{t}")
                ew = [(sQK, sQK.n)]
                slot = gi % 3
                if gi >= 3:
                    ew.append((sPV, ptile_pv[par][gi - 3]))
                P.op("scalar", lambda e, par=par, gi=gi, slot=slot: e.activation(
                    pt_slot(par, slot), S_bk(par, gi), AF.Exp),
                    waits=ew, incs=[(sExp, 1)], tag=f"exp.{u}.{t}")
                bank_last_exp[par][gi % 2] = sExp.n
                if mid is not None:
                    mw = [(sExp, sExp.n)]
                    if first_mask[0]:
                        mw = mw + mask_tok
                        first_mask[0] = False
                    P.op("vector", lambda e, par=par, slot=slot, mid=mid: e.tensor_tensor(
                        pt_slot(par, slot), pt_slot(par, slot), mask_sl(mid), ALU.mult),
                        waits=mw, incs=[(sMsk, 1)], tag=f"msk.{u}.{t}")
                    rdy = (sMsk, sMsk.n)
                else:
                    rdy = (sExp, sExp.n)
                unit_tiles[u].append((t, gi, slot, kt, rdy))
                ptile_cnt[par] += 1
                if t >= 2:
                    emit_sum_pv(u, t - 2)
                if t == min(6, nkt - 2) and pend_tail:
                    emit_tail()
            emit_sum_pv(u, nkt - 2)
            emit_sum_pv(u, nkt - 1)
            pend_tail.append((u, par, h))
            unit += 1

        def emit_ln_finalize(statS, statS2, R_ps, B_ps, prev_bc, tag, mrow=None, vrow=None):
            if mrow is None:
                mrow, vrow = mrow1, vrow1
            P.op("vector", lambda e: e.tensor_scalar(mrow, statS, cvec_s[0:1, 3:4], None, ALU.mult),
                 waits=[(sSt, sSt.n), (sBc, prev_bc)], tag=f"{tag}.mean")
            P.op("vector", lambda e: e.tensor_scalar(trow, statS2, cvec_s[0:1, 3:4], None, ALU.mult),
                 tag=f"{tag}.e2")
            P.op("vector", lambda e: e.tensor_tensor(vrow, mrow, mrow, ALU.mult),
                 tag=f"{tag}.m2")
            P.op("vector", lambda e: e.tensor_tensor(trow, trow, vrow, ALU.subtract),
                 incs=[(sFinV, 1)], tag=f"{tag}.var")
            P.op("scalar", lambda e: e.activation(vrow, trow, AF.Ln, bias=cvec_s[0:1, 0:1]),
                 waits=[(sFinV, sFinV.n), (sBc, prev_bc)], tag=f"{tag}.lnf")
            P.op("scalar", lambda e: e.activation(vrow, vrow, AF.Exp, scale=cvec_s[0:1, 1:2]),
                 incs=[(sFinA, 1)], tag=f"{tag}.expf")
            P.op("tensor", lambda e: e.matmul(R_ps, onesrow_s[:], vrow, start=True, stop=True),
                 waits=[(sFinA, sFinA.n)], incs=[(sBc, 1)], tag=f"{tag}.bcR")
            P.op("vector", lambda e: e.tensor_tensor(mrow, mrow, vrow, ALU.mult),
                 waits=[(sFinA, sFinA.n)], incs=[(sFinV, 1)], tag=f"{tag}.brow")
            P.op("tensor", lambda e: e.matmul(B_ps, negrow_s[:], mrow, start=True, stop=True),
                 waits=[(sFinV, sFinV.n)], incs=[(sBc, 1)], tag=f"{tag}.bcB")

        wov = wo.rearrange("(kc p) o -> p kc o", p=128)
        wo_user_done = {0: None, 1: None}
        x1_pend = []
        x1_tok = {}

        def flush_x1():
            qg_, dc_, apl = x1_pend.pop(0)
            x1_tok[(qg_, dc_)] = sy.dma(lambda e, dc=dc_, qg0_=qg_ * QG: e.dma_start(
                x1d[dc, :, qg0_:qg0_ + QG], z1_t(dc)),
                waits=[(sApl, apl)], tag=f"x1w.{qg_}.{dc_}")
        R1 = PSA[:, 1024:1536]
        B1 = PSB[:, 1024:1536]
        R1a = PTO[:, 0:512]
        B1a = PTO[:, 512:1024]
        statS = PSB[0:1, 0:512]
        statS2 = PSB[0:1, 512:1024]

        for qg in range(NQG):
            qg0 = qg * QG
            for h in range(NH):
                emit_unit(h, qg)
            while pend_tail:
                emit_tail()
            nrm_all = sNrm.n
            z_done, st_done = {}, {}
            stat_pend = []

            def emit_stats(z_t, zsq_t, which):
                dc, zpar, zc, sqc = stat_pend.pop(0)
                P.op("tensor", lambda e, dc=dc, zpar=zpar: e.matmul(
                    statS_a, onescolf[:], z_t(dc),
                    start=(dc == 0), stop=(dc == NDC - 1)),
                    waits=[(sZ, zc)], tag=f"st1.{which}.{dc}")
                P.op("tensor", lambda e, dc=dc, zpar=zpar: e.matmul(
                    statS2_a, onescolf[:], zsq_t(zpar),
                    start=(dc == 0), stop=(dc == NDC - 1)),
                    waits=[(sSq, sqc)],
                    incs=([(sSt, 1)] if dc == NDC - 1 else ()), tag=f"st2.{which}.{dc}")
                st_done[dc] = sSt.n

            prev_bc = sBc.n
            for dcg in range(4):
                wpar = dcg % 2
                wobuf = W0 if wpar == 0 else W1b
                ldw = []
                if wo_user_done[wpar] is not None:
                    ldw.append((sWo, wo_user_done[wpar]))
                elif qg == 0 and dcg == 0:
                    ldw += [(sTw, tw_mm_all)] + tw_st_all
                elif qg == 0 and dcg == 1:
                    ldw.append((sPre, sPre.n))
                wo_ld = gp.dma(lambda e, dcg=dcg, wobuf=wobuf: e.dma_start(
                    wobuf[:], wov[:, :, dcg * 512:(dcg + 1) * 512]),
                    waits=ldw, tag=f"ld.wo.{qg}.{dcg}")
                for _ in range(4):
                    if x1_pend:
                        flush_x1()
                for dc_l in range(4):
                    dc = dcg * 4 + dc_l
                    zpar = dc % 2
                    woPS = PSA[:, zpar * 512:(zpar + 1) * 512]
                    xw = [] if dc < 2 else [(sZ, z_done[dc - 2])]
                    xres_tok = sy.dma(lambda e, dc=dc, zpar=zpar, qg0=qg0: e.dma_start(
                        xres_t(zpar), xv[:, dc, qg0:qg0 + QG]),
                        waits=xw, tag=f"xres.{qg}.{dc}")
                    w0 = wo_ld
                    if dc >= 2:
                        w0 = w0 + [(sZ, z_done[dc - 2])]
                    for kc in range(NDC):
                        kw = list(w0) if kc == 0 else []
                        if dc == 0:
                            kw.append((sNrm, nrm_cnt[qg * 16 + kc]))
                        elif kc == 0:
                            kw.append((sNrm, nrm_all))
                        P.op("tensor", lambda e, kc=kc, dc_l=dc_l, wobuf=wobuf, woPS=woPS: e.matmul(
                            woPS, wobuf[:, kc * 512 + dc_l * 128: kc * 512 + (dc_l + 1) * 128],
                            attn_col[:, kc * 512:(kc + 1) * 512],
                            start=(kc == 0), stop=(kc == NDC - 1)),
                            waits=kw,
                            incs=([(sWo, 1)] if kc == NDC - 1 else ()), tag=f"wo.{qg}.{dc}")
                    zaw = [(sWo, sWo.n)] + xres_tok
                    if (qg - 1, dc) in x1_tok:
                        zaw += x1_tok[(qg - 1, dc)]
                    P.op("vector", lambda e, dc=dc, zpar=zpar, woPS=woPS: e.tensor_tensor(
                        z1_t(dc), woPS, xres_t(zpar), ALU.add),
                        waits=zaw, incs=[(sZ, 1)], tag=f"z1.{qg}.{dc}")
                    z_done[dc] = sZ.n
                    zw = [(sZ, sZ.n)] if dc < 2 else [(sZ, sZ.n), (sSt, st_done.get(dc - 2, 0))]
                    P.op("vector", lambda e, dc=dc, zpar=zpar: e.tensor_tensor(
                        zsq1_t(zpar), z1_t(dc), z1_t(dc), ALU.mult),
                        waits=zw, incs=[(sSq, 1)], tag=f"zsq.{qg}.{dc}")
                    stat_pend.append((dc, zpar, z_done[dc], sSq.n))
                    if len(stat_pend) >= 2:
                        emit_stats(z1_t, zsq1_t, 1)
                wo_user_done[wpar] = sWo.n
            while stat_pend:
                emit_stats(z1_t, zsq1_t, 1)
            emit_ln_finalize(statS_a, statS2_a, R1a, B1a, prev_bc, f"ln1.{qg}")
            for dc in range(NDC):
                aw = [(sBc, sBc.n)]
                P.op("vector", lambda e, dc=dc: e.tensor_tensor(
                    z1_t(dc), z1_t(dc), R1a, ALU.mult),
                    waits=aw, tag=f"ap1.{qg}.{dc}")
                P.op("vector", lambda e, dc=dc: e.tensor_tensor(
                    z1_t(dc), z1_t(dc), B1a, ALU.add),
                    incs=[(sApl, 1)], tag=f"ap2.{qg}.{dc}")
                x1_pend.append((qg, dc, sApl.n))
            apl_gate[0] = sApl.n
            fin_gate[0] = sFinV.n
            wo_all_prev[0] = sWo.n
            z_all_prev[0] = sZ.n
            if qg == NQG - 1:
                while x1_pend:
                    flush_x1()

        x1w_all = sy.snap()
        qk_all = sQK.n
        pv_all = sPV.n
        wo_all = sWo.n

        x1v = x1d.rearrange("dc p t -> p dc t")
        for i in range(4):
            gp.dma(lambda e, i=i: e.dma_start(
                B[:, i * 8192:(i + 1) * 8192], x1v[:, 4 * i:4 * (i + 1), :]),
                waits=x1w_all + [(sQK, qk_all)], tag=f"ld.x1bf{i}")

        P.barrier(sBar, 2)
        h128_tok = gp.dma(lambda e: e.dma_start(h128b, h128d), tag="ld.h128")


        sg_t = lambda par: SCR[:, par * 512:(par + 1) * 512]
        hbf_t = lambda par: SCRb[:, 2048 + par * 512: 2048 + (par + 1) * 512]
        h2st_t = lambda par: SCRb[:, 3072 + par * 512: 3072 + (par + 1) * 512]

        def up_ps(par, k):
            return (PSA if par == 0 else PSB)[:, k * 512:(k + 1) * 512]

        up_idx = 0
        up_mm_done = {}
        sil_done, gat_done, h2e_done, h2w_done = {}, {}, {}, {}
        pend_up = []

        def up_tail(i):
            c, ts, par = pend_up.pop(0)
            hw = [(sGat, gat_done[i])]
            if i == 0:
                hw += h128_tok
            if i >= 2:
                hw.append((sH2e, h2e_done[i - 2]))
            P.op("tensor", lambda e, par=par: e.matmul(
                up_ps(par, 2), h128b, hbf_t(par), start=True, stop=True),
                waits=hw, incs=[(sUp, 1)], tag=f"h128.{i}")
            ew = [(sUp, sUp.n)]
            if i >= 2:
                ew += h2w_done[i - 2]
            P.op("scalar", lambda e, par=par: e.activation(
                h2st_t(par), up_ps(par, 2), AF.Copy, scale=cvec_s[:, 2:3]),
                waits=ew, incs=[(sH2e, 1)], tag=f"h2e.{i}")
            h2e_done[i] = sH2e.n
            h2w_done[i] = sy.dma(lambda e, c=c, ts=ts, par=par: e.dma_start(h2d[c, ts], h2st_t(par)),
                                 waits=[(sH2e, sH2e.n)], tag=f"h2w.{i}")

        for cg in range(16):
            wpar = cg % 2
            w1buf = C if wpar == 0 else W0
            w3buf = Dv if wpar == 0 else W1b
            ldw = []
            if cg >= 2:
                ldw.append((sUp, up_mm_done[(cg - 2) * 16 + 15]))
            else:
                ldw += [(sQK, qk_all), (sPV, pv_all), (sWo, wo_all)]
            gp.dma(lambda e, cg=cg, w1buf=w1buf: e.dma_start(w1buf[:], w1t[cg]),
                waits=ldw, tag=f"ld.w1.{cg}")
            gp.dma(lambda e, cg=cg, w3buf=w3buf: e.dma_start(w3buf[:], w3t[cg]), tag=f"ld.w3.{cg}")
            ld_done = gp.snap()
            for cl in range(4):
                c = cg * 4 + cl
                for ts in range(NTS):
                    i = up_idx
                    par = i % 2
                    w0 = list(ld_done)
                    if i >= 2:
                        w0.append((sSil, sil_done[i - 2]))
                    for dc in range(NDC):
                        P.op("tensor", lambda e, dc=dc, cl=cl, ts=ts, par=par, w1buf=w1buf: e.matmul(
                            up_ps(par, 0), w1buf[:, dc * 512 + cl * 128: dc * 512 + (cl + 1) * 128],
                            B[:, dc * 2048 + ts * 512: dc * 2048 + (ts + 1) * 512],
                            start=(dc == 0), stop=(dc == NDC - 1)),
                            waits=(w0 if dc == 0 else ()),
                            incs=([(sUp, 1)] if dc == NDC - 1 else ()), tag=f"upA.{i}")
                    w0b = [(sGat, gat_done[i - 2])] if i >= 2 else []
                    for dc in range(NDC):
                        P.op("tensor", lambda e, dc=dc, cl=cl, ts=ts, par=par, w3buf=w3buf: e.matmul(
                            up_ps(par, 1), w3buf[:, dc * 512 + cl * 128: dc * 512 + (cl + 1) * 128],
                            B[:, dc * 2048 + ts * 512: dc * 2048 + (ts + 1) * 512],
                            start=(dc == 0), stop=(dc == NDC - 1)),
                            waits=(w0b if dc == 0 else ()),
                            incs=([(sUp, 1)] if dc == NDC - 1 else ()), tag=f"upB.{i}")
                    up_mm_done[i] = sUp.n
                    P.op("scalar", lambda e, par=par: e.activation(sg_t(par), up_ps(par, 0), AF.Silu),
                         waits=[(sUp, sUp.n)], incs=[(sSil, 1)], tag=f"sil.{i}")
                    sil_done[i] = sSil.n
                    P.op("vector", lambda e, par=par: e.tensor_tensor(
                        hbf_t(par), sg_t(par), up_ps(par, 1), ALU.mult),
                        waits=[(sSil, sSil.n)], incs=[(sGat, 1)], tag=f"gat.{i}")
                    gat_done[i] = sGat.n
                    pend_up.append((c, ts, par))
                    if i >= 1:
                        up_tail(i - 1)
                    up_idx += 1
        up_tail(up_idx - 1)
        up_all = sUp.n
        h2w_all = sy.snap()

        h2rd = h2d.rearrange("c s p t -> s p c t")
        x1r_t = lambda par: SCR[:, par * 512:(par + 1) * 512]
        zsq2_t = lambda par: SCR[:, 1024 + par * 512: 1024 + (par + 1) * 512]

        def z2_t(dc):
            return CF32[:, dc * 512:(dc + 1) * 512] if dc < 8 else DF32[:, (dc - 8) * 512:(dc - 7) * 512]

        yw = []
        y_pend = []
        y_tok = {}

        def flush_y():
            ts_, dc_, apl = y_pend.pop(0)
            y_tok[(ts_, dc_)] = sc.dma(lambda e, dc=dc_, ts=ts_: e.dma_start(
                yT[dc * 128:(dc + 1) * 128, ts * 512:(ts + 1) * 512], z2_t(dc)),
                waits=[(sApl, apl)], tag=f"yw.{ts_}.{dc_}")
            yw.append(y_tok[(ts_, dc_)])

        dn_done, dne_done = {}, {}
        st2_done = {}
        dn_mm_at_ts = {}
        gat_all = sGat.n
        for ts in range(NTS):
            lw = list(h2w_all)
            if ts == 0:
                lw += [(sApl, apl_gate[0])] + x1w_all
            else:
                lw.append((sDn, dn_mm_at_ts[ts - 1]))
            Av = A[:].rearrange("p (c t) -> p c t", t=512)
            for i8 in range(4):
                gp.dma(lambda e, ts=ts, i8=i8: e.dma_start(
                    Av[:, i8 * 16:(i8 + 1) * 16, :],
                    h2rd[ts, :, i8 * 16:(i8 + 1) * 16, :]),
                    waits=(lw if i8 == 0 else ()), tag=f"h2l.{ts}.{i8}")
            h3l_done = gp.snap()
            prev_bc2 = sBc.n
            stat_pend2 = []
            st_done2 = {}

            def emit_stats2(ts=ts):
                dc, zpar, zc, sqc = stat_pend2.pop(0)
                P.op("tensor", lambda e, dc=dc, zpar=zpar: e.matmul(
                    statS, onescolf[:], z2_t(dc), start=(dc == 0), stop=(dc == NDC - 1)),
                    waits=[(sDnE, zc)], tag=f"st1b.{ts}.{dc}")
                P.op("tensor", lambda e, dc=dc, zpar=zpar: e.matmul(
                    statS2, onescolf[:], zsq2_t(zpar), start=(dc == 0), stop=(dc == NDC - 1)),
                    waits=[(sSq, sqc)],
                    incs=([(sSt, 1)] if dc == NDC - 1 else ()), tag=f"st2b.{ts}.{dc}")
                st2_done[(ts, dc)] = sSt.n

            for dg in range(8):
                wpar = dg % 2
                ldw = []
                if ts == 0 and dg < 2:
                    ldw = [(sUp, up_all)] + tw_st_all
                else:
                    key = (ts, dg - 2) if dg >= 2 else (ts - 1, dg + 6)
                    ldw.append((sDn, dn_done[key]))
                gp.dma(lambda e, dg=dg, wpar=wpar: e.dma_start(
                    B[:, wpar * 16384:(wpar + 1) * 16384], w2bd[dg]),
                    waits=ldw, tag=f"w2l.{ts}.{dg}")
                w2_ld = gp.snap()
                for _ in range(2):
                    if y_pend:
                        flush_y()
                for dc2 in range(2):
                    dc = dg * 2 + dc2
                    dps = PSA[:, dc2 * 512:(dc2 + 1) * 512]
                    zpar = dc % 2
                    xw = []
                    if ts > 0 or dc >= 1:
                        xw.append((sDnE, dne_done[(ts, dc - 1) if dc >= 1 else (ts - 1, 15)]))
                    else:
                        xw.append((sGat, gat_all))
                    x1r_tok = sy.dma(lambda e, dc=dc, ts=ts, zpar=zpar: e.dma_start(
                        x1r_t(zpar), x1d[dc, :, ts * 512:(ts + 1) * 512]),
                        waits=xw + (x1w_all if ts == 0 and dc == 0 else []), tag=f"x1r.{ts}.{dc}")
                    mw = w2_ld + h3l_done
                    if ts > 0 or dc >= 2:
                        mw.append((sDnE, dne_done[(ts, dc - 2) if dc >= 2 else (ts - 1, dc + 14)]))
                    for cc in range(64):
                        P.op("tensor", lambda e, cc=cc, dc2=dc2, dps=dps, wpar=wpar: e.matmul(
                            dps, B[:, wpar * 16384 + cc * 256 + dc2 * 128: wpar * 16384 + cc * 256 + (dc2 + 1) * 128],
                            A[:, cc * 512:(cc + 1) * 512],
                            start=(cc == 0), stop=(cc == 63)),
                            waits=(mw if cc == 0 else ()),
                            incs=([(sDn, 1)] if cc == 63 else ()), tag=f"dn.{ts}.{dc}")
                    z2w = [(sDn, sDn.n)] + x1r_tok
                    if (ts - 1, dc) in y_tok:
                        z2w += y_tok[(ts - 1, dc)]
                    P.op("vector", lambda e, dc=dc, dps=dps, zpar=zpar: e.tensor_tensor(
                        z2_t(dc), dps, x1r_t(zpar), ALU.add),
                        waits=z2w, incs=[(sDnE, 1)], tag=f"z2.{ts}.{dc}")
                    dne_done[(ts, dc)] = sDnE.n
                    zw = [(sDnE, sDnE.n)]
                    if ts > 0 or dc >= 2:
                        zw.append((sSt, st2_done.get((ts, dc - 2) if dc >= 2 else (ts - 1, dc + 14), 0)))
                    P.op("vector", lambda e, dc=dc, zpar=zpar: e.tensor_tensor(
                        zsq2_t(zpar), z2_t(dc), z2_t(dc), ALU.mult),
                        waits=zw, incs=[(sSq, 1)], tag=f"zsq2.{ts}.{dc}")
                    stat_pend2.append((dc, zpar, dne_done[(ts, dc)], sSq.n))
                    if len(stat_pend2) >= 2:
                        emit_stats2()
                dn_done[(ts, dg)] = sDn.n
            while stat_pend2:
                emit_stats2()
            dn_mm_at_ts[ts] = sDn.n
            emit_ln_finalize(statS, statS2, R1, B1, prev_bc2, f"ln2.{ts}")
            for dc in range(NDC):
                aw = [(sBc, sBc.n)]
                P.op("vector", lambda e, dc=dc: e.tensor_tensor(
                    z2_t(dc), z2_t(dc), R1, ALU.mult),
                    waits=aw, tag=f"apy1.{ts}.{dc}")
                P.op("vector", lambda e, dc=dc: e.tensor_tensor(
                    z2_t(dc), z2_t(dc), B1, ALU.add),
                    incs=[(sApl, 1)], tag=f"apy2.{ts}.{dc}")
                y_pend.append((ts, dc, sApl.n))
            if ts == NTS - 1:
                while y_pend:
                    flush_y()

        P.op("sync", None, waits=sy.snap(), tag="final.sy")
        P.op("gpsimd", None, waits=gp.snap(), tag="final.gp")
        P.op("scalar", None, waits=sc.snap(), tag="final.sc")

        P.check_deadlock()
        nops = {e: len(P.ops[e]) for e in ENGINES}

        with nc.Block() as block:
            @block.tensor
            def _(eng):
                P.replay("tensor", eng)

            @block.vector
            def _(eng):
                P.replay("vector", eng)

            @block.scalar
            def _(eng):
                P.replay("scalar", eng)

            @block.sync
            def _(eng):
                P.replay("sync", eng)

            @block.gpsimd
            def _(eng):
                P.replay("gpsimd", eng)

    return nc, nops


_CACHE = {}


def _get_nc(taps=False):
    key = bool(taps)
    if key not in _CACHE:
        _CACHE[key] = build_nc(taps=taps)
    return _CACHE[key]


def make_in_maps(inputs):
    x = np.asarray(inputs["L0_attn_hidden_states"], dtype=np.float32)
    consts = host_constants()
    shared = {
        "wq": np.ascontiguousarray(inputs["L0_attn_w_q"], dtype=np.float32),
        "wk": np.ascontiguousarray(inputs["L0_attn_w_k"], dtype=np.float32),
        "wv": np.ascontiguousarray(inputs["L0_attn_w_v"], dtype=np.float32),
        "wo": np.ascontiguousarray(inputs["L0_attn_w_o"], dtype=np.float32),
        "w1t": np.ascontiguousarray(
            np.asarray(inputs["L0_ffn_W1"], dtype=np.float32).T.reshape(16, 128, 16, 512)
            .transpose(3, 1, 0, 2).reshape(16, 128, 16, 512)) if False else
            np.ascontiguousarray(np.asarray(inputs["L0_ffn_W1"], dtype=np.float32).T
                                 .reshape(16, 128, 16, 512).transpose(2, 1, 0, 3)),
        "w3t": np.ascontiguousarray(np.asarray(inputs["L0_ffn_W3"], dtype=np.float32).T
                                    .reshape(16, 128, 16, 512).transpose(2, 1, 0, 3)),
        "w2p": np.ascontiguousarray(np.asarray(inputs["L0_ffn_W2"], dtype=np.float32).T
                                    .reshape(64, 64, 2, 2048).transpose(1, 2, 0, 3)
                                    .reshape(64, 128, 2048)),
    }
    shared.update(consts)
    in_maps = []
    for i in range(8):
        m = dict(shared)
        m["xT"] = np.ascontiguousarray(x[i].T)
        in_maps.append(m)
    return in_maps


def kernel(**inputs):
    nc, _ = _get_nc(taps=False)
    in_maps = make_in_maps(inputs)
    res = run_bass_kernel_spmd(nc, in_maps, core_ids=list(range(8)))
    out = np.empty((8, S, D), dtype=np.float32)
    for i in range(8):
        out[i] = res.results[i]["yT"].T
    return out
